# version 12
# baseline (speedup 1.0000x reference)
import numpy as np
import ml_dtypes
from contextlib import ExitStack

import concourse.bass as bass
import concourse.mybir as mybir
from concourse.bass_utils import run_bass_kernel_spmd

F32 = mybir.dt.float32
BF16 = mybir.dt.bfloat16
AF = mybir.ActivationFunctionType
ALU = mybir.AluOpType
AX = mybir.AxisListType
BF = ml_dtypes.bfloat16

C_RND = 12582912.0
RSQRT_N = 0.022097086912079608
NCORES = 8
B, N, C = 32, 2048, 1024
BPC = B // NCORES
NT = N // 128
KC = C // 128
I = 2816
IC = I // 128
DEBUG_L1 = False
DEBUG_BARRIER = False
DIAG_TAB_BOUNCE = False
GUARD = 16
NF = GUARD + N + GUARD


class Prog:
    STRICT = False
    STRICT_ENGS = ()
    ENG = ("sync", "act", "pe", "dve", "pool")
    _BLK = {"sync": "sync", "act": "scalar", "pe": "tensor", "dve": "vector",
            "pool": "gpsimd"}
    NSLOT = {"sync": 8, "act": 8, "pe": 0, "dve": 0, "pool": 8}
    KEYS = tuple([(e, "c") for e in ("sync", "act", "pe", "dve", "pool")]
                 + [(e, "d", s) for e in ("sync", "act", "pool")
                    for s in range(8)])

    def __init__(self, nc, ctx):
        self.nc = nc
        self.sems = {k: ctx.enter_context(
            nc.semaphore("sem_" + "_".join(str(p) for p in k)))
            for k in self.KEYS}
        self.cnt = {k: 0 for k in self.KEYS}
        self.known = {e: {k: 0 for k in self.KEYS} for e in self.ENG}
        self.dma_seq = {e: 0 for e in self.ENG}
        self.last_w = {}
        self.readers = {}
        self.ops = {e: [] for e in self.ENG}

    @staticmethod
    def _key(a):
        if a is None:
            return None
        if isinstance(a, str):
            return a
        t = getattr(a, "tensor", a)
        return id(t)

    def op(self, eng, emit, reads=(), writes=(), inc=1):
        if inc >= 16:
            key = (eng, "d", self.dma_seq[eng] % self.NSLOT[eng])
            self.dma_seq[eng] += 1
        else:
            key = (eng, "c")
        waits = {}
        if Prog.STRICT or eng in Prog.STRICT_ENGS:
            for k2 in self.KEYS:
                if self.cnt[k2] > self.known[eng].get(k2, 0):
                    waits[k2] = self.cnt[k2]

        def need(k2, c2):
            if k2 == key == ("pe", "c"):
                return
            if self.known[eng].get(k2, 0) < c2 and waits.get(k2, 0) < c2:
                waits[k2] = c2

        if inc >= 16 and self.cnt[key]:
            need(key, self.cnt[key])
        rk = [k for k in (self._key(a) for a in reads) if k is not None]
        wk = [k for k in (self._key(a) for a in writes) if k is not None]
        for k in rk:
            lw = self.last_w.get(k)
            if lw:
                need(*lw)
        for k in wk:
            lw = self.last_w.get(k)
            if lw:
                need(*lw)
            for k2, c2 in self.readers.get(k, {}).items():
                need(k2, c2)
        after = self.cnt[key] + inc
        self.cnt[key] = after
        for k2, c2 in waits.items():
            self.known[eng][k2] = max(self.known[eng][k2], c2)
        for k in wk:
            self.last_w[k] = (key, after)
            self.readers[k] = {}
        for k in rk:
            d = self.readers.setdefault(k, {})
            d[key] = after
        self.ops[eng].append((waits, emit, inc, key))

    def wait_all(self, eng):
        waits = {}
        for k2 in self.KEYS:
            if self.cnt[k2] > self.known[eng][k2]:
                waits[k2] = self.cnt[k2]
                self.known[eng][k2] = self.cnt[k2]
        if waits:
            self.ops[eng].append((waits, None, 0, None))

    def barrier(self):
        for e in self.ENG:
            self.wait_all(e)

    def dma(self, q, out, in_, extra_reads=(), **kw):
        self.op(q, lambda e: e.dma_start(out=out, in_=in_, **kw),
                reads=[in_, *extra_reads], writes=[out], inc=16)

    def dmat(self, q, out, in_):
        self.op(q, lambda e: e.dma_start(out=out, in_=in_, transpose=True),
                reads=[in_], writes=[out], inc=16)

    def mm(self, out, lhsT, rhs, start, stop):
        self.op("pe", lambda e: e.matmul(out, lhsT, rhs, start=start, stop=stop,
                                         skip_group_check=True),
                reads=[lhsT, rhs], writes=[out], inc=1)

    def act(self, out, in_, func, bias=0.0, scale=1.0, eng="act"):
        er = [a for a in (bias, scale) if hasattr(a, "tensor")]
        self.op(eng, lambda e: e.activation(out, in_, func, bias=bias, scale=scale),
                reads=[in_, *er], writes=[out])

    def ts(self, out, in0, s1, s2, op0, op1=ALU.bypass, eng="dve"):
        er = [a for a in (s1, s2) if hasattr(a, "tensor")]
        self.op(eng, lambda e: e.tensor_scalar(out, in0, s1, s2, op0, op1),
                reads=[in0, *er], writes=[out])

    def tt(self, out, a, b, alu, eng="dve"):
        self.op(eng, lambda e: e.tensor_tensor(out, a, b, alu),
                reads=[a, b], writes=[out])

    def stt(self, out, in0, s, in1, op0, op1, eng="dve"):
        er = [s] if hasattr(s, "tensor") else []
        self.op(eng, lambda e: e.scalar_tensor_tensor(out, in0, s, in1, op0, op1),
                reads=[in0, in1, *er], writes=[out])

    def red(self, out, in_, alu, absval=False, eng="dve"):
        self.op(eng, lambda e: e.tensor_reduce(out, in_, AX.X, alu,
                                               apply_absolute_value=absval),
                reads=[in_], writes=[out])

    def copy(self, out, in_, eng="dve"):
        self.op(eng, lambda e: e.tensor_copy(out, in_), reads=[in_], writes=[out])

    def recip(self, out, in_):
        self.op("dve", lambda e: e.reciprocal(out, in_), reads=[in_], writes=[out])

    def memset(self, ap, v, eng="dve"):
        self.op(eng, lambda e: e.memset(ap, v), reads=[], writes=[ap])

    def finalize(self, blk):
        for e in self.ENG:
            w = {k: self.cnt[k] for k in self.KEYS if k[0] == e and self.cnt[k]}
            if w:
                self.ops[e].append((w, None, 0, None))
        for e in self.ENG:
            def run(eng, _e=e):
                for waits, emit, inc, key in self.ops[_e]:
                    for k2 in self.KEYS:
                        if k2 in waits:
                            eng.wait_ge(self.sems[k2], waits[k2])
                    if emit is not None:
                        ins = emit(eng)
                        if inc:
                            ins.then_inc(self.sems[key], inc)
            getattr(blk, self._BLK[e])(run)


def _sb(nc, ctx, name, shape, dt):
    return ctx.enter_context(nc.sbuf_tensor(name, shape, dt))


def _ps(nc, ctx, name, shape, dt=F32):
    return ctx.enter_context(nc.psum_tensor(name, shape, dt))


def _colmax_to_scalar(P, t, col_f32, out11, psrow):
    P.op("pe", lambda e: e.transpose(psrow[0:1, 0:128], col_f32, t["ident_f"][:]),
         reads=[col_f32, t["ident_f"][:]], writes=[psrow])
    P.copy(t["m1row"][:], psrow[0:1, 0:128])
    P.red(out11, t["m1row"][:], ALU.max)


def _scalar_bcast(P, t, q, src11, dst_col, scratch_dram):
    P.dma("pool", scratch_dram, src11)
    P.dma(q, dst_col, scratch_dram.partition_broadcast(128))


def _emit_quant3(P, out_bf, in_ap, rcol, tmp_f32, tmp2_f32):
    P.ts(tmp_f32, in_ap, rcol, C_RND, ALU.mult, ALU.add)
    P.ts(tmp2_f32, tmp_f32, C_RND, 127.0, ALU.subtract, ALU.min)
    P.ts(out_bf, tmp2_f32, -128.0, None, ALU.max)


def build_l1():
    nc = bass.Bass("TRN2", target_bir_lowering=False)
    d = {}
    d["x"] = nc.dram_tensor("x", [BPC, N, C], F32, kind="ExternalInput").ap()
    d["wq"] = nc.dram_tensor("wq", [C, C], F32, kind="ExternalInput").ap()
    d["wk"] = nc.dram_tensor("wk", [C, C], F32, kind="ExternalInput").ap()
    d["bq"] = nc.dram_tensor("bq", [128, KC], F32, kind="ExternalInput").ap()
    d["bk"] = nc.dram_tensor("bk", [1, C], F32, kind="ExternalInput").ap()
    d["lw"] = nc.dram_tensor("lw", [3, C, C], F32, kind="ExternalInput").ap()
    d["cosF"] = nc.dram_tensor("cosF", [512, N], F32, kind="ExternalInput").ap()
    d["sinF"] = nc.dram_tensor("sinF", [512, N], F32, kind="ExternalInput").ap()
    d["cosT"] = nc.dram_tensor("cosT", [N, 512], F32, kind="ExternalInput").ap()
    d["sinT"] = nc.dram_tensor("sinT", [N, 512], F32, kind="ExternalInput").ap()
    d["eye8"] = nc.dram_tensor("eye8", [8, 8], F32, kind="ExternalInput").ap()
    d["ident"] = nc.dram_tensor("ident", [128, 128], F32, kind="ExternalInput").ap()
    d["mask"] = nc.dram_tensor("mask", [8, 128, 8], F32, kind="ExternalInput").ap()
    d["S_out"] = nc.dram_tensor("S_out", [BPC, 128, NT, C], BF16, kind="ExternalOutput").ap()
    d["xn_out"] = nc.dram_tensor("xn_out", [BPC, 128, NT, C], F32, kind="ExternalOutput").ap()
    d["m1_out"] = nc.dram_tensor("m1_out", [1, 1], F32, kind="ExternalOutput").ap()
    d["cosT_s"] = nc.dram_tensor("cosT_s", [N, 512], BF16, kind="Internal").ap()
    d["sinT_s"] = nc.dram_tensor("sinT_s", [N, 512], BF16, kind="Internal").ap()
    d["cosF_s"] = nc.dram_tensor("cosF_s", [512, N], BF16, kind="Internal").ap()
    d["sinF_s"] = nc.dram_tensor("sinF_s", [512, N], BF16, kind="Internal").ap()
    d["wq_s"] = nc.dram_tensor("wq_s", [128, KC, C], BF16, kind="Internal").ap()
    if DEBUG_L1:
        d["kv_dbg"] = nc.dram_tensor("kv_dbg", [128, C], F32, kind="ExternalOutput").ap()
        d["z_dbg"] = nc.dram_tensor("z_dbg", [128, NT * 8], F32, kind="ExternalOutput").ap()
        d["qr_dbg"] = nc.dram_tensor("qr_dbg", [128, KC, N], F32, kind="ExternalOutput").ap()
        d["q1_dbg"] = nc.dram_tensor("q1_dbg", [128, N], F32, kind="ExternalOutput").ap()
        d["kr_dbg"] = nc.dram_tensor("kr_dbg", [128, C], F32, kind="ExternalOutput").ap()
        d["kr2_dbg"] = nc.dram_tensor("kr2_dbg", [128, C], BF16, kind="ExternalOutput").ap()
        d["t1_dbg"] = nc.dram_tensor("t1_dbg", [128, C], F32, kind="ExternalOutput").ap()
        d["t2_dbg"] = nc.dram_tensor("t2_dbg", [128, C], F32, kind="ExternalOutput").ap()
        d["k1_dbg"] = nc.dram_tensor("k1_dbg", [128, C], F32, kind="ExternalOutput").ap()
        d["tb_dbg"] = nc.dram_tensor("tb_dbg", [128, 2, 512], F32, kind="ExternalOutput").ap()
        d["psk_dbg"] = nc.dram_tensor("psk_dbg", [128, C], F32, kind="ExternalOutput").ap()
        d["psk2_dbg"] = nc.dram_tensor("psk2_dbg", [128, C], F32, kind="ExternalOutput").ap()
        d["km_dbg"] = nc.dram_tensor("km_dbg", [128, KC], F32, kind="ExternalOutput").ap()
        d["xnfm_dbg"] = nc.dram_tensor("xnfm_dbg", [128, KC, NF], BF16, kind="ExternalOutput").ap()
        d["psq_dbg"] = nc.dram_tensor("psq_dbg", [128, C], F32, kind="ExternalOutput").ap()

    with ExitStack() as ctx:
        t = {}
        t["ident_f"] = _sb(nc, ctx, "ident_f", [128, 128], F32)
        t["mask_sb"] = _sb(nc, ctx, "mask_sb", [128, 8, 8], F32)
        t["bq_sb"] = _sb(nc, ctx, "bq_sb", [128, KC], F32)
        t["nbq_sb"] = _sb(nc, ctx, "nbq_sb", [128, KC], F32)
        t["bkb"] = _sb(nc, ctx, "bkb", [1, C], BF16)
        t["ones_row"] = _sb(nc, ctx, "ones_row", [1, 128], BF16)
        t["ones_col"] = _sb(nc, ctx, "ones_col", [128, 1], BF16)
        t["ones11"] = _sb(nc, ctx, "ones11", [1, 1], BF16)
        t["krhs"] = _sb(nc, ctx, "krhs", [128, KC, C], BF16)
        t["lw_sb"] = _sb(nc, ctx, "lw_sb", [128, 24, C], BF16)
        t["wq_buf"] = [_sb(nc, ctx, f"wq{i}", [128, KC, 128], BF16) for i in range(2)]
        t["tabF"] = _sb(nc, ctx, "tabF", [128, 2, 1024], BF16)
        t["tabT"] = [_sb(nc, ctx, f"tabT{i}", [128, 2, 512], BF16) for i in range(2)]
        t["tabT2"] = [_sb(nc, ctx, f"tabT2_{i}", [128, 2, 512], BF16) for i in range(2)] if DIAG_TAB_BOUNCE else None
        t["xn_ring"] = _sb(nc, ctx, "xn_ring", [128, 2, C], BF16)
        t["vh_ring"] = [_sb(nc, ctx, f"vh{i}", [128, C], BF16) for i in range(3)]
        t["qr"] = _sb(nc, ctx, "qr", [128, KC * N], BF16)
        t["xn_fm"] = _sb(nc, ctx, "xn_fm", [128, KC, NF], BF16)
        t["bst"] = [_sb(nc, ctx, f"bst{i}", [128, 12], F32) for i in range(2)]
        t["mv"] = [_sb(nc, ctx, f"mv{i}", [128, 2], F32) for i in range(2)]
        t["vtmp"] = [_sb(nc, ctx, f"vtmp{i}", [128, 1], F32) for i in range(2)]
        t["rstd"] = [_sb(nc, ctx, f"rstd{i}", [128, 1], F32) for i in range(2)]
        t["rstd2"] = [_sb(nc, ctx, f"rstd2{i}", [128, 1], F32) for i in range(2)]
        t["nbias"] = [_sb(nc, ctx, f"nbias{i}", [128, 1], F32) for i in range(2)]
        t["t1"] = [_sb(nc, ctx, f"t1_{i}", [128, C], BF16) for i in range(2)]
        t["t2"] = [_sb(nc, ctx, f"t2_{i}", [128, C], BF16) for i in range(2)]
        t["x_buf"] = [_sb(nc, ctx, f"xbuf{i}", [128, C], F32) for i in range(2)]
        t["k1"] = [_sb(nc, ctx, f"k1_{i}", [128, C], BF16) for i in range(3)]
        t["kr"] = [_sb(nc, ctx, f"kr_{i}", [128, C], BF16) for i in range(3)]
        t["kv_sb"] = _sb(nc, ctx, "kv_sb", [128, C], BF16)
        t["km_bf"] = _sb(nc, ctx, "km_bf", [1, C], BF16)
        t["km_cols"] = _sb(nc, ctx, "km_cols", [128, KC], F32)
        t["A_sb"] = _sb(nc, ctx, "A_sb", [128, KC, 8], BF16)
        t["q1"] = [_sb(nc, ctx, f"q1_{i}", [128, N], BF16) for i in range(2)]
        t["r1"] = [_sb(nc, ctx, f"r1_{i}", [128, C], BF16) for i in range(2)]
        t["r2k"] = [_sb(nc, ctx, f"r2k_{i}", [128, C], BF16) for i in range(2)]
        t["eye8_f"] = _sb(nc, ctx, "eye8_f", [8, 8], F32)
        t["z_tm"] = _sb(nc, ctx, "z_tm", [128, NT, 8], F32)
        t["zrow"] = _sb(nc, ctx, "zrow", [128, 64], F32)
        t["S_buf"] = [_sb(nc, ctx, f"Sb{i}", [128, C], BF16) for i in range(2)]
        t["psq_sav"] = None
        t["t5_dbg"] = None
        t["mx"] = _sb(nc, ctx, "mx", [128, BPC * NT], F32)
        t["mxf"] = _sb(nc, ctx, "mxf", [128, 1], F32)
        t["m1row"] = _sb(nc, ctx, "m1row", [1, 128], F32)
        t["m1v"] = _sb(nc, ctx, "m1v", [1, 1], F32)
        P0 = _ps(nc, ctx, "P0", [128, 1024])
        P1 = _ps(nc, ctx, "P1", [128, 1024])
        P2 = _ps(nc, ctx, "P2", [128, 1024])
        P3 = _ps(nc, ctx, "P3", [128, 1024])

        P = Prog(nc, ctx)
        P.dma("sync", t["ident_f"][:], d["ident"])
        P.dma("sync", t["eye8_f"][:], d["eye8"])
        P.dma("sync", t["mask_sb"][:], d["mask"].rearrange("c p h -> p c h"))
        P.dma("sync", t["bq_sb"][:], d["bq"])
        P.ts(t["nbq_sb"][:], t["bq_sb"][:], -1.0, None, ALU.mult)
        P.dma("pool", t["bkb"][:], d["bk"])
        P.memset(t["ones_row"][:], 1.0)
        P.memset(t["ones_col"][:], 1.0)
        P.memset(t["ones11"][:], 1.0)
        P.op("dve", lambda e: e.memset(t["xn_fm"][:], 0.0),
             reads=[], writes=[f"xnfm{i}" for i in range(NT)])
        P.dma("pool", t["krhs"][:], d["wk"].rearrange("(kc p) n -> p kc n", p=128))
        P.dma("pool", t["lw_sb"][:], d["lw"].rearrange("t (kc p) m -> p (t kc) m", p=128))
        P.dma("pool", d["cosT_s"], d["cosT"])
        P.dma("pool", d["sinT_s"], d["sinT"])
        P.dma("pool", d["cosF_s"], d["cosF"])
        P.dma("pool", d["sinF_s"], d["sinF"])
        P.dma("pool", d["wq_s"], d["wq"].rearrange("(kc p) m -> p kc m", p=128))

        qr = t["qr"][:].rearrange("p (c n) -> p c n", c=KC)

        XK = [f"xnfm{i}" for i in range(NT)]

        def emit_ln(b, tau):
            xb = t["x_buf"][tau % 2]
            P.dma("pool", xb[:], d["x"][b, 128 * tau:128 * (tau + 1), :])
            bst = t["bst"][tau % 2]
            mv = t["mv"][tau % 2]
            vtmp = t["vtmp"][tau % 2]
            rstd = t["rstd"][tau % 2]
            rstd2 = t["rstd2"][tau % 2]
            nbias = t["nbias"][tau % 2]
            for g in range(2):
                P.op("dve", lambda e, o=bst[:, 6 * g:6 * (g + 1)],
                     i=xb[:, 512 * g:512 * (g + 1)]: e.bn_stats(o, i),
                     reads=[xb[:]], writes=[bst[:]])
            P.op("dve", lambda e, o=mv[:], i=bst[:]: e.bn_aggr(o, i),
                 reads=[bst[:]], writes=[mv[:]])
            P.ts(vtmp[:], mv[:, 1:2], 1e-5, None, ALU.add)
            P.act(rstd[:], vtmp[:], AF.Ln)
            P.act(rstd2[:], rstd[:], AF.Exp, scale=-0.5)
            P.stt(nbias[:], mv[:, 0:1], -1.0, rstd2[:, 0:1], ALU.mult, ALU.mult)
            P.act(xb[:], xb[:], AF.Identity, bias=nbias[:, 0:1],
                  scale=rstd2[:, 0:1])
            P.copy(t["xn_ring"][:, tau % 2, :], xb[:])
            P.dma("pool", d["xn_out"][b, :, tau, :], xb[:])
            P.red(t["mx"][:, b * NT + tau: b * NT + tau + 1], xb[:],
                  ALU.max, absval=True)
            out = t["xn_fm"][:, :, GUARD + 128 * tau: GUARD + 128 * (tau + 1)]
            srcv = t["xn_ring"][:, tau % 2, :].rearrange("p (c n) -> p c n", c=KC)
            P.op("sync", lambda e, o=out, s=srcv: e.dma_start(out=o, in_=s, transpose=True),
                 reads=[srcv], writes=[XK[tau]], inc=16)

        for b in range(BPC):
            def emit_kv(tau):
                k1 = t["k1"][tau % 3]
                kr = t["kr"][tau % 3]
                for nh in range(2):
                    P.mm(P3[0:1, 512 * nh:512 * (nh + 1)], t["ones_col"][:],
                         k1[:, 512 * nh:512 * (nh + 1)],
                         start=(tau == 0), stop=(tau == NT - 1))
                for h in range(8):
                    P.mm(P2[:, 128 * h:128 * (h + 1)],
                         kr[:, 128 * h:128 * (h + 1)],
                         t["vh_ring"][tau % 3][:, 128 * h:128 * (h + 1)],
                         start=(tau == 0 and h % 4 == 0), stop=(tau == NT - 1))

            if b == 0:
                for tau in range(NT):
                    emit_ln(0, tau)
            for tau in range(NT):
                for k in range(KC):
                    vout = t["vh_ring"][tau % 3][:, 128 * k:128 * (k + 1)]
                    vin = t["xn_fm"][:, k, GUARD + 128 * tau: GUARD + 128 * (tau + 1)]
                    P.op("sync", lambda e, o=vout, s=vin:
                         e.dma_start(out=o, in_=s, transpose=True),
                         reads=[XK[tau]], writes=[t["vh_ring"][tau % 3][:]], inc=16)
                tb = t["tabT"][tau % 2]
                P.dma("act", tb[:, 0, :], d["cosT_s"][128 * tau:128 * (tau + 1), :])
                P.dma("act", tb[:, 1, :], d["sinT_s"][128 * tau:128 * (tau + 1), :])
                psK = (P0, P1)[tau % 2]
                for k in range(KC):
                    lhsT = t["xn_fm"][:, k, GUARD + 128 * tau: GUARD + 128 * (tau + 1)]
                    for nh in range(2):
                        P.op("pe", lambda e, o=psK[:, 512 * nh:512 * (nh + 1)], l=lhsT,
                             r=t["krhs"][:, k, 512 * nh:512 * (nh + 1)],
                             st=(k == 0): e.matmul(o, l, r, start=st, stop=False,
                                                   skip_group_check=True),
                             reads=[XK[tau], t["krhs"][:]], writes=[psK[:]], inc=1)
                for nh in range(2):
                    P.mm(psK[:, 512 * nh:512 * (nh + 1)], t["ones_row"][:],
                         t["bkb"][0:1, 512 * nh:512 * (nh + 1)], start=False, stop=True)
                if tau > 1:
                    emit_kv(tau - 2)
                k1 = t["k1"][tau % 3]
                t1 = t["t1"][tau % 2]
                t2 = t["t2"][tau % 2]
                P.act(t2[:], psK[:], AF.Relu, scale=-1.0)
                P.act(t1[:], t2[:], AF.Exp, scale=-1.0)
                P.act(t2[:], psK[:], AF.Relu)
                P.tt(k1[:], t1[:], t2[:], ALU.add)
                kr = t["kr"][tau % 3]
                cosT_, sinT_ = tb[:, 0, :], tb[:, 1, :]
                r1 = t["r1"][tau % 2]
                r2k = t["r2k"][tau % 2]
                P.tt(r1[:, 0:512], cosT_, k1[:, 0:512], ALU.mult)
                P.tt(r1[:, 512:1024], sinT_, k1[:, 512:1024], ALU.mult)
                P.tt(kr[:, 0:512], r1[:, 0:512], r1[:, 512:1024],
                     ALU.subtract)
                P.tt(r2k[:, 0:512], cosT_, k1[:, 512:1024], ALU.mult)
                P.tt(r2k[:, 512:1024], sinT_, k1[:, 0:512], ALU.mult)
                P.tt(kr[:, 512:1024], r2k[:, 0:512], r2k[:, 512:1024],
                     ALU.add)
                if DEBUG_L1 and b == 0 and tau == 0:
                    P.dma("pool", d["kr_dbg"], kr[:])
                    P.dma("act", d["kr2_dbg"], kr[:])
                    P.dma("pool", d["t1_dbg"], t["t1"][tau % 2][:])
                    P.dma("pool", d["t2_dbg"], t["t2"][tau % 2][:])
                    P.dma("pool", d["k1_dbg"], k1[:])
                    P.dma("pool", d["tb_dbg"], tb[:])
            emit_kv(NT - 2)
            emit_kv(NT - 1)
            P.ts(t["kv_sb"][:], P2[:], RSQRT_N, None, ALU.mult)
            P.ts(t["km_bf"][:], P3[0:1, :], 1.0 / N, None, ALU.mult)
            for c in range(KC):
                P.mm(P2[:, c:c + 1], t["km_bf"][0:1, 128 * c:128 * (c + 1)],
                     t["ones11"][:], start=(c == 0), stop=(c == KC - 1))
            P.copy(t["km_cols"][:], P2[:, 0:KC])
            for c in range(KC):
                P.ts(t["A_sb"][:, c, :], t["mask_sb"][:, c, :],
                     t["km_cols"][:, c:c + 1], None, ALU.mult)

            def emit_zmm(m, zi):
                q1 = t["q1"][zi % 2]
                for th in range(2):
                    psZ = (P2, P3)[th]
                    for ns in range(2):
                        lo = 1024 * th + 512 * ns
                        P.mm(psZ[0:8, 512 * ns:512 * (ns + 1)], t["A_sb"][:, m, :],
                             q1[:, lo:lo + 512], start=(zi == 0), stop=(zi == 7))

            pair_done = {}
            order = [0, 4, 1, 5, 2, 6, 3, 7]
            for mi, m in enumerate(order):
                wqb = t["wq_buf"][mi % 2]
                P.dma("sync", wqb[:], d["wq_s"][:, :, 128 * m:128 * (m + 1)])
                q1 = t["q1"][mi % 2]
                for th in range(2):
                    psQ = (P0, P1)[th]
                    for k in range(KC):
                        for ns in range(2):
                            lo = 1024 * th + 512 * ns
                            seg = 2 * th + ns
                            P.op("pe", lambda e, o=psQ[:, 512 * ns:512 * (ns + 1)],
                                 l=wqb[:, k, :],
                                 r=t["xn_fm"][:, k, GUARD + lo: GUARD + lo + 512],
                                 st=(k == 0), sp=(k == KC - 1):
                                 e.matmul(o, l, r, start=st, stop=sp,
                                          skip_group_check=True),
                                 reads=[wqb[:], *(XK[4 * seg + j] for j in range(4))],
                                 writes=[psQ[:]], inc=1)
                    if th == 1 and mi > 0:
                        emit_zmm(order[mi - 1], mi - 1)
                    sl_q = slice(1024 * th, 1024 * (th + 1))
                    t1 = t["t1"][th]
                    t2 = t["t2"][th]
                    P.act(t2[:], psQ[:], AF.Relu, scale=-1.0,
                          bias=t["nbq_sb"][:, m:m + 1])
                    P.act(t1[:], t2[:], AF.Exp, scale=-1.0)
                    P.act(t2[:], psQ[:], AF.Relu, bias=t["bq_sb"][:, m:m + 1])
                    P.tt(q1[:, sl_q], t2[:], t1[:], ALU.add)
                if DEBUG_L1 and b == 0 and m == 0:
                    P.dma("pool", d["q1_dbg"], q1[:])
                pair_done[m] = q1
                cpair = m if m < 4 else m - 4
                if cpair in pair_done and cpair + 4 in pair_done:
                    qE, qO = pair_done[cpair], pair_done[cpair + 4]
                    for th in range(2):
                        sl = slice(1024 * th, 1024 * (th + 1))
                        tf = t["tabF"]
                        r1 = t["r1"][th]
                        r2k = t["r2k"][th]
                        P.dma("act", tf[:, 0, :],
                              d["cosF_s"][128 * cpair:128 * (cpair + 1), sl])
                        P.dma("act", tf[:, 1, :],
                              d["sinF_s"][128 * cpair:128 * (cpair + 1), sl])
                        cosF_, sinF_ = tf[:, 0, :], tf[:, 1, :]
                        P.tt(r1[:], cosF_, qE[:, sl], ALU.mult)
                        P.tt(r2k[:], sinF_, qO[:, sl], ALU.mult)
                        P.tt(qr[:, cpair, sl], r1[:], r2k[:], ALU.subtract)
                        P.tt(r1[:, 0:512], cosF_[:, 0:512], qO[:, sl][:, 0:512], ALU.mult)
                        P.tt(r1[:, 512:1024], sinF_[:, 0:512], qE[:, sl][:, 0:512], ALU.mult)
                        P.tt(qr[:, cpair + 4, sl][:, 0:512], r1[:, 0:512],
                             r1[:, 512:1024], ALU.add)
                        P.tt(r2k[:, 0:512], cosF_[:, 512:1024], qO[:, sl][:, 512:1024], ALU.mult)
                        P.tt(r2k[:, 512:1024], sinF_[:, 512:1024], qE[:, sl][:, 512:1024], ALU.mult)
                        P.tt(qr[:, cpair + 4, sl][:, 512:1024], r2k[:, 0:512],
                             r2k[:, 512:1024], ALU.add)
                    del pair_done[cpair], pair_done[cpair + 4]
            emit_zmm(order[7], 7)
            for th in range(2):
                psZ = (P2, P3)[th]
                P.ts(t["x_buf"][0][0:8, 0:1024], psZ[0:8, :], 1e-6, None, ALU.add)
                for tt8 in range(8):
                    P.mm(P0[:, 8 * tt8:8 * (tt8 + 1)],
                         t["x_buf"][0][0:8, 128 * tt8:128 * (tt8 + 1)], t["eye8_f"][:],
                         start=(tt8 == 0), stop=True)
                P.copy(t["zrow"][:], P0[:, 0:64])
                P.recip(t["z_tm"][:, 8 * th:8 * (th + 1), :],
                        t["zrow"][:].rearrange("p (a e) -> p a e", a=8))

            for tau in range(NT):
                psA = (P0, P1)[tau % 2]
                psL = (P2, P3)[tau % 2]
                for h in range(8):
                    P.mm(psA[:, 128 * h:128 * (h + 1)],
                         qr[:, h, 128 * tau:128 * (tau + 1)],
                         t["kv_sb"][:, 128 * h:128 * (h + 1)],
                         start=(h % 4 == 0), stop=True)
                nmm = 0
                for tap in range(3):
                    for k in range(KC):
                        lhsT = t["xn_fm"][:, k,
                                          GUARD + 128 * tau + tap - 1:
                                          GUARD + 128 * tau + tap - 1 + 128]
                        rkeys = [XK[j] for j in (tau - 1, tau, tau + 1)
                                 if 0 <= j < NT]
                        for nh in range(2):
                            P.op("pe", lambda e, o=psL[:, 512 * nh:512 * (nh + 1)],
                                 l=lhsT,
                                 r=t["lw_sb"][:, 8 * tap + k, 512 * nh:512 * (nh + 1)],
                                 st=(nmm == 0), sp=(nmm == 23):
                                 e.matmul(o, l, r, start=st, stop=sp,
                                          skip_group_check=True),
                                 reads=[t["lw_sb"][:], *rkeys], writes=[psL[:]], inc=1)
                        nmm += 1
                zb = t["z_tm"][:, tau, :].unsqueeze(2).broadcast_to([128, 8, 128])
                Sb = t["S_buf"][tau % 2]
                P.tt(Sb[:].rearrange("p (h n) -> p h n", h=8),
                     psA[:].rearrange("p (h n) -> p h n", h=8), zb, ALU.mult)
                P.tt(Sb[:], Sb[:], psL[:], ALU.add)
                P.dma("act", d["S_out"][b, :, tau, :], Sb[:])
                if tau >= 1 and b + 1 < BPC:
                    emit_ln(b + 1, tau - 1)
            if b + 1 < BPC:
                emit_ln(b + 1, NT - 1)
            if DEBUG_L1 and b == 0:
                P.op("pool", lambda e: e.dma_start(out=d["xnfm_dbg"], in_=t["xn_fm"][:]),
                     reads=list(XK), writes=[d["xnfm_dbg"]], inc=16)
                P.dma("pool", d["kv_dbg"], t["kv_sb"][:])
                P.dma("pool", d["z_dbg"], t["z_tm"][:].rearrange("p a e -> p (a e)"))
                P.dma("pool", d["qr_dbg"], qr)
                P.dma("pool", d["km_dbg"], t["km_cols"][:])

        P.red(t["mxf"][:], t["mx"][:], ALU.max)
        _colmax_to_scalar(P, t, t["mxf"][:], t["m1v"][:], P0)
        P.dma("pool", d["m1_out"], t["m1v"][:])
        P.wait_all("pool")

        blk = ctx.enter_context(nc.Block())
        P.finalize(blk)
    return nc


def build_l2():
    nc = bass.Bass("TRN2", target_bir_lowering=False)
    d = {}
    d["xn_in"] = nc.dram_tensor("xn_in", [BPC, 128, NT, C], F32, kind="ExternalInput").ap()
    d["m1vec"] = nc.dram_tensor("m1vec", [1, NCORES], F32, kind="ExternalInput").ap()
    d["clip_x"] = nc.dram_tensor("clip_x", [1, 1], F32, kind="ExternalInput").ap()
    d["wg"] = nc.dram_tensor("wg", [I, C], F32, kind="ExternalInput").ap()
    d["wu"] = nc.dram_tensor("wu", [I, C], F32, kind="ExternalInput").ap()
    d["ident"] = nc.dram_tensor("ident", [128, 128], F32, kind="ExternalInput").ap()
    d["h_out"] = nc.dram_tensor("h_out", [BPC, 128, IC, N], F32, kind="ExternalOutput").ap()
    d["m2_out"] = nc.dram_tensor("m2_out", [1, 1], F32, kind="ExternalOutput").ap()
    scr = nc.dram_tensor("scr11", [1, 1], F32, kind="Internal").ap()

    with ExitStack() as ctx:
        t = {}
        t["ident_f"] = _sb(nc, ctx, "ident_f", [128, 128], F32)
        t["wgT"] = _sb(nc, ctx, "wgT", [128, KC, I], BF16)
        t["wuT"] = _sb(nc, ctx, "wuT", [128, KC, I], BF16)
        t["rmax"] = _sb(nc, ctx, "rmax", [128, 1], F32)
        t["rq"] = _sb(nc, ctx, "rq", [128, 1], F32)
        t["swg"] = _sb(nc, ctx, "swg", [128, IC], F32)
        t["swu"] = _sb(nc, ctx, "swu", [128, IC], F32)
        t["sgc"] = _sb(nc, ctx, "sgc", [128, IC], F32)
        t["suc"] = _sb(nc, ctx, "suc", [128, IC], F32)
        t["m1s"] = _sb(nc, ctx, "m1s", [1, NCORES], F32)
        t["m1"] = _sb(nc, ctx, "m1", [1, 1], F32)
        t["sig"] = _sb(nc, ctx, "sig", [1, 1], F32)
        t["s1"] = _sb(nc, ctx, "s1", [1, 1], F32)
        t["r1"] = _sb(nc, ctx, "r1", [1, 1], F32)
        t["r1b"] = _sb(nc, ctx, "r1b", [128, 1], F32)
        t["s1b"] = _sb(nc, ctx, "s1b", [128, 1], F32)
        t["hm"] = _sb(nc, ctx, "hm", [128, 1], F32)
        t["hmx"] = _sb(nc, ctx, "hmx", [128, 1], F32)
        t["m1row"] = _sb(nc, ctx, "m1row", [1, 128], F32)
        t["m2v"] = _sb(nc, ctx, "m2v", [1, 1], F32)
        P0 = _ps(nc, ctx, "P0", [128, 512])
        P1 = _ps(nc, ctx, "P1", [128, 512])
        P2 = _ps(nc, ctx, "P2", [128, 512])
        P3 = _ps(nc, ctx, "P3", [128, 512])

        P = Prog(nc, ctx)
        P.dma("sync", t["ident_f"][:], d["ident"])
        P.dma("sync", t["m1s"][:], d["m1vec"])
        P.red(t["m1"][:], t["m1s"][:], ALU.max)
        P.dma("sync", t["sig"][:], d["clip_x"])
        P.act(t["sig"][:], t["sig"][:], AF.Sigmoid)
        P.tt(t["s1"][:], t["sig"][:], t["m1"][:], ALU.mult)
        P.ts(t["s1"][:], t["s1"][:], 1.0 / 127.0, None, ALU.mult)
        P.recip(t["r1"][:], t["s1"][:])
        _scalar_bcast(P, t, "sync", t["s1"][:], t["s1b"][:], scr)
        _scalar_bcast(P, t, "sync", t["r1"][:], t["r1b"][:], scr)
        with ExitStack() as c2:
            wnat = _sb(nc, c2, "wnat", [128, KC, IC, 128], BF16)
            wrow = [_sb(nc, c2, f"wrow{i}", [128, C], F32) for i in range(2)]
            qt = _sb(nc, c2, "qt", [128, C], F32)
            for wsrc, wT, sw in ((d["wg"], t["wgT"], t["swg"]), (d["wu"], t["wuT"], t["swu"])):
                for cc in range(IC):
                    wr = wrow[cc % 2]
                    P.dma("sync", wr[:], wsrc.rearrange("(cc p) m -> p cc m", p=128)[:, cc, :])
                    P.red(t["rmax"][:], wr[:], ALU.max, absval=True)
                    P.recip(t["rq"][:], t["rmax"][:])
                    P.ts(t["rq"][:], t["rq"][:], 127.0, None, ALU.mult)
                    P.ts(sw[:, cc:cc + 1], t["rmax"][:], 1.0 / 127.0, None, ALU.mult)
                    P.ts(qt[:], wr[:], t["rq"][:, 0:1], C_RND, ALU.mult, ALU.add)
                    P.ts(wnat[:, :, cc, :], qt[:].rearrange("p (h n) -> p h n", h=KC),
                         C_RND, None, ALU.subtract)
                for hc in range(KC):
                    P.dmat("act", wT[:, hc, :].rearrange("p (i n) -> p i n", i=IC),
                           wnat[:, hc, :, :])
            P.barrier()
        t["xq"] = [_sb(nc, ctx, f"xq{i}", [128, KC, N], BF16) for i in range(2)]
        t["xnb"] = [_sb(nc, ctx, f"xnb{i}", [128, C], F32) for i in range(2)]
        t["xqt"] = [_sb(nc, ctx, f"xqt{i}", [128, C], BF16) for i in range(2)]
        t["qtmp"] = _sb(nc, ctx, "qtmp", [128, C], F32)
        t["qtmp2"] = _sb(nc, ctx, "qtmp2", [128, C], F32)
        t["sg_buf"] = _sb(nc, ctx, "sg_buf", [128, 512], F32)
        t["up_buf"] = _sb(nc, ctx, "up_buf", [128, 512], F32)
        t["h_buf"] = [_sb(nc, ctx, f"hb{i}", [128, 512], F32) for i in range(2)]
        P.tt(t["sgc"][:], t["swg"][:], t["s1b"][:, 0:1].broadcast_to([128, IC]), ALU.mult)
        P.tt(t["suc"][:], t["swu"][:], t["s1b"][:, 0:1].broadcast_to([128, IC]), ALU.mult)
        P.memset(t["hmx"][:], 0.0)

        def emit_quant_tau(g, j):
            gb, gs = divmod(g, 4)
            tau = 4 * gs + j
            xnb = t["xnb"][tau % 2]
            xqt = t["xqt"][tau % 2]
            P.dma("sync", xnb[:], d["xn_in"][gb, :, tau, :])
            _emit_quant3(P, xqt[:], xnb[:], t["r1b"][:, 0:1], t["qtmp"][:], t["qtmp2"][:])
            out = t["xq"][gb % 2][:, :, 128 * tau:128 * (tau + 1)]
            src = xqt[:].rearrange("p (c n) -> p c n", c=KC)
            P.op("act", lambda e, o=out, s=src: e.dma_start(out=o, in_=s, transpose=True),
                 reads=[src], writes=[f"xq{gb % 2}s{gs}"], inc=16)

        for j in range(4):
            emit_quant_tau(0, j)
        SLOTS = (4, 9, 14, 19)
        for b in range(BPC):
            for s in range(4):
                g = 4 * b + s
                lo = 512 * s
                xqb = t["xq"][b % 2]
                xqkey = f"xq{b % 2}s{s}"
                for m in range(IC):
                    psG = (P0, P1)[m % 2]
                    psU = (P2, P3)[m % 2]
                    for k in range(KC):
                        P.op("pe", lambda e, o=psG[:], l=t["wgT"][:, k, 128 * m:128 * (m + 1)],
                             r=xqb[:, k, lo:lo + 512], st=(k == 0), sp=(k == KC - 1):
                             e.matmul(o, l, r, start=st, stop=sp, skip_group_check=True),
                             reads=[t["wgT"][:, k, 128 * m:128 * (m + 1)], xqkey],
                             writes=[psG[:]], inc=1)
                        P.op("pe", lambda e, o=psU[:], l=t["wuT"][:, k, 128 * m:128 * (m + 1)],
                             r=xqb[:, k, lo:lo + 512], st=(k == 0), sp=(k == KC - 1):
                             e.matmul(o, l, r, start=st, stop=sp, skip_group_check=True),
                             reads=[t["wuT"][:, k, 128 * m:128 * (m + 1)], xqkey],
                             writes=[psU[:]], inc=1)
                    P.act(t["sg_buf"][:], psG[:], AF.Silu, scale=t["sgc"][:, m:m + 1])
                    P.ts(t["up_buf"][:], psU[:], t["suc"][:, m:m + 1], None, ALU.mult)
                    hb = t["h_buf"][m % 2]
                    P.tt(hb[:], t["sg_buf"][:], t["up_buf"][:], ALU.mult)
                    P.red(t["hm"][:], hb[:], ALU.max, absval=True)
                    P.tt(t["hmx"][:], t["hmx"][:], t["hm"][:], ALU.max)
                    P.dma("pool", d["h_out"][b, :, m, lo:lo + 512], hb[:])
                    if m in SLOTS and g + 1 <= 4 * BPC - 1:
                        emit_quant_tau(g + 1, SLOTS.index(m))

        _colmax_to_scalar(P, t, t["hmx"][:], t["m2v"][:], P0)
        P.dma("pool", d["m2_out"], t["m2v"][:])
        P.wait_all("pool")
        blk = ctx.enter_context(nc.Block())
        P.finalize(blk)
    return nc


def build_l3():
    nc = bass.Bass("TRN2", target_bir_lowering=False)
    d = {}
    d["h_in"] = nc.dram_tensor("h_in", [BPC, 128, IC, N], F32, kind="ExternalInput").ap()
    d["m2vec"] = nc.dram_tensor("m2vec", [1, NCORES], F32, kind="ExternalInput").ap()
    d["clip_d"] = nc.dram_tensor("clip_d", [1, 1], F32, kind="ExternalInput").ap()
    d["wd"] = nc.dram_tensor("wd", [C, I], F32, kind="ExternalInput").ap()
    d["S_in"] = nc.dram_tensor("S_in", [BPC, 128, NT, C], BF16, kind="ExternalInput").ap()
    d["x"] = nc.dram_tensor("x", [BPC, N, C], F32, kind="ExternalInput").ap()
    d["lb"] = nc.dram_tensor("lb", [1, C], F32, kind="ExternalInput").ap()
    d["y"] = nc.dram_tensor("y", [BPC, N, C], F32, kind="ExternalOutput").ap()
    scr = nc.dram_tensor("scr11", [1, 1], F32, kind="Internal").ap()

    with ExitStack() as ctx:
        t = {}
        t["wdT"] = _sb(nc, ctx, "wdT", [128, IC, C], BF16)
        t["rmax"] = _sb(nc, ctx, "rmax", [128, 1], F32)
        t["rq"] = _sb(nc, ctx, "rq", [128, 1], F32)
        t["sw"] = _sb(nc, ctx, "sw", [128, 1], F32)
        t["m2s"] = _sb(nc, ctx, "m2s", [1, NCORES], F32)
        t["m2"] = _sb(nc, ctx, "m2", [1, 1], F32)
        t["sig"] = _sb(nc, ctx, "sig", [1, 1], F32)
        t["s2"] = _sb(nc, ctx, "s2", [1, 1], F32)
        t["r2"] = _sb(nc, ctx, "r2", [1, 1], F32)
        t["r2b"] = _sb(nc, ctx, "r2b", [128, 1], F32)
        t["s2b"] = _sb(nc, ctx, "s2b", [128, 1], F32)
        t["brep"] = _sb(nc, ctx, "brep", [128, C], BF16)
        t["hb"] = [_sb(nc, ctx, f"hb{i}", [128, C], F32) for i in range(2)]
        t["Sb"] = [_sb(nc, ctx, "Sb0", [128, C], BF16)] * 2
        t["xb"] = [_sb(nc, ctx, f"xb{i}", [128, C], F32) for i in range(2)]
        t["ub"] = [_sb(nc, ctx, f"ub{i}", [128, C], F32) for i in range(2)]
        P0 = _ps(nc, ctx, "P0", [128, 1024])
        P1 = _ps(nc, ctx, "P1", [128, 1024])

        P = Prog(nc, ctx)
        P.dma("sync", t["m2s"][:], d["m2vec"])
        P.red(t["m2"][:], t["m2s"][:], ALU.max)
        P.dma("sync", t["sig"][:], d["clip_d"])
        P.act(t["sig"][:], t["sig"][:], AF.Sigmoid)
        P.tt(t["s2"][:], t["sig"][:], t["m2"][:], ALU.mult)
        P.ts(t["s2"][:], t["s2"][:], 1.0 / 127.0, None, ALU.mult)
        P.recip(t["r2"][:], t["s2"][:])
        _scalar_bcast(P, t, "sync", t["s2"][:], t["s2b"][:], scr)
        _scalar_bcast(P, t, "sync", t["r2"][:], t["r2b"][:], scr)
        P.dma("pool", t["brep"][:], d["lb"].partition_broadcast(128))
        with ExitStack() as c2:
            wdnat = _sb(nc, c2, "wdnat", [128, IC, KC, 128], BF16)
            wrow = [_sb(nc, c2, f"wrow{i}", [128, I], F32) for i in range(2)]
            qt = _sb(nc, c2, "qt", [128, I], F32)
            qt2 = _sb(nc, c2, "qt2", [128, I], F32)
            for hc in range(KC):
                wr = wrow[hc % 2]
                P.dma("sync", wr[:], d["wd"].rearrange("(hc p) m -> p hc m", p=128)[:, hc, :])
                P.red(t["rmax"][:], wr[:], ALU.max, absval=True)
                P.recip(t["rq"][:], t["rmax"][:])
                P.ts(t["rq"][:], t["rq"][:], 127.0, None, ALU.mult)
                P.ts(t["sw"][:], t["rmax"][:], 1.0 / 127.0, None, ALU.mult)
                P.ts(qt[:], wr[:], t["rq"][:, 0:1], C_RND, ALU.mult, ALU.add)
                P.stt(wdnat[:, :, hc, :], qt[:].rearrange("p (i n) -> p i n", i=IC),
                      C_RND, t["sw"][:, 0:1].unsqueeze(1).broadcast_to([128, IC, 128]),
                      ALU.subtract, ALU.mult)
            for ic in range(IC):
                P.dmat("act", t["wdT"][:, ic, :].rearrange("p (h n) -> p h n", h=KC),
                       wdnat[:, ic, :, :])
            P.barrier()
        HH = (IC + 1) // 2
        t["hq"] = [_sb(nc, ctx, f"hq{i}", [128, HH, N], BF16) for i in range(3)]
        t["q2b"] = _sb(nc, ctx, "q2b", [128, 512], BF16)
        t["q2b2"] = _sb(nc, ctx, "q2b2", [128, 512], BF16)
        t["crnd"] = _sb(nc, ctx, "crnd", [128, 1], F32)

        P.memset(t["crnd"][:], C_RND)

        def emit_hquant(b, ic):
            half, j = divmod(ic, HH)
            slot = (2 * b + half) % 3
            for g in range(2):
                hb = t["hb"][g]
                P.dma("sync", hb[:], d["h_in"][b, :, ic, 1024 * g:1024 * (g + 1)])
                P.act(hb[:], hb[:], AF.Identity, bias=t["crnd"][:, 0:1],
                      scale=t["r2b"][:, 0:1])
                for gg in range(2):
                    qb = (t["q2b"], t["q2b2"])[gg]
                    P.ts(qb[:], hb[:, 512 * gg:512 * (gg + 1)],
                         C_RND, 127.0, ALU.subtract, ALU.min,
                         eng=("pool", "dve")[gg])
                    P.ts(t["hq"][slot][:, j, 1024 * g + 512 * gg:
                                       1024 * g + 512 * (gg + 1)],
                         qb[:], -128.0, None, ALU.max)

        for ic in range(IC):
            emit_hquant(0, ic)
        for b in range(BPC):
            for tau in range(NT):
                psD = (P0, P1)[tau % 2]
                for ic in range(IC):
                    half, j = divmod(ic, HH)
                    slot = (2 * b + half) % 3
                    for nh in range(2):
                        P.mm(psD[:, 512 * nh:512 * (nh + 1)],
                             t["hq"][slot][:, j, 128 * tau:128 * (tau + 1)],
                             t["wdT"][:, ic, 512 * nh:512 * (nh + 1)],
                             start=(ic == 0), stop=(ic == IC - 1))
                Sb = t["Sb"][tau % 2]
                xb = t["xb"][tau % 2]
                P.dma("sync", Sb[:], d["S_in"][b, :, tau, :])
                P.dma("sync", xb[:], d["x"][b, 128 * tau:128 * (tau + 1), :])
                ub = t["ub"][tau % 2]
                P.stt(ub[:], psD[:], t["s2b"][:, 0:1], Sb[:], ALU.mult, ALU.add)
                P.tt(ub[:], ub[:], xb[:], ALU.add)
                P.tt(ub[:], ub[:], t["brep"][:], ALU.add)
                P.dma("pool", d["y"][b, 128 * tau:128 * (tau + 1), :], ub[:])
                if b + 1 < BPC and 1 <= tau <= HH:
                    emit_hquant(b + 1, tau - 1)
            if b + 1 < BPC:
                for ic in range(HH, IC):
                    emit_hquant(b + 1, ic)
        P.wait_all("pool")
        blk = ctx.enter_context(nc.Block())
        P.finalize(blk)
    return nc


def _host_tables():
    j = np.arange(512, dtype=np.float64)
    theta = 10000.0 ** (-j / 512.0)
    n = np.arange(N, dtype=np.float64)
    ang = n[:, None] * theta[None, :]
    cosT = np.cos(ang).astype(np.float32)
    sinT = np.sin(ang).astype(np.float32)
    return cosT, sinT, np.ascontiguousarray(cosT.T), np.ascontiguousarray(sinT.T)


def _host_masks():
    m = np.zeros((8, 128, 8), dtype=np.float32)
    for c in range(8):
        for p in range(128):
            r = 128 * c + p
            orig = 2 * r if r < 512 else 2 * (r - 512) + 1
            m[c, p, orig // 128] = 1.0
    return m


_PERM = np.concatenate([np.arange(0, 1024, 2), np.arange(1, 1024, 2)])


def kernel(L0_attn_x, L0_attn_qk_w, L0_attn_qk_b, L0_attn_lepe_w, L0_attn_lepe_b,
           L0_ffn_clip_x, L0_ffn_clip_down, L0_ffn_w_gate, L0_ffn_w_up, L0_ffn_w_down):
    x = np.ascontiguousarray(np.asarray(L0_attn_x, dtype=np.float32))
    qk_w = np.asarray(L0_attn_qk_w, dtype=np.float32)
    qk_b = np.asarray(L0_attn_qk_b, dtype=np.float32)
    wq = np.ascontiguousarray(qk_w[:, :C][:, _PERM])
    wk = np.ascontiguousarray(qk_w[:, C:][:, _PERM])
    bq_p = qk_b[:C][_PERM]
    bk_p = qk_b[C:][_PERM]
    bq_cols = np.ascontiguousarray(bq_p.reshape(KC, 128).T)
    bk_row = np.ascontiguousarray(bk_p.reshape(1, C))
    lw = np.ascontiguousarray(np.asarray(L0_attn_lepe_w, dtype=np.float32))
    lb = np.ascontiguousarray(np.asarray(L0_attn_lepe_b, dtype=np.float32).reshape(1, C))
    wg = np.ascontiguousarray(np.asarray(L0_ffn_w_gate, dtype=np.float32))
    wu = np.ascontiguousarray(np.asarray(L0_ffn_w_up, dtype=np.float32))
    wd = np.ascontiguousarray(np.asarray(L0_ffn_w_down, dtype=np.float32))
    clip_x = np.asarray(L0_ffn_clip_x, dtype=np.float32).reshape(1, 1)
    clip_d = np.asarray(L0_ffn_clip_down, dtype=np.float32).reshape(1, 1)
    cosT, sinT, cosF, sinF = _host_tables()
    eye8 = np.eye(8, dtype=np.float32)
    ident = np.eye(128, dtype=np.float32)
    mask = _host_masks()

    xs = [np.ascontiguousarray(x[BPC * c:BPC * (c + 1)]) for c in range(NCORES)]

    nc1 = build_l1()
    in1 = [dict(x=xs[c], wq=wq, wk=wk, bq=bq_cols, bk=bk_row, lw=lw,
                cosF=cosF, sinF=sinF, cosT=cosT, sinT=sinT,
                eye8=eye8, ident=ident, mask=mask) for c in range(NCORES)]
    r1 = run_bass_kernel_spmd(nc1, in1, core_ids=list(range(NCORES)))
    m1vec = np.concatenate([r1.results[c]["m1_out"].reshape(1, 1)
                            for c in range(NCORES)], axis=1)

    nc2 = build_l2()
    in2 = [dict(xn_in=r1.results[c]["xn_out"], m1vec=m1vec, clip_x=clip_x,
                wg=wg, wu=wu, ident=ident) for c in range(NCORES)]
    r2 = run_bass_kernel_spmd(nc2, in2, core_ids=list(range(NCORES)))
    m2vec = np.concatenate([r2.results[c]["m2_out"].reshape(1, 1)
                            for c in range(NCORES)], axis=1)

    nc3 = build_l3()
    in3 = [dict(h_in=r2.results[c]["h_out"], m2vec=m2vec, clip_d=clip_d,
                wd=wd, S_in=r1.results[c]["S_out"], x=xs[c], lb=lb)
           for c in range(NCORES)]
    r3 = run_bass_kernel_spmd(nc3, in3, core_ids=list(range(NCORES)))
    y = np.concatenate([np.asarray(r3.results[c]["y"]) for c in range(NCORES)], axis=0)
    return y.astype(np.float32)


# version 13
# speedup vs baseline: 1.1214x; 1.1214x over previous
import numpy as np
import ml_dtypes
from contextlib import ExitStack

import concourse.bass as bass
import concourse.mybir as mybir
from concourse.bass_utils import run_bass_kernel_spmd

F32 = mybir.dt.float32
BF16 = mybir.dt.bfloat16
AF = mybir.ActivationFunctionType
ALU = mybir.AluOpType
AX = mybir.AxisListType
BF = ml_dtypes.bfloat16

C_RND = 12582912.0
RSQRT_N = 0.022097086912079608
NCORES = 8
B, N, C = 32, 2048, 1024
BPC = B // NCORES
NT = N // 128
KC = C // 128
I = 2816
IC = I // 128
DEBUG_L1 = False
DEBUG_BARRIER = False
DIAG_TAB_BOUNCE = False
GUARD = 16
NF = GUARD + N + GUARD


class Prog:
    STRICT = False
    STRICT_ENGS = ()
    ENG = ("sync", "act", "pe", "dve", "pool")
    _BLK = {"sync": "sync", "act": "scalar", "pe": "tensor", "dve": "vector",
            "pool": "gpsimd"}
    NSLOT = {"sync": 8, "act": 8, "pe": 0, "dve": 0, "pool": 8}
    KEYS = tuple([(e, "c") for e in ("sync", "act", "pe", "dve", "pool")]
                 + [(e, "d", s) for e in ("sync", "act", "pool")
                    for s in range(8)])

    def __init__(self, nc, ctx):
        self.nc = nc
        self.sems = {k: ctx.enter_context(
            nc.semaphore("sem_" + "_".join(str(p) for p in k)))
            for k in self.KEYS}
        self.cnt = {k: 0 for k in self.KEYS}
        self.known = {e: {k: 0 for k in self.KEYS} for e in self.ENG}
        self.dma_seq = {e: 0 for e in self.ENG}
        self.last_w = {}
        self.readers = {}
        self.ops = {e: [] for e in self.ENG}

    @staticmethod
    def _key(a):
        if a is None:
            return None
        if isinstance(a, str):
            return a
        t = getattr(a, "tensor", a)
        return id(t)

    def op(self, eng, emit, reads=(), writes=(), inc=1):
        if inc >= 16:
            key = (eng, "d", self.dma_seq[eng] % self.NSLOT[eng])
            self.dma_seq[eng] += 1
        else:
            key = (eng, "c")
        waits = {}
        if Prog.STRICT or eng in Prog.STRICT_ENGS:
            for k2 in self.KEYS:
                if self.cnt[k2] > self.known[eng].get(k2, 0):
                    waits[k2] = self.cnt[k2]

        def need(k2, c2):
            if k2 == key == ("pe", "c"):
                return
            if self.known[eng].get(k2, 0) < c2 and waits.get(k2, 0) < c2:
                waits[k2] = c2

        if inc >= 16 and self.cnt[key]:
            need(key, self.cnt[key])
        rk = [k for k in (self._key(a) for a in reads) if k is not None]
        wk = [k for k in (self._key(a) for a in writes) if k is not None]
        for k in rk:
            lw = self.last_w.get(k)
            if lw:
                need(*lw)
        for k in wk:
            lw = self.last_w.get(k)
            if lw:
                need(*lw)
            for k2, c2 in self.readers.get(k, {}).items():
                need(k2, c2)
        after = self.cnt[key] + inc
        self.cnt[key] = after
        for k2, c2 in waits.items():
            self.known[eng][k2] = max(self.known[eng][k2], c2)
        for k in wk:
            self.last_w[k] = (key, after)
            self.readers[k] = {}
        for k in rk:
            d = self.readers.setdefault(k, {})
            d[key] = after
        self.ops[eng].append((waits, emit, inc, key))

    def wait_all(self, eng):
        waits = {}
        for k2 in self.KEYS:
            if self.cnt[k2] > self.known[eng][k2]:
                waits[k2] = self.cnt[k2]
                self.known[eng][k2] = self.cnt[k2]
        if waits:
            self.ops[eng].append((waits, None, 0, None))

    def barrier(self):
        for e in self.ENG:
            self.wait_all(e)

    def dma(self, q, out, in_, extra_reads=(), **kw):
        self.op(q, lambda e: e.dma_start(out=out, in_=in_, **kw),
                reads=[in_, *extra_reads], writes=[out], inc=16)

    def dmat(self, q, out, in_):
        self.op(q, lambda e: e.dma_start(out=out, in_=in_, transpose=True),
                reads=[in_], writes=[out], inc=16)

    def mm(self, out, lhsT, rhs, start, stop):
        self.op("pe", lambda e: e.matmul(out, lhsT, rhs, start=start, stop=stop,
                                         skip_group_check=True),
                reads=[lhsT, rhs], writes=[out], inc=1)

    def act(self, out, in_, func, bias=0.0, scale=1.0, eng="act"):
        er = [a for a in (bias, scale) if hasattr(a, "tensor")]
        self.op(eng, lambda e: e.activation(out, in_, func, bias=bias, scale=scale),
                reads=[in_, *er], writes=[out])

    def ts(self, out, in0, s1, s2, op0, op1=ALU.bypass, eng="dve"):
        er = [a for a in (s1, s2) if hasattr(a, "tensor")]
        self.op(eng, lambda e: e.tensor_scalar(out, in0, s1, s2, op0, op1),
                reads=[in0, *er], writes=[out])

    def tt(self, out, a, b, alu, eng="dve"):
        self.op(eng, lambda e: e.tensor_tensor(out, a, b, alu),
                reads=[a, b], writes=[out])

    def stt(self, out, in0, s, in1, op0, op1, eng="dve"):
        er = [s] if hasattr(s, "tensor") else []
        self.op(eng, lambda e: e.scalar_tensor_tensor(out, in0, s, in1, op0, op1),
                reads=[in0, in1, *er], writes=[out])

    def red(self, out, in_, alu, absval=False, eng="dve"):
        self.op(eng, lambda e: e.tensor_reduce(out, in_, AX.X, alu,
                                               apply_absolute_value=absval),
                reads=[in_], writes=[out])

    def copy(self, out, in_, eng="dve"):
        self.op(eng, lambda e: e.tensor_copy(out, in_), reads=[in_], writes=[out])

    def recip(self, out, in_):
        self.op("dve", lambda e: e.reciprocal(out, in_), reads=[in_], writes=[out])

    def memset(self, ap, v, eng="dve"):
        self.op(eng, lambda e: e.memset(ap, v), reads=[], writes=[ap])

    def finalize(self, blk):
        for e in self.ENG:
            w = {k: self.cnt[k] for k in self.KEYS if k[0] == e and self.cnt[k]}
            if w:
                self.ops[e].append((w, None, 0, None))
        for e in self.ENG:
            def run(eng, _e=e):
                for waits, emit, inc, key in self.ops[_e]:
                    for k2 in self.KEYS:
                        if k2 in waits:
                            eng.wait_ge(self.sems[k2], waits[k2])
                    if emit is not None:
                        ins = emit(eng)
                        if inc:
                            ins.then_inc(self.sems[key], inc)
            getattr(blk, self._BLK[e])(run)


def _sb(nc, ctx, name, shape, dt):
    return ctx.enter_context(nc.sbuf_tensor(name, shape, dt))


def _ps(nc, ctx, name, shape, dt=F32):
    return ctx.enter_context(nc.psum_tensor(name, shape, dt))


def _colmax_to_scalar(P, t, col_f32, out11, psrow):
    P.op("pe", lambda e: e.transpose(psrow[0:1, 0:128], col_f32, t["ident_f"][:]),
         reads=[col_f32, t["ident_f"][:]], writes=[psrow])
    P.copy(t["m1row"][:], psrow[0:1, 0:128])
    P.red(out11, t["m1row"][:], ALU.max)


def _scalar_bcast(P, t, q, src11, dst_col, scratch_dram):
    P.dma("pool", scratch_dram, src11)
    P.dma(q, dst_col, scratch_dram.partition_broadcast(128))


def _emit_quant3(P, out_bf, in_ap, rcol, tmp_f32, tmp2_f32):
    P.ts(tmp_f32, in_ap, rcol, C_RND, ALU.mult, ALU.add)
    P.ts(tmp2_f32, tmp_f32, C_RND, 127.0, ALU.subtract, ALU.min)
    P.ts(out_bf, tmp2_f32, -128.0, None, ALU.max)


def build_l1():
    nc = bass.Bass("TRN2", target_bir_lowering=False)
    d = {}
    d["x"] = nc.dram_tensor("x", [BPC, N, C], F32, kind="ExternalInput").ap()
    d["wq"] = nc.dram_tensor("wq", [C, C], F32, kind="ExternalInput").ap()
    d["wk"] = nc.dram_tensor("wk", [C, C], F32, kind="ExternalInput").ap()
    d["bq"] = nc.dram_tensor("bq", [128, KC], F32, kind="ExternalInput").ap()
    d["bk"] = nc.dram_tensor("bk", [1, C], F32, kind="ExternalInput").ap()
    d["lw"] = nc.dram_tensor("lw", [3, C, C], F32, kind="ExternalInput").ap()
    d["cosF"] = nc.dram_tensor("cosF", [512, N], F32, kind="ExternalInput").ap()
    d["sinF"] = nc.dram_tensor("sinF", [512, N], F32, kind="ExternalInput").ap()
    d["cosT"] = nc.dram_tensor("cosT", [N, 512], F32, kind="ExternalInput").ap()
    d["sinT"] = nc.dram_tensor("sinT", [N, 512], F32, kind="ExternalInput").ap()
    d["eye8"] = nc.dram_tensor("eye8", [8, 8], F32, kind="ExternalInput").ap()
    d["ident"] = nc.dram_tensor("ident", [128, 128], F32, kind="ExternalInput").ap()
    d["mask"] = nc.dram_tensor("mask", [8, 128, 8], F32, kind="ExternalInput").ap()
    d["S_out"] = nc.dram_tensor("S_out", [BPC, 128, NT, C], BF16, kind="ExternalOutput").ap()
    d["xn_out"] = nc.dram_tensor("xn_out", [BPC, 128, NT, C], F32, kind="ExternalOutput").ap()
    d["m1_out"] = nc.dram_tensor("m1_out", [1, 1], F32, kind="ExternalOutput").ap()
    d["cosT_s"] = nc.dram_tensor("cosT_s", [N, 512], BF16, kind="Internal").ap()
    d["sinT_s"] = nc.dram_tensor("sinT_s", [N, 512], BF16, kind="Internal").ap()
    d["cosF_s"] = nc.dram_tensor("cosF_s", [512, N], BF16, kind="Internal").ap()
    d["sinF_s"] = nc.dram_tensor("sinF_s", [512, N], BF16, kind="Internal").ap()
    d["wq_s"] = nc.dram_tensor("wq_s", [128, KC, C], BF16, kind="Internal").ap()
    if DEBUG_L1:
        d["kv_dbg"] = nc.dram_tensor("kv_dbg", [128, C], F32, kind="ExternalOutput").ap()
        d["z_dbg"] = nc.dram_tensor("z_dbg", [128, NT * 8], F32, kind="ExternalOutput").ap()
        d["qr_dbg"] = nc.dram_tensor("qr_dbg", [128, KC, N], F32, kind="ExternalOutput").ap()
        d["q1_dbg"] = nc.dram_tensor("q1_dbg", [128, N], F32, kind="ExternalOutput").ap()
        d["kr_dbg"] = nc.dram_tensor("kr_dbg", [128, C], F32, kind="ExternalOutput").ap()
        d["kr2_dbg"] = nc.dram_tensor("kr2_dbg", [128, C], BF16, kind="ExternalOutput").ap()
        d["t1_dbg"] = nc.dram_tensor("t1_dbg", [128, C], F32, kind="ExternalOutput").ap()
        d["t2_dbg"] = nc.dram_tensor("t2_dbg", [128, C], F32, kind="ExternalOutput").ap()
        d["k1_dbg"] = nc.dram_tensor("k1_dbg", [128, C], F32, kind="ExternalOutput").ap()
        d["tb_dbg"] = nc.dram_tensor("tb_dbg", [128, 2, 512], F32, kind="ExternalOutput").ap()
        d["psk_dbg"] = nc.dram_tensor("psk_dbg", [128, C], F32, kind="ExternalOutput").ap()
        d["psk2_dbg"] = nc.dram_tensor("psk2_dbg", [128, C], F32, kind="ExternalOutput").ap()
        d["km_dbg"] = nc.dram_tensor("km_dbg", [128, KC], F32, kind="ExternalOutput").ap()
        d["xnfm_dbg"] = nc.dram_tensor("xnfm_dbg", [128, KC, NF], BF16, kind="ExternalOutput").ap()
        d["psq_dbg"] = nc.dram_tensor("psq_dbg", [128, C], F32, kind="ExternalOutput").ap()

    with ExitStack() as ctx:
        t = {}
        t["ident_f"] = _sb(nc, ctx, "ident_f", [128, 128], F32)
        t["mask_sb"] = _sb(nc, ctx, "mask_sb", [128, 8, 8], F32)
        t["bq_sb"] = _sb(nc, ctx, "bq_sb", [128, KC], F32)
        t["nbq_sb"] = _sb(nc, ctx, "nbq_sb", [128, KC], F32)
        t["bkb"] = _sb(nc, ctx, "bkb", [1, C], BF16)
        t["ones_row"] = _sb(nc, ctx, "ones_row", [1, 128], BF16)
        t["ones_col"] = _sb(nc, ctx, "ones_col", [128, 1], BF16)
        t["ones11"] = _sb(nc, ctx, "ones11", [1, 1], BF16)
        t["krhs"] = _sb(nc, ctx, "krhs", [128, KC, C], BF16)
        t["lw_sb"] = _sb(nc, ctx, "lw_sb", [128, 24, C], BF16)
        t["wq_buf"] = [_sb(nc, ctx, f"wq{i}", [128, KC, 128], BF16) for i in range(2)]
        t["tabF"] = _sb(nc, ctx, "tabF", [128, 2, 1024], BF16)
        t["tabT"] = [_sb(nc, ctx, f"tabT{i}", [128, 2, 512], BF16) for i in range(2)]
        t["tabT2"] = [_sb(nc, ctx, f"tabT2_{i}", [128, 2, 512], BF16) for i in range(2)] if DIAG_TAB_BOUNCE else None
        t["xn_ring"] = _sb(nc, ctx, "xn_ring", [128, 2, C], BF16)
        t["vh_ring"] = [_sb(nc, ctx, f"vh{i}", [128, C], BF16) for i in range(3)]
        t["qr"] = _sb(nc, ctx, "qr", [128, KC * N], BF16)
        t["xn_fm"] = _sb(nc, ctx, "xn_fm", [128, KC, NF], BF16)
        t["bst"] = [_sb(nc, ctx, f"bst{i}", [128, 12], F32) for i in range(2)]
        t["mv"] = [_sb(nc, ctx, f"mv{i}", [128, 2], F32) for i in range(2)]
        t["vtmp"] = [_sb(nc, ctx, f"vtmp{i}", [128, 1], F32) for i in range(2)]
        t["rstd"] = [_sb(nc, ctx, f"rstd{i}", [128, 1], F32) for i in range(2)]
        t["rstd2"] = [_sb(nc, ctx, f"rstd2{i}", [128, 1], F32) for i in range(2)]
        t["nbias"] = [_sb(nc, ctx, f"nbias{i}", [128, 1], F32) for i in range(2)]
        t["t1"] = [_sb(nc, ctx, f"t1_{i}", [128, C], BF16) for i in range(2)]
        t["t2"] = [_sb(nc, ctx, f"t2_{i}", [128, C], BF16) for i in range(2)]
        t["x_buf"] = [_sb(nc, ctx, f"xbuf{i}", [128, C], F32) for i in range(2)]
        t["k1"] = [_sb(nc, ctx, f"k1_{i}", [128, C], BF16) for i in range(3)]
        t["kr"] = [_sb(nc, ctx, f"kr_{i}", [128, C], BF16) for i in range(3)]
        t["kv_sb"] = _sb(nc, ctx, "kv_sb", [128, C], BF16)
        t["km_bf"] = _sb(nc, ctx, "km_bf", [1, C], BF16)
        t["km_cols"] = _sb(nc, ctx, "km_cols", [128, KC], F32)
        t["A_sb"] = _sb(nc, ctx, "A_sb", [128, KC, 8], BF16)
        t["q1"] = [_sb(nc, ctx, f"q1_{i}", [128, N], BF16) for i in range(2)]
        t["r1"] = [_sb(nc, ctx, f"r1_{i}", [128, C], BF16) for i in range(2)]
        t["r2k"] = [_sb(nc, ctx, f"r2k_{i}", [128, C], BF16) for i in range(2)]
        t["eye8_f"] = _sb(nc, ctx, "eye8_f", [8, 8], F32)
        t["z_tm"] = _sb(nc, ctx, "z_tm", [128, NT, 8], F32)
        t["zrow"] = _sb(nc, ctx, "zrow", [128, 64], F32)
        t["S_buf"] = [_sb(nc, ctx, f"Sb{i}", [128, C], BF16) for i in range(2)]
        t["psq_sav"] = None
        t["t5_dbg"] = None
        t["mx"] = _sb(nc, ctx, "mx", [128, BPC * NT], F32)
        t["mxf"] = _sb(nc, ctx, "mxf", [128, 1], F32)
        t["m1row"] = _sb(nc, ctx, "m1row", [1, 128], F32)
        t["m1v"] = _sb(nc, ctx, "m1v", [1, 1], F32)
        P0 = _ps(nc, ctx, "P0", [128, 1024])
        P1 = _ps(nc, ctx, "P1", [128, 1024])
        P2 = _ps(nc, ctx, "P2", [128, 1024])
        P3 = _ps(nc, ctx, "P3", [128, 1024])

        P = Prog(nc, ctx)
        P.dma("sync", t["ident_f"][:], d["ident"])
        P.dma("sync", t["eye8_f"][:], d["eye8"])
        P.dma("sync", t["mask_sb"][:], d["mask"].rearrange("c p h -> p c h"))
        P.dma("sync", t["bq_sb"][:], d["bq"])
        P.ts(t["nbq_sb"][:], t["bq_sb"][:], -1.0, None, ALU.mult)
        P.dma("pool", t["bkb"][:], d["bk"])
        P.memset(t["ones_row"][:], 1.0)
        P.memset(t["ones_col"][:], 1.0)
        P.memset(t["ones11"][:], 1.0)
        P.op("dve", lambda e: e.memset(t["xn_fm"][:], 0.0),
             reads=[], writes=[f"xnfm{i}" for i in range(NT)])
        P.dma("pool", t["krhs"][:], d["wk"].rearrange("(kc p) n -> p kc n", p=128))
        P.dma("pool", t["lw_sb"][:], d["lw"].rearrange("t (kc p) m -> p (t kc) m", p=128))
        P.dma("pool", d["cosT_s"], d["cosT"])
        P.dma("pool", d["sinT_s"], d["sinT"])
        P.dma("pool", d["cosF_s"], d["cosF"])
        P.dma("pool", d["sinF_s"], d["sinF"])
        P.dma("pool", d["wq_s"], d["wq"].rearrange("(kc p) m -> p kc m", p=128))

        qr = t["qr"][:].rearrange("p (c n) -> p c n", c=KC)

        XK = [f"xnfm{i}" for i in range(NT)]

        def emit_ln(b, tau):
            xb = t["x_buf"][tau % 2]
            P.dma("pool", xb[:], d["x"][b, 128 * tau:128 * (tau + 1), :])
            bst = t["bst"][tau % 2]
            mv = t["mv"][tau % 2]
            vtmp = t["vtmp"][tau % 2]
            rstd = t["rstd"][tau % 2]
            rstd2 = t["rstd2"][tau % 2]
            nbias = t["nbias"][tau % 2]
            for g in range(2):
                P.op("dve", lambda e, o=bst[:, 6 * g:6 * (g + 1)],
                     i=xb[:, 512 * g:512 * (g + 1)]: e.bn_stats(o, i),
                     reads=[xb[:]], writes=[bst[:]])
            P.op("dve", lambda e, o=mv[:], i=bst[:]: e.bn_aggr(o, i),
                 reads=[bst[:]], writes=[mv[:]])
            P.ts(vtmp[:], mv[:, 1:2], 1e-5, None, ALU.add)
            P.act(rstd[:], vtmp[:], AF.Ln)
            P.act(rstd2[:], rstd[:], AF.Exp, scale=-0.5)
            P.stt(nbias[:], mv[:, 0:1], -1.0, rstd2[:, 0:1], ALU.mult, ALU.mult)
            P.act(xb[:], xb[:], AF.Identity, bias=nbias[:, 0:1],
                  scale=rstd2[:, 0:1])
            P.copy(t["xn_ring"][:, tau % 2, :], xb[:])
            P.dma("pool", d["xn_out"][b, :, tau, :], xb[:])
            P.red(t["mx"][:, b * NT + tau: b * NT + tau + 1], xb[:],
                  ALU.max, absval=True)
            out = t["xn_fm"][:, :, GUARD + 128 * tau: GUARD + 128 * (tau + 1)]
            srcv = t["xn_ring"][:, tau % 2, :].rearrange("p (c n) -> p c n", c=KC)
            P.op("sync", lambda e, o=out, s=srcv: e.dma_start(out=o, in_=s, transpose=True),
                 reads=[srcv], writes=[XK[tau]], inc=16)

        for b in range(BPC):
            def emit_kv(tau):
                k1 = t["k1"][tau % 3]
                kr = t["kr"][tau % 3]
                for nh in range(2):
                    P.mm(P3[0:1, 512 * nh:512 * (nh + 1)], t["ones_col"][:],
                         k1[:, 512 * nh:512 * (nh + 1)],
                         start=(tau == 0), stop=(tau == NT - 1))
                for h in range(8):
                    P.mm(P2[:, 128 * h:128 * (h + 1)],
                         kr[:, 128 * h:128 * (h + 1)],
                         t["vh_ring"][tau % 3][:, 128 * h:128 * (h + 1)],
                         start=(tau == 0 and h % 4 == 0), stop=(tau == NT - 1))

            if b == 0:
                for tau in range(NT):
                    emit_ln(0, tau)
            for tau in range(NT):
                for k in range(KC):
                    vout = t["vh_ring"][tau % 3][:, 128 * k:128 * (k + 1)]
                    vin = t["xn_fm"][:, k, GUARD + 128 * tau: GUARD + 128 * (tau + 1)]
                    P.op("sync", lambda e, o=vout, s=vin:
                         e.dma_start(out=o, in_=s, transpose=True),
                         reads=[XK[tau]], writes=[t["vh_ring"][tau % 3][:]], inc=16)
                tb = t["tabT"][tau % 2]
                P.dma("act", tb[:, 0, :], d["cosT_s"][128 * tau:128 * (tau + 1), :])
                P.dma("act", tb[:, 1, :], d["sinT_s"][128 * tau:128 * (tau + 1), :])
                psK = (P0, P1)[tau % 2]
                for k in range(KC):
                    lhsT = t["xn_fm"][:, k, GUARD + 128 * tau: GUARD + 128 * (tau + 1)]
                    for nh in range(2):
                        P.op("pe", lambda e, o=psK[:, 512 * nh:512 * (nh + 1)], l=lhsT,
                             r=t["krhs"][:, k, 512 * nh:512 * (nh + 1)],
                             st=(k == 0): e.matmul(o, l, r, start=st, stop=False,
                                                   skip_group_check=True),
                             reads=[XK[tau], t["krhs"][:]], writes=[psK[:]], inc=1)
                for nh in range(2):
                    P.mm(psK[:, 512 * nh:512 * (nh + 1)], t["ones_row"][:],
                         t["bkb"][0:1, 512 * nh:512 * (nh + 1)], start=False, stop=True)
                if tau > 1:
                    emit_kv(tau - 2)
                k1 = t["k1"][tau % 3]
                t1 = t["t1"][tau % 2]
                t2 = t["t2"][tau % 2]
                P.act(t2[:], psK[:], AF.Relu, scale=-1.0)
                P.act(t1[:], t2[:], AF.Exp, scale=-1.0)
                P.act(t2[:], psK[:], AF.Relu)
                P.tt(k1[:], t1[:], t2[:], ALU.add)
                kr = t["kr"][tau % 3]
                cosT_, sinT_ = tb[:, 0, :], tb[:, 1, :]
                r1 = t["r1"][tau % 2]
                r2k = t["r2k"][tau % 2]
                P.tt(r1[:, 0:512], cosT_, k1[:, 0:512], ALU.mult)
                P.tt(r1[:, 512:1024], sinT_, k1[:, 512:1024], ALU.mult)
                P.tt(kr[:, 0:512], r1[:, 0:512], r1[:, 512:1024],
                     ALU.subtract)
                P.tt(r2k[:, 0:512], cosT_, k1[:, 512:1024], ALU.mult)
                P.tt(r2k[:, 512:1024], sinT_, k1[:, 0:512], ALU.mult)
                P.tt(kr[:, 512:1024], r2k[:, 0:512], r2k[:, 512:1024],
                     ALU.add)
                if DEBUG_L1 and b == 0 and tau == 0:
                    P.dma("pool", d["kr_dbg"], kr[:])
                    P.dma("act", d["kr2_dbg"], kr[:])
                    P.dma("pool", d["t1_dbg"], t["t1"][tau % 2][:])
                    P.dma("pool", d["t2_dbg"], t["t2"][tau % 2][:])
                    P.dma("pool", d["k1_dbg"], k1[:])
                    P.dma("pool", d["tb_dbg"], tb[:])
            emit_kv(NT - 2)
            emit_kv(NT - 1)
            P.ts(t["kv_sb"][:], P2[:], RSQRT_N, None, ALU.mult)
            P.ts(t["km_bf"][:], P3[0:1, :], 1.0 / N, None, ALU.mult)
            for c in range(KC):
                P.mm(P2[:, c:c + 1], t["km_bf"][0:1, 128 * c:128 * (c + 1)],
                     t["ones11"][:], start=(c == 0), stop=(c == KC - 1))
            P.copy(t["km_cols"][:], P2[:, 0:KC])
            for c in range(KC):
                P.ts(t["A_sb"][:, c, :], t["mask_sb"][:, c, :],
                     t["km_cols"][:, c:c + 1], None, ALU.mult)

            def emit_zmm(m, zi):
                q1 = t["q1"][zi % 2]
                for th in range(2):
                    psZ = (P2, P3)[th]
                    for ns in range(2):
                        lo = 1024 * th + 512 * ns
                        P.mm(psZ[0:8, 512 * ns:512 * (ns + 1)], t["A_sb"][:, m, :],
                             q1[:, lo:lo + 512], start=(zi == 0), stop=(zi == 7))

            pair_done = {}
            order = [0, 4, 1, 5, 2, 6, 3, 7]
            for mi, m in enumerate(order):
                wqb = t["wq_buf"][mi % 2]
                P.dma("sync", wqb[:], d["wq_s"][:, :, 128 * m:128 * (m + 1)])
                q1 = t["q1"][mi % 2]
                for th in range(2):
                    psQ = (P0, P1)[th]
                    for k in range(KC):
                        for ns in range(2):
                            lo = 1024 * th + 512 * ns
                            seg = 2 * th + ns
                            P.op("pe", lambda e, o=psQ[:, 512 * ns:512 * (ns + 1)],
                                 l=wqb[:, k, :],
                                 r=t["xn_fm"][:, k, GUARD + lo: GUARD + lo + 512],
                                 st=(k == 0), sp=(k == KC - 1):
                                 e.matmul(o, l, r, start=st, stop=sp,
                                          skip_group_check=True),
                                 reads=[wqb[:], *(XK[4 * seg + j] for j in range(4))],
                                 writes=[psQ[:]], inc=1)
                    if th == 1 and mi > 0:
                        emit_zmm(order[mi - 1], mi - 1)
                    sl_q = slice(1024 * th, 1024 * (th + 1))
                    t1 = t["t1"][th]
                    t2 = t["t2"][th]
                    P.act(t2[:], psQ[:], AF.Relu, scale=-1.0,
                          bias=t["nbq_sb"][:, m:m + 1])
                    P.act(t1[:], t2[:], AF.Exp, scale=-1.0)
                    P.act(t2[:], psQ[:], AF.Relu, bias=t["bq_sb"][:, m:m + 1])
                    P.tt(q1[:, sl_q], t2[:], t1[:], ALU.add)
                if DEBUG_L1 and b == 0 and m == 0:
                    P.dma("pool", d["q1_dbg"], q1[:])
                pair_done[m] = q1
                cpair = m if m < 4 else m - 4
                if cpair in pair_done and cpair + 4 in pair_done:
                    qE, qO = pair_done[cpair], pair_done[cpair + 4]
                    for th in range(2):
                        sl = slice(1024 * th, 1024 * (th + 1))
                        tf = t["tabF"]
                        r1 = t["r1"][th]
                        r2k = t["r2k"][th]
                        P.dma("act", tf[:, 0, :],
                              d["cosF_s"][128 * cpair:128 * (cpair + 1), sl])
                        P.dma("act", tf[:, 1, :],
                              d["sinF_s"][128 * cpair:128 * (cpair + 1), sl])
                        cosF_, sinF_ = tf[:, 0, :], tf[:, 1, :]
                        P.tt(r1[:], cosF_, qE[:, sl], ALU.mult)
                        P.tt(r2k[:], sinF_, qO[:, sl], ALU.mult)
                        P.tt(qr[:, cpair, sl], r1[:], r2k[:], ALU.subtract)
                        P.tt(r1[:, 0:512], cosF_[:, 0:512], qO[:, sl][:, 0:512], ALU.mult)
                        P.tt(r1[:, 512:1024], sinF_[:, 0:512], qE[:, sl][:, 0:512], ALU.mult)
                        P.tt(qr[:, cpair + 4, sl][:, 0:512], r1[:, 0:512],
                             r1[:, 512:1024], ALU.add)
                        P.tt(r2k[:, 0:512], cosF_[:, 512:1024], qO[:, sl][:, 512:1024], ALU.mult)
                        P.tt(r2k[:, 512:1024], sinF_[:, 512:1024], qE[:, sl][:, 512:1024], ALU.mult)
                        P.tt(qr[:, cpair + 4, sl][:, 512:1024], r2k[:, 0:512],
                             r2k[:, 512:1024], ALU.add)
                    del pair_done[cpair], pair_done[cpair + 4]
            emit_zmm(order[7], 7)
            for th in range(2):
                psZ = (P2, P3)[th]
                P.ts(t["x_buf"][0][0:8, 0:1024], psZ[0:8, :], 1e-6, None, ALU.add)
                for tt8 in range(8):
                    P.mm(P0[:, 8 * tt8:8 * (tt8 + 1)],
                         t["x_buf"][0][0:8, 128 * tt8:128 * (tt8 + 1)], t["eye8_f"][:],
                         start=(tt8 == 0), stop=True)
                P.copy(t["zrow"][:], P0[:, 0:64])
                P.recip(t["z_tm"][:, 8 * th:8 * (th + 1), :],
                        t["zrow"][:].rearrange("p (a e) -> p a e", a=8))

            for tau in range(NT):
                psA = (P0, P1)[tau % 2]
                psL = (P2, P3)[tau % 2]
                for h in range(8):
                    P.mm(psA[:, 128 * h:128 * (h + 1)],
                         qr[:, h, 128 * tau:128 * (tau + 1)],
                         t["kv_sb"][:, 128 * h:128 * (h + 1)],
                         start=(h % 4 == 0), stop=True)
                nmm = 0
                for tap in range(3):
                    for k in range(KC):
                        lhsT = t["xn_fm"][:, k,
                                          GUARD + 128 * tau + tap - 1:
                                          GUARD + 128 * tau + tap - 1 + 128]
                        rkeys = [XK[j] for j in (tau - 1, tau, tau + 1)
                                 if 0 <= j < NT]
                        for nh in range(2):
                            P.op("pe", lambda e, o=psL[:, 512 * nh:512 * (nh + 1)],
                                 l=lhsT,
                                 r=t["lw_sb"][:, 8 * tap + k, 512 * nh:512 * (nh + 1)],
                                 st=(nmm == 0), sp=(nmm == 23):
                                 e.matmul(o, l, r, start=st, stop=sp,
                                          skip_group_check=True),
                                 reads=[t["lw_sb"][:], *rkeys], writes=[psL[:]], inc=1)
                        nmm += 1
                zb = t["z_tm"][:, tau, :].unsqueeze(2).broadcast_to([128, 8, 128])
                Sb = t["S_buf"][tau % 2]
                P.tt(Sb[:].rearrange("p (h n) -> p h n", h=8),
                     psA[:].rearrange("p (h n) -> p h n", h=8), zb, ALU.mult)
                P.tt(Sb[:], Sb[:], psL[:], ALU.add)
                P.dma("act", d["S_out"][b, :, tau, :], Sb[:])
                if tau >= 1 and b + 1 < BPC:
                    emit_ln(b + 1, tau - 1)
            if b + 1 < BPC:
                emit_ln(b + 1, NT - 1)
            if DEBUG_L1 and b == 0:
                P.op("pool", lambda e: e.dma_start(out=d["xnfm_dbg"], in_=t["xn_fm"][:]),
                     reads=list(XK), writes=[d["xnfm_dbg"]], inc=16)
                P.dma("pool", d["kv_dbg"], t["kv_sb"][:])
                P.dma("pool", d["z_dbg"], t["z_tm"][:].rearrange("p a e -> p (a e)"))
                P.dma("pool", d["qr_dbg"], qr)
                P.dma("pool", d["km_dbg"], t["km_cols"][:])

        P.red(t["mxf"][:], t["mx"][:], ALU.max)
        _colmax_to_scalar(P, t, t["mxf"][:], t["m1v"][:], P0)
        P.dma("pool", d["m1_out"], t["m1v"][:])
        P.wait_all("pool")

        blk = ctx.enter_context(nc.Block())
        P.finalize(blk)
    return nc


def build_l2():
    nc = bass.Bass("TRN2", target_bir_lowering=False)
    d = {}
    d["xn_in"] = nc.dram_tensor("xn_in", [BPC, 128, NT, C], F32, kind="ExternalInput").ap()
    d["m1vec"] = nc.dram_tensor("m1vec", [1, NCORES], F32, kind="ExternalInput").ap()
    d["clip_x"] = nc.dram_tensor("clip_x", [1, 1], F32, kind="ExternalInput").ap()
    d["wg"] = nc.dram_tensor("wg", [I, C], F32, kind="ExternalInput").ap()
    d["wu"] = nc.dram_tensor("wu", [I, C], F32, kind="ExternalInput").ap()
    d["ident"] = nc.dram_tensor("ident", [128, 128], F32, kind="ExternalInput").ap()
    d["h_out"] = nc.dram_tensor("h_out", [BPC, 128, IC, N], F32, kind="ExternalOutput").ap()
    d["m2_out"] = nc.dram_tensor("m2_out", [1, 1], F32, kind="ExternalOutput").ap()
    scr = nc.dram_tensor("scr11", [1, 1], F32, kind="Internal").ap()

    with ExitStack() as ctx:
        t = {}
        t["ident_f"] = _sb(nc, ctx, "ident_f", [128, 128], F32)
        t["wgT"] = _sb(nc, ctx, "wgT", [128, KC, I], BF16)
        t["wuT"] = _sb(nc, ctx, "wuT", [128, KC, I], BF16)
        t["rmax"] = _sb(nc, ctx, "rmax", [128, 1], F32)
        t["rq"] = _sb(nc, ctx, "rq", [128, 1], F32)
        t["swg"] = _sb(nc, ctx, "swg", [128, IC], F32)
        t["swu"] = _sb(nc, ctx, "swu", [128, IC], F32)
        t["sgc"] = _sb(nc, ctx, "sgc", [128, IC], F32)
        t["suc"] = _sb(nc, ctx, "suc", [128, IC], F32)
        t["m1s"] = _sb(nc, ctx, "m1s", [1, NCORES], F32)
        t["m1"] = _sb(nc, ctx, "m1", [1, 1], F32)
        t["sig"] = _sb(nc, ctx, "sig", [1, 1], F32)
        t["s1"] = _sb(nc, ctx, "s1", [1, 1], F32)
        t["r1"] = _sb(nc, ctx, "r1", [1, 1], F32)
        t["r1b"] = _sb(nc, ctx, "r1b", [128, 1], F32)
        t["s1b"] = _sb(nc, ctx, "s1b", [128, 1], F32)
        t["hm"] = _sb(nc, ctx, "hm", [128, 1], F32)
        t["hmx"] = _sb(nc, ctx, "hmx", [128, 1], F32)
        t["m1row"] = _sb(nc, ctx, "m1row", [1, 128], F32)
        t["m2v"] = _sb(nc, ctx, "m2v", [1, 1], F32)
        P0 = _ps(nc, ctx, "P0", [128, 512])
        P1 = _ps(nc, ctx, "P1", [128, 512])
        P2 = _ps(nc, ctx, "P2", [128, 512])
        P3 = _ps(nc, ctx, "P3", [128, 512])

        P = Prog(nc, ctx)
        P.dma("sync", t["ident_f"][:], d["ident"])
        P.dma("sync", t["m1s"][:], d["m1vec"])
        P.red(t["m1"][:], t["m1s"][:], ALU.max)
        P.dma("sync", t["sig"][:], d["clip_x"])
        P.act(t["sig"][:], t["sig"][:], AF.Sigmoid)
        P.tt(t["s1"][:], t["sig"][:], t["m1"][:], ALU.mult)
        P.ts(t["s1"][:], t["s1"][:], 1.0 / 127.0, None, ALU.mult)
        P.recip(t["r1"][:], t["s1"][:])
        _scalar_bcast(P, t, "sync", t["s1"][:], t["s1b"][:], scr)
        _scalar_bcast(P, t, "sync", t["r1"][:], t["r1b"][:], scr)
        with ExitStack() as c2:
            wnat = _sb(nc, c2, "wnat", [128, KC, IC, 128], BF16)
            wrow = [_sb(nc, c2, f"wrow{i}", [128, C], F32) for i in range(2)]
            qt = _sb(nc, c2, "qt", [128, C], F32)
            for wsrc, wT, sw in ((d["wg"], t["wgT"], t["swg"]), (d["wu"], t["wuT"], t["swu"])):
                for cc in range(IC):
                    wr = wrow[cc % 2]
                    P.dma("sync", wr[:], wsrc.rearrange("(cc p) m -> p cc m", p=128)[:, cc, :])
                    P.red(t["rmax"][:], wr[:], ALU.max, absval=True)
                    P.recip(t["rq"][:], t["rmax"][:])
                    P.ts(t["rq"][:], t["rq"][:], 127.0, None, ALU.mult)
                    P.ts(sw[:, cc:cc + 1], t["rmax"][:], 1.0 / 127.0, None, ALU.mult)
                    P.ts(qt[:], wr[:], t["rq"][:, 0:1], C_RND, ALU.mult, ALU.add)
                    P.ts(wnat[:, :, cc, :], qt[:].rearrange("p (h n) -> p h n", h=KC),
                         C_RND, None, ALU.subtract)
                for hc in range(KC):
                    P.dmat("act", wT[:, hc, :].rearrange("p (i n) -> p i n", i=IC),
                           wnat[:, hc, :, :])
            P.barrier()
        t["xq"] = [_sb(nc, ctx, f"xq{i}", [128, KC, N], BF16) for i in range(2)]
        t["xnb"] = [_sb(nc, ctx, f"xnb{i}", [128, C], F32) for i in range(2)]
        t["xqt"] = [_sb(nc, ctx, f"xqt{i}", [128, C], BF16) for i in range(2)]
        t["qtmp"] = _sb(nc, ctx, "qtmp", [128, C], F32)
        t["qtmp2"] = _sb(nc, ctx, "qtmp2", [128, C], F32)
        t["sg_buf"] = _sb(nc, ctx, "sg_buf", [128, 512], F32)
        t["up_buf"] = _sb(nc, ctx, "up_buf", [128, 512], F32)
        t["h_buf"] = [_sb(nc, ctx, f"hb{i}", [128, 512], F32) for i in range(2)]
        P.tt(t["sgc"][:], t["swg"][:], t["s1b"][:, 0:1].broadcast_to([128, IC]), ALU.mult)
        P.tt(t["suc"][:], t["swu"][:], t["s1b"][:, 0:1].broadcast_to([128, IC]), ALU.mult)
        P.memset(t["hmx"][:], 0.0)

        def emit_quant_tau(g, j):
            gb, gs = divmod(g, 4)
            tau = 4 * gs + j
            xnb = t["xnb"][tau % 2]
            xqt = t["xqt"][tau % 2]
            P.dma("sync", xnb[:], d["xn_in"][gb, :, tau, :])
            _emit_quant3(P, xqt[:], xnb[:], t["r1b"][:, 0:1], t["qtmp"][:], t["qtmp2"][:])
            out = t["xq"][gb % 2][:, :, 128 * tau:128 * (tau + 1)]
            src = xqt[:].rearrange("p (c n) -> p c n", c=KC)
            P.op("act", lambda e, o=out, s=src: e.dma_start(out=o, in_=s, transpose=True),
                 reads=[src], writes=[f"xq{gb % 2}s{gs}"], inc=16)

        for j in range(4):
            emit_quant_tau(0, j)
        SLOTS = (4, 9, 14, 19)
        for b in range(BPC):
            for s in range(4):
                g = 4 * b + s
                lo = 512 * s
                xqb = t["xq"][b % 2]
                xqkey = f"xq{b % 2}s{s}"
                for m in range(IC):
                    psG = (P0, P1)[m % 2]
                    psU = (P2, P3)[m % 2]
                    for k in range(KC):
                        P.op("pe", lambda e, o=psG[:], l=t["wgT"][:, k, 128 * m:128 * (m + 1)],
                             r=xqb[:, k, lo:lo + 512], st=(k == 0), sp=(k == KC - 1):
                             e.matmul(o, l, r, start=st, stop=sp, skip_group_check=True),
                             reads=[t["wgT"][:, k, 128 * m:128 * (m + 1)], xqkey],
                             writes=[psG[:]], inc=1)
                        P.op("pe", lambda e, o=psU[:], l=t["wuT"][:, k, 128 * m:128 * (m + 1)],
                             r=xqb[:, k, lo:lo + 512], st=(k == 0), sp=(k == KC - 1):
                             e.matmul(o, l, r, start=st, stop=sp, skip_group_check=True),
                             reads=[t["wuT"][:, k, 128 * m:128 * (m + 1)], xqkey],
                             writes=[psU[:]], inc=1)
                    P.act(t["sg_buf"][:], psG[:], AF.Silu, scale=t["sgc"][:, m:m + 1])
                    P.ts(t["up_buf"][:], psU[:], t["suc"][:, m:m + 1], None, ALU.mult)
                    hb = t["h_buf"][m % 2]
                    P.tt(hb[:], t["sg_buf"][:], t["up_buf"][:], ALU.mult)
                    P.red(t["hm"][:], hb[:], ALU.max, absval=True)
                    P.tt(t["hmx"][:], t["hmx"][:], t["hm"][:], ALU.max)
                    P.dma("pool", d["h_out"][b, :, m, lo:lo + 512], hb[:])
                    if m in SLOTS and g + 1 <= 4 * BPC - 1:
                        emit_quant_tau(g + 1, SLOTS.index(m))

        _colmax_to_scalar(P, t, t["hmx"][:], t["m2v"][:], P0)
        P.dma("pool", d["m2_out"], t["m2v"][:])
        P.wait_all("pool")
        blk = ctx.enter_context(nc.Block())
        P.finalize(blk)
    return nc


def build_l3():
    nc = bass.Bass("TRN2", target_bir_lowering=False)
    d = {}
    d["h_in"] = nc.dram_tensor("h_in", [BPC, 128, IC, N], F32, kind="ExternalInput").ap()
    d["m2vec"] = nc.dram_tensor("m2vec", [1, NCORES], F32, kind="ExternalInput").ap()
    d["clip_d"] = nc.dram_tensor("clip_d", [1, 1], F32, kind="ExternalInput").ap()
    d["wd"] = nc.dram_tensor("wd", [C, I], F32, kind="ExternalInput").ap()
    d["S_in"] = nc.dram_tensor("S_in", [BPC, 128, NT, C], BF16, kind="ExternalInput").ap()
    d["x"] = nc.dram_tensor("x", [BPC, N, C], F32, kind="ExternalInput").ap()
    d["lb"] = nc.dram_tensor("lb", [1, C], F32, kind="ExternalInput").ap()
    d["y"] = nc.dram_tensor("y", [BPC, N, C], F32, kind="ExternalOutput").ap()
    scr = nc.dram_tensor("scr11", [1, 1], F32, kind="Internal").ap()
    HH = (IC + 1) // 2
    hq2s = nc.dram_tensor("hq2s", [2, 128, IC - HH, N], BF16, kind="Internal").ap()

    with ExitStack() as ctx:
        t = {}
        t["wdT"] = _sb(nc, ctx, "wdT", [128, IC, C], BF16)
        t["rmax"] = _sb(nc, ctx, "rmax", [128, 1], F32)
        t["rq"] = _sb(nc, ctx, "rq", [128, 1], F32)
        t["sw"] = _sb(nc, ctx, "sw", [128, 1], F32)
        t["m2s"] = _sb(nc, ctx, "m2s", [1, NCORES], F32)
        t["m2"] = _sb(nc, ctx, "m2", [1, 1], F32)
        t["sig"] = _sb(nc, ctx, "sig", [1, 1], F32)
        t["s2"] = _sb(nc, ctx, "s2", [1, 1], F32)
        t["r2"] = _sb(nc, ctx, "r2", [1, 1], F32)
        t["r2b"] = _sb(nc, ctx, "r2b", [128, 1], F32)
        t["s2b"] = _sb(nc, ctx, "s2b", [128, 1], F32)
        t["brep"] = _sb(nc, ctx, "brep", [128, C], BF16)
        t["hb"] = [_sb(nc, ctx, f"hb{i}", [128, C], F32) for i in range(2)]
        t["Sb"] = [_sb(nc, ctx, f"Sb{i}", [128, C], BF16) for i in range(2)]
        t["xb"] = [_sb(nc, ctx, f"xb{i}", [128, C], F32) for i in range(2)]
        t["ub"] = [_sb(nc, ctx, f"ub{i}", [128, C], F32) for i in range(2)]
        P0 = _ps(nc, ctx, "P0", [128, 1024])
        P1 = _ps(nc, ctx, "P1", [128, 1024])

        P = Prog(nc, ctx)
        P.dma("sync", t["m2s"][:], d["m2vec"])
        P.red(t["m2"][:], t["m2s"][:], ALU.max)
        P.dma("sync", t["sig"][:], d["clip_d"])
        P.act(t["sig"][:], t["sig"][:], AF.Sigmoid)
        P.tt(t["s2"][:], t["sig"][:], t["m2"][:], ALU.mult)
        P.ts(t["s2"][:], t["s2"][:], 1.0 / 127.0, None, ALU.mult)
        P.recip(t["r2"][:], t["s2"][:])
        _scalar_bcast(P, t, "sync", t["s2"][:], t["s2b"][:], scr)
        _scalar_bcast(P, t, "sync", t["r2"][:], t["r2b"][:], scr)
        P.dma("pool", t["brep"][:], d["lb"].partition_broadcast(128))
        with ExitStack() as c2:
            wdnat = _sb(nc, c2, "wdnat", [128, IC, KC, 128], BF16)
            wrow = [_sb(nc, c2, f"wrow{i}", [128, I], F32) for i in range(2)]
            qt = _sb(nc, c2, "qt", [128, I], F32)
            qt2 = _sb(nc, c2, "qt2", [128, I], F32)
            for hc in range(KC):
                wr = wrow[hc % 2]
                P.dma("sync", wr[:], d["wd"].rearrange("(hc p) m -> p hc m", p=128)[:, hc, :])
                P.red(t["rmax"][:], wr[:], ALU.max, absval=True)
                P.recip(t["rq"][:], t["rmax"][:])
                P.ts(t["rq"][:], t["rq"][:], 127.0, None, ALU.mult)
                P.ts(t["sw"][:], t["rmax"][:], 1.0 / 127.0, None, ALU.mult)
                P.ts(qt[:], wr[:], t["rq"][:, 0:1], C_RND, ALU.mult, ALU.add)
                P.stt(wdnat[:, :, hc, :], qt[:].rearrange("p (i n) -> p i n", i=IC),
                      C_RND, t["sw"][:, 0:1].unsqueeze(1).broadcast_to([128, IC, 128]),
                      ALU.subtract, ALU.mult)
            for ic in range(IC):
                P.dmat("act", t["wdT"][:, ic, :].rearrange("p (h n) -> p h n", h=KC),
                       wdnat[:, ic, :, :])
            P.barrier()
        t["hq"] = [_sb(nc, ctx, f"hq{i}", [128, HH, N], BF16) for i in range(2)]
        t["h2r"] = [_sb(nc, ctx, f"h2r{i}", [128, IC - HH, 128], BF16) for i in range(2)]
        t["h2stg"] = [_sb(nc, ctx, f"h2stg{i}", [128, 512], BF16) for i in range(2)]
        t["q2b"] = _sb(nc, ctx, "q2b", [128, 512], BF16)

        def emit_hquant(b, ic):
            half, j = divmod(ic, HH)
            for g in range(2):
                hb = t["hb"][g]
                P.dma("sync", hb[:], d["h_in"][b, :, ic, 1024 * g:1024 * (g + 1)])
                P.ts(hb[:], hb[:], t["r2b"][:, 0:1], C_RND, ALU.mult, ALU.add)
                for gg in range(2):
                    P.ts(t["q2b"][:], hb[:, 512 * gg:512 * (gg + 1)],
                         C_RND, 127.0, ALU.subtract, ALU.min)
                    if half == 0:
                        P.ts(t["hq"][b % 2][:, j, 1024 * g + 512 * gg:
                                            1024 * g + 512 * (gg + 1)],
                             t["q2b"][:], -128.0, None, ALU.max)
                    else:
                        stg = t["h2stg"][gg]
                        P.ts(stg[:], t["q2b"][:], -128.0, None, ALU.max)
                        out = hq2s[b % 2][:, j, 1024 * g + 512 * gg:
                                          1024 * g + 512 * (gg + 1)]
                        P.op("pool", lambda e, o=out, s=stg[:]:
                             e.dma_start(out=o, in_=s),
                             reads=[stg[:]], writes=[f"hq2s{b % 2}"], inc=16)

        for ic in range(IC):
            emit_hquant(0, ic)
        for b in range(BPC):
            sched = [list(range((22 * tau + 15) // 16, (22 * (tau + 1) + 15) // 16))
                     for tau in range(NT)]
            for tau in range(NT):
                h2r = t["h2r"][tau % 2]
                P.op("sync", lambda e, o=h2r[:],
                     s=hq2s[b % 2][:, :, 128 * tau:128 * (tau + 1)]:
                     e.dma_start(out=o, in_=s),
                     reads=[f"hq2s{b % 2}"], writes=[h2r[:]], inc=16)
                psD = (P0, P1)[tau % 2]
                for ic in range(IC):
                    half, j = divmod(ic, HH)
                    if half == 0:
                        lhsT = t["hq"][b % 2][:, j, 128 * tau:128 * (tau + 1)]
                    else:
                        lhsT = h2r[:, j, :]
                    for nh in range(2):
                        P.mm(psD[:, 512 * nh:512 * (nh + 1)], lhsT,
                             t["wdT"][:, ic, 512 * nh:512 * (nh + 1)],
                             start=(ic == 0), stop=(ic == IC - 1))
                Sb = t["Sb"][tau % 2]
                xb = t["xb"][tau % 2]
                P.dma("sync", Sb[:], d["S_in"][b, :, tau, :])
                P.dma("sync", xb[:], d["x"][b, 128 * tau:128 * (tau + 1), :])
                ub = t["ub"][tau % 2]
                P.stt(ub[:], psD[:], t["s2b"][:, 0:1], Sb[:], ALU.mult, ALU.add)
                P.tt(ub[:], ub[:], xb[:], ALU.add)
                P.tt(ub[:], ub[:], t["brep"][:], ALU.add)
                P.dma("pool", d["y"][b, 128 * tau:128 * (tau + 1), :], ub[:])
                if b + 1 < BPC:
                    for ic in sched[tau]:
                        if ic < IC:
                            emit_hquant(b + 1, ic)
        P.wait_all("pool")
        blk = ctx.enter_context(nc.Block())
        P.finalize(blk)
    return nc


def _host_tables():
    j = np.arange(512, dtype=np.float64)
    theta = 10000.0 ** (-j / 512.0)
    n = np.arange(N, dtype=np.float64)
    ang = n[:, None] * theta[None, :]
    cosT = np.cos(ang).astype(np.float32)
    sinT = np.sin(ang).astype(np.float32)
    return cosT, sinT, np.ascontiguousarray(cosT.T), np.ascontiguousarray(sinT.T)


def _host_masks():
    m = np.zeros((8, 128, 8), dtype=np.float32)
    for c in range(8):
        for p in range(128):
            r = 128 * c + p
            orig = 2 * r if r < 512 else 2 * (r - 512) + 1
            m[c, p, orig // 128] = 1.0
    return m


_PERM = np.concatenate([np.arange(0, 1024, 2), np.arange(1, 1024, 2)])


def kernel(L0_attn_x, L0_attn_qk_w, L0_attn_qk_b, L0_attn_lepe_w, L0_attn_lepe_b,
           L0_ffn_clip_x, L0_ffn_clip_down, L0_ffn_w_gate, L0_ffn_w_up, L0_ffn_w_down):
    x = np.ascontiguousarray(np.asarray(L0_attn_x, dtype=np.float32))
    qk_w = np.asarray(L0_attn_qk_w, dtype=np.float32)
    qk_b = np.asarray(L0_attn_qk_b, dtype=np.float32)
    wq = np.ascontiguousarray(qk_w[:, :C][:, _PERM])
    wk = np.ascontiguousarray(qk_w[:, C:][:, _PERM])
    bq_p = qk_b[:C][_PERM]
    bk_p = qk_b[C:][_PERM]
    bq_cols = np.ascontiguousarray(bq_p.reshape(KC, 128).T)
    bk_row = np.ascontiguousarray(bk_p.reshape(1, C))
    lw = np.ascontiguousarray(np.asarray(L0_attn_lepe_w, dtype=np.float32))
    lb = np.ascontiguousarray(np.asarray(L0_attn_lepe_b, dtype=np.float32).reshape(1, C))
    wg = np.ascontiguousarray(np.asarray(L0_ffn_w_gate, dtype=np.float32))
    wu = np.ascontiguousarray(np.asarray(L0_ffn_w_up, dtype=np.float32))
    wd = np.ascontiguousarray(np.asarray(L0_ffn_w_down, dtype=np.float32))
    clip_x = np.asarray(L0_ffn_clip_x, dtype=np.float32).reshape(1, 1)
    clip_d = np.asarray(L0_ffn_clip_down, dtype=np.float32).reshape(1, 1)
    cosT, sinT, cosF, sinF = _host_tables()
    eye8 = np.eye(8, dtype=np.float32)
    ident = np.eye(128, dtype=np.float32)
    mask = _host_masks()

    xs = [np.ascontiguousarray(x[BPC * c:BPC * (c + 1)]) for c in range(NCORES)]

    nc1 = build_l1()
    in1 = [dict(x=xs[c], wq=wq, wk=wk, bq=bq_cols, bk=bk_row, lw=lw,
                cosF=cosF, sinF=sinF, cosT=cosT, sinT=sinT,
                eye8=eye8, ident=ident, mask=mask) for c in range(NCORES)]
    r1 = run_bass_kernel_spmd(nc1, in1, core_ids=list(range(NCORES)))
    m1vec = np.concatenate([r1.results[c]["m1_out"].reshape(1, 1)
                            for c in range(NCORES)], axis=1)

    nc2 = build_l2()
    in2 = [dict(xn_in=r1.results[c]["xn_out"], m1vec=m1vec, clip_x=clip_x,
                wg=wg, wu=wu, ident=ident) for c in range(NCORES)]
    r2 = run_bass_kernel_spmd(nc2, in2, core_ids=list(range(NCORES)))
    m2vec = np.concatenate([r2.results[c]["m2_out"].reshape(1, 1)
                            for c in range(NCORES)], axis=1)

    nc3 = build_l3()
    in3 = [dict(h_in=r2.results[c]["h_out"], m2vec=m2vec, clip_d=clip_d,
                wd=wd, S_in=r1.results[c]["S_out"], x=xs[c], lb=lb)
           for c in range(NCORES)]
    r3 = run_bass_kernel_spmd(nc3, in3, core_ids=list(range(NCORES)))
    y = np.concatenate([np.asarray(r3.results[c]["y"]) for c in range(NCORES)], axis=0)
    return y.astype(np.float32)


# version 14
# speedup vs baseline: 1.1497x; 1.0253x over previous
import numpy as np
import ml_dtypes
from contextlib import ExitStack

import concourse.bass as bass
import concourse.mybir as mybir
from concourse.bass_utils import run_bass_kernel_spmd

F32 = mybir.dt.float32
I32 = mybir.dt.int32
BF16 = mybir.dt.bfloat16
AF = mybir.ActivationFunctionType
ALU = mybir.AluOpType
AX = mybir.AxisListType
BF = ml_dtypes.bfloat16

C_RND = 12582912.0
RSQRT_N = 0.022097086912079608
NCORES = 8
B, N, C = 32, 2048, 1024
BPC = B // NCORES
NT = N // 128
KC = C // 128
I = 2816
IC = I // 128
DEBUG_L1 = False
DEBUG_BARRIER = False
DIAG_TAB_BOUNCE = False
GUARD = 16
NF = GUARD + N + GUARD


class Prog:
    STRICT = False
    STRICT_ENGS = ()
    ENG = ("sync", "act", "pe", "dve", "pool")
    _BLK = {"sync": "sync", "act": "scalar", "pe": "tensor", "dve": "vector",
            "pool": "gpsimd"}
    NSLOT = {"sync": 8, "act": 8, "pe": 0, "dve": 0, "pool": 8}
    KEYS = tuple([(e, "c") for e in ("sync", "act", "pe", "dve", "pool")]
                 + [(e, "d", s) for e in ("sync", "act", "pool")
                    for s in range(8)])

    def __init__(self, nc, ctx):
        self.nc = nc
        self.sems = {k: ctx.enter_context(
            nc.semaphore("sem_" + "_".join(str(p) for p in k)))
            for k in self.KEYS}
        self.cnt = {k: 0 for k in self.KEYS}
        self.known = {e: {k: 0 for k in self.KEYS} for e in self.ENG}
        self.dma_seq = {e: 0 for e in self.ENG}
        self.last_w = {}
        self.readers = {}
        self.ops = {e: [] for e in self.ENG}

    @staticmethod
    def _key(a):
        if a is None:
            return None
        if isinstance(a, str):
            return a
        t = getattr(a, "tensor", a)
        return id(t)

    def op(self, eng, emit, reads=(), writes=(), inc=1):
        if inc >= 16:
            key = (eng, "d", self.dma_seq[eng] % self.NSLOT[eng])
            self.dma_seq[eng] += 1
        else:
            key = (eng, "c")
        waits = {}
        if Prog.STRICT or eng in Prog.STRICT_ENGS:
            for k2 in self.KEYS:
                if self.cnt[k2] > self.known[eng].get(k2, 0):
                    waits[k2] = self.cnt[k2]

        def need(k2, c2):
            if k2 == key == ("pe", "c"):
                return
            if self.known[eng].get(k2, 0) < c2 and waits.get(k2, 0) < c2:
                waits[k2] = c2

        if inc >= 16 and self.cnt[key]:
            need(key, self.cnt[key])
        rk = [k for k in (self._key(a) for a in reads) if k is not None]
        wk = [k for k in (self._key(a) for a in writes) if k is not None]
        for k in rk:
            lw = self.last_w.get(k)
            if lw:
                need(*lw)
        for k in wk:
            lw = self.last_w.get(k)
            if lw:
                need(*lw)
            for k2, c2 in self.readers.get(k, {}).items():
                need(k2, c2)
        after = self.cnt[key] + inc
        self.cnt[key] = after
        for k2, c2 in waits.items():
            self.known[eng][k2] = max(self.known[eng][k2], c2)
        for k in wk:
            self.last_w[k] = (key, after)
            self.readers[k] = {}
        for k in rk:
            d = self.readers.setdefault(k, {})
            d[key] = after
        self.ops[eng].append((waits, emit, inc, key))

    def wait_all(self, eng):
        waits = {}
        for k2 in self.KEYS:
            if self.cnt[k2] > self.known[eng][k2]:
                waits[k2] = self.cnt[k2]
                self.known[eng][k2] = self.cnt[k2]
        if waits:
            self.ops[eng].append((waits, None, 0, None))

    def barrier(self):
        for e in self.ENG:
            self.wait_all(e)

    def dma(self, q, out, in_, extra_reads=(), **kw):
        self.op(q, lambda e: e.dma_start(out=out, in_=in_, **kw),
                reads=[in_, *extra_reads], writes=[out], inc=16)

    def dmat(self, q, out, in_):
        self.op(q, lambda e: e.dma_start(out=out, in_=in_, transpose=True),
                reads=[in_], writes=[out], inc=16)

    def mm(self, out, lhsT, rhs, start, stop):
        self.op("pe", lambda e: e.matmul(out, lhsT, rhs, start=start, stop=stop,
                                         skip_group_check=True),
                reads=[lhsT, rhs], writes=[out], inc=1)

    def act(self, out, in_, func, bias=0.0, scale=1.0, eng="act"):
        er = [a for a in (bias, scale) if hasattr(a, "tensor")]
        self.op(eng, lambda e: e.activation(out, in_, func, bias=bias, scale=scale),
                reads=[in_, *er], writes=[out])

    def ts(self, out, in0, s1, s2, op0, op1=ALU.bypass, eng="dve"):
        er = [a for a in (s1, s2) if hasattr(a, "tensor")]
        self.op(eng, lambda e: e.tensor_scalar(out, in0, s1, s2, op0, op1),
                reads=[in0, *er], writes=[out])

    def tt(self, out, a, b, alu, eng="dve"):
        self.op(eng, lambda e: e.tensor_tensor(out, a, b, alu),
                reads=[a, b], writes=[out])

    def stt(self, out, in0, s, in1, op0, op1, eng="dve"):
        er = [s] if hasattr(s, "tensor") else []
        self.op(eng, lambda e: e.scalar_tensor_tensor(out, in0, s, in1, op0, op1),
                reads=[in0, in1, *er], writes=[out])

    def red(self, out, in_, alu, absval=False, eng="dve"):
        self.op(eng, lambda e: e.tensor_reduce(out, in_, AX.X, alu,
                                               apply_absolute_value=absval),
                reads=[in_], writes=[out])

    def copy(self, out, in_, eng="dve"):
        self.op(eng, lambda e: e.tensor_copy(out, in_), reads=[in_], writes=[out])

    def recip(self, out, in_):
        self.op("dve", lambda e: e.reciprocal(out, in_), reads=[in_], writes=[out])

    def memset(self, ap, v, eng="dve"):
        self.op(eng, lambda e: e.memset(ap, v), reads=[], writes=[ap])

    def finalize(self, blk):
        for e in self.ENG:
            w = {k: self.cnt[k] for k in self.KEYS if k[0] == e and self.cnt[k]}
            if w:
                self.ops[e].append((w, None, 0, None))
        for e in self.ENG:
            def run(eng, _e=e):
                for waits, emit, inc, key in self.ops[_e]:
                    for k2 in self.KEYS:
                        if k2 in waits:
                            eng.wait_ge(self.sems[k2], waits[k2])
                    if emit is not None:
                        ins = emit(eng)
                        if inc:
                            ins.then_inc(self.sems[key], inc)
            getattr(blk, self._BLK[e])(run)


def _sb(nc, ctx, name, shape, dt):
    return ctx.enter_context(nc.sbuf_tensor(name, shape, dt))


def _ps(nc, ctx, name, shape, dt=F32):
    return ctx.enter_context(nc.psum_tensor(name, shape, dt))


def _colmax_to_scalar(P, t, col_f32, out11, psrow):
    P.op("pe", lambda e: e.transpose(psrow[0:1, 0:128], col_f32, t["ident_f"][:]),
         reads=[col_f32, t["ident_f"][:]], writes=[psrow])
    P.copy(t["m1row"][:], psrow[0:1, 0:128])
    P.red(out11, t["m1row"][:], ALU.max)


def _scalar_bcast(P, t, q, src11, dst_col, scratch_dram):
    P.dma("pool", scratch_dram, src11)
    P.dma(q, dst_col, scratch_dram.partition_broadcast(128))


def _emit_quant3(P, out_bf, in_ap, rcol, tmp_f32, tmp_i32):
    P.ts(tmp_f32, in_ap, rcol, None, ALU.mult)
    P.copy(tmp_i32, tmp_f32)
    P.copy(tmp_f32, tmp_i32)
    P.ts(out_bf, tmp_f32, 127.0, -128.0, ALU.min, ALU.max)


def build_l1():
    nc = bass.Bass("TRN2", target_bir_lowering=False)
    d = {}
    d["x"] = nc.dram_tensor("x", [BPC, N, C], F32, kind="ExternalInput").ap()
    d["wq"] = nc.dram_tensor("wq", [C, C], F32, kind="ExternalInput").ap()
    d["wk"] = nc.dram_tensor("wk", [C, C], F32, kind="ExternalInput").ap()
    d["bq"] = nc.dram_tensor("bq", [128, KC], F32, kind="ExternalInput").ap()
    d["bk"] = nc.dram_tensor("bk", [1, C], F32, kind="ExternalInput").ap()
    d["lw"] = nc.dram_tensor("lw", [3, C, C], F32, kind="ExternalInput").ap()
    d["cosF"] = nc.dram_tensor("cosF", [512, N], F32, kind="ExternalInput").ap()
    d["sinF"] = nc.dram_tensor("sinF", [512, N], F32, kind="ExternalInput").ap()
    d["cosT"] = nc.dram_tensor("cosT", [N, 512], F32, kind="ExternalInput").ap()
    d["sinT"] = nc.dram_tensor("sinT", [N, 512], F32, kind="ExternalInput").ap()
    d["eye8"] = nc.dram_tensor("eye8", [8, 8], F32, kind="ExternalInput").ap()
    d["ident"] = nc.dram_tensor("ident", [128, 128], F32, kind="ExternalInput").ap()
    d["mask"] = nc.dram_tensor("mask", [8, 128, 8], F32, kind="ExternalInput").ap()
    d["S_out"] = nc.dram_tensor("S_out", [BPC, 128, NT, C], BF16, kind="ExternalOutput").ap()
    d["xn_out"] = nc.dram_tensor("xn_out", [BPC, 128, NT, C], F32, kind="ExternalOutput").ap()
    d["m1_out"] = nc.dram_tensor("m1_out", [1, 1], F32, kind="ExternalOutput").ap()
    d["cosT_s"] = nc.dram_tensor("cosT_s", [N, 512], BF16, kind="Internal").ap()
    d["sinT_s"] = nc.dram_tensor("sinT_s", [N, 512], BF16, kind="Internal").ap()
    d["cosF_s"] = nc.dram_tensor("cosF_s", [512, N], BF16, kind="Internal").ap()
    d["sinF_s"] = nc.dram_tensor("sinF_s", [512, N], BF16, kind="Internal").ap()
    d["wq_s"] = nc.dram_tensor("wq_s", [128, KC, C], BF16, kind="Internal").ap()
    if DEBUG_L1:
        d["kv_dbg"] = nc.dram_tensor("kv_dbg", [128, C], F32, kind="ExternalOutput").ap()
        d["z_dbg"] = nc.dram_tensor("z_dbg", [128, NT * 8], F32, kind="ExternalOutput").ap()
        d["qr_dbg"] = nc.dram_tensor("qr_dbg", [128, KC, N], F32, kind="ExternalOutput").ap()
        d["q1_dbg"] = nc.dram_tensor("q1_dbg", [128, N], F32, kind="ExternalOutput").ap()
        d["kr_dbg"] = nc.dram_tensor("kr_dbg", [128, C], F32, kind="ExternalOutput").ap()
        d["kr2_dbg"] = nc.dram_tensor("kr2_dbg", [128, C], BF16, kind="ExternalOutput").ap()
        d["t1_dbg"] = nc.dram_tensor("t1_dbg", [128, C], F32, kind="ExternalOutput").ap()
        d["t2_dbg"] = nc.dram_tensor("t2_dbg", [128, C], F32, kind="ExternalOutput").ap()
        d["k1_dbg"] = nc.dram_tensor("k1_dbg", [128, C], F32, kind="ExternalOutput").ap()
        d["tb_dbg"] = nc.dram_tensor("tb_dbg", [128, 2, 512], F32, kind="ExternalOutput").ap()
        d["psk_dbg"] = nc.dram_tensor("psk_dbg", [128, C], F32, kind="ExternalOutput").ap()
        d["psk2_dbg"] = nc.dram_tensor("psk2_dbg", [128, C], F32, kind="ExternalOutput").ap()
        d["km_dbg"] = nc.dram_tensor("km_dbg", [128, KC], F32, kind="ExternalOutput").ap()
        d["xnfm_dbg"] = nc.dram_tensor("xnfm_dbg", [128, KC, NF], BF16, kind="ExternalOutput").ap()
        d["psq_dbg"] = nc.dram_tensor("psq_dbg", [128, C], F32, kind="ExternalOutput").ap()

    with ExitStack() as ctx:
        t = {}
        t["ident_f"] = _sb(nc, ctx, "ident_f", [128, 128], F32)
        t["mask_sb"] = _sb(nc, ctx, "mask_sb", [128, 8, 8], F32)
        t["bq_sb"] = _sb(nc, ctx, "bq_sb", [128, KC], F32)
        t["nbq_sb"] = _sb(nc, ctx, "nbq_sb", [128, KC], F32)
        t["bkb"] = _sb(nc, ctx, "bkb", [1, C], BF16)
        t["ones_row"] = _sb(nc, ctx, "ones_row", [1, 128], BF16)
        t["ones_col"] = _sb(nc, ctx, "ones_col", [128, 1], BF16)
        t["ones11"] = _sb(nc, ctx, "ones11", [1, 1], BF16)
        t["krhs"] = _sb(nc, ctx, "krhs", [128, KC, C], BF16)
        t["lw_sb"] = _sb(nc, ctx, "lw_sb", [128, 24, C], BF16)
        t["wq_buf"] = [_sb(nc, ctx, f"wq{i}", [128, KC, 128], BF16) for i in range(2)]
        t["tabF"] = _sb(nc, ctx, "tabF", [128, 2, 1024], BF16)
        t["tabT"] = [_sb(nc, ctx, f"tabT{i}", [128, 2, 512], BF16) for i in range(2)]
        t["tabT2"] = [_sb(nc, ctx, f"tabT2_{i}", [128, 2, 512], BF16) for i in range(2)] if DIAG_TAB_BOUNCE else None
        t["xn_ring"] = _sb(nc, ctx, "xn_ring", [128, 2, C], BF16)
        t["vh_ring"] = [_sb(nc, ctx, f"vh{i}", [128, C], BF16) for i in range(3)]
        t["qr"] = _sb(nc, ctx, "qr", [128, KC * N], BF16)
        t["xn_fm"] = _sb(nc, ctx, "xn_fm", [128, KC, NF], BF16)
        t["bst"] = [_sb(nc, ctx, f"bst{i}", [128, 12], F32) for i in range(2)]
        t["mv"] = [_sb(nc, ctx, f"mv{i}", [128, 2], F32) for i in range(2)]
        t["vtmp"] = [_sb(nc, ctx, f"vtmp{i}", [128, 1], F32) for i in range(2)]
        t["rstd"] = [_sb(nc, ctx, f"rstd{i}", [128, 1], F32) for i in range(2)]
        t["rstd2"] = [_sb(nc, ctx, f"rstd2{i}", [128, 1], F32) for i in range(2)]
        t["nbias"] = [_sb(nc, ctx, f"nbias{i}", [128, 1], F32) for i in range(2)]
        t["t1"] = [_sb(nc, ctx, f"t1_{i}", [128, C], BF16) for i in range(2)]
        t["t2"] = [_sb(nc, ctx, f"t2_{i}", [128, C], BF16) for i in range(2)]
        t["x_buf"] = [_sb(nc, ctx, f"xbuf{i}", [128, C], F32) for i in range(2)]
        t["k1"] = [_sb(nc, ctx, f"k1_{i}", [128, C], BF16) for i in range(3)]
        t["kr"] = [_sb(nc, ctx, f"kr_{i}", [128, C], BF16) for i in range(3)]
        t["kv_sb"] = _sb(nc, ctx, "kv_sb", [128, C], BF16)
        t["km_bf"] = _sb(nc, ctx, "km_bf", [1, C], BF16)
        t["km_cols"] = _sb(nc, ctx, "km_cols", [128, KC], F32)
        t["A_sb"] = _sb(nc, ctx, "A_sb", [128, KC, 8], BF16)
        t["q1"] = [_sb(nc, ctx, f"q1_{i}", [128, N], BF16) for i in range(2)]
        t["r1"] = [_sb(nc, ctx, f"r1_{i}", [128, C], BF16) for i in range(2)]
        t["r2k"] = [_sb(nc, ctx, f"r2k_{i}", [128, C], BF16) for i in range(2)]
        t["eye8_f"] = _sb(nc, ctx, "eye8_f", [8, 8], F32)
        t["z_tm"] = _sb(nc, ctx, "z_tm", [128, NT, 8], F32)
        t["zrow"] = _sb(nc, ctx, "zrow", [128, 64], F32)
        t["S_buf"] = [_sb(nc, ctx, f"Sb{i}", [128, C], BF16) for i in range(2)]
        t["psq_sav"] = None
        t["t5_dbg"] = None
        t["mx"] = _sb(nc, ctx, "mx", [128, BPC * NT], F32)
        t["mxf"] = _sb(nc, ctx, "mxf", [128, 1], F32)
        t["m1row"] = _sb(nc, ctx, "m1row", [1, 128], F32)
        t["m1v"] = _sb(nc, ctx, "m1v", [1, 1], F32)
        P0 = _ps(nc, ctx, "P0", [128, 1024])
        P1 = _ps(nc, ctx, "P1", [128, 1024])
        P2 = _ps(nc, ctx, "P2", [128, 1024])
        P3 = _ps(nc, ctx, "P3", [128, 1024])

        P = Prog(nc, ctx)
        P.dma("sync", t["ident_f"][:], d["ident"])
        P.dma("sync", t["eye8_f"][:], d["eye8"])
        P.dma("sync", t["mask_sb"][:], d["mask"].rearrange("c p h -> p c h"))
        P.dma("sync", t["bq_sb"][:], d["bq"])
        P.ts(t["nbq_sb"][:], t["bq_sb"][:], -1.0, None, ALU.mult)
        P.dma("pool", t["bkb"][:], d["bk"])
        P.memset(t["ones_row"][:], 1.0)
        P.memset(t["ones_col"][:], 1.0)
        P.memset(t["ones11"][:], 1.0)
        P.op("dve", lambda e: e.memset(t["xn_fm"][:], 0.0),
             reads=[], writes=[f"xnfm{i}" for i in range(NT)])
        P.dma("pool", t["krhs"][:], d["wk"].rearrange("(kc p) n -> p kc n", p=128))
        P.dma("pool", t["lw_sb"][:], d["lw"].rearrange("t (kc p) m -> p (t kc) m", p=128))
        P.dma("pool", d["cosT_s"], d["cosT"])
        P.dma("pool", d["sinT_s"], d["sinT"])
        P.dma("pool", d["cosF_s"], d["cosF"])
        P.dma("pool", d["sinF_s"], d["sinF"])
        P.dma("pool", d["wq_s"], d["wq"].rearrange("(kc p) m -> p kc m", p=128))

        qr = t["qr"][:].rearrange("p (c n) -> p c n", c=KC)

        XK = [f"xnfm{i}" for i in range(NT)]

        def emit_ln(b, tau):
            xb = t["x_buf"][tau % 2]
            P.dma("pool", xb[:], d["x"][b, 128 * tau:128 * (tau + 1), :])
            bst = t["bst"][tau % 2]
            mv = t["mv"][tau % 2]
            vtmp = t["vtmp"][tau % 2]
            rstd = t["rstd"][tau % 2]
            rstd2 = t["rstd2"][tau % 2]
            nbias = t["nbias"][tau % 2]
            for g in range(2):
                P.op("dve", lambda e, o=bst[:, 6 * g:6 * (g + 1)],
                     i=xb[:, 512 * g:512 * (g + 1)]: e.bn_stats(o, i),
                     reads=[xb[:]], writes=[bst[:]])
            P.op("dve", lambda e, o=mv[:], i=bst[:]: e.bn_aggr(o, i),
                 reads=[bst[:]], writes=[mv[:]])
            P.ts(vtmp[:], mv[:, 1:2], 1e-5, None, ALU.add)
            P.act(rstd[:], vtmp[:], AF.Ln)
            P.act(rstd2[:], rstd[:], AF.Exp, scale=-0.5)
            P.stt(nbias[:], mv[:, 0:1], -1.0, rstd2[:, 0:1], ALU.mult, ALU.mult)
            P.act(xb[:], xb[:], AF.Identity, bias=nbias[:, 0:1],
                  scale=rstd2[:, 0:1])
            P.copy(t["xn_ring"][:, tau % 2, :], xb[:])
            P.dma("pool", d["xn_out"][b, :, tau, :], xb[:])
            P.red(t["mx"][:, b * NT + tau: b * NT + tau + 1], xb[:],
                  ALU.max, absval=True)
            out = t["xn_fm"][:, :, GUARD + 128 * tau: GUARD + 128 * (tau + 1)]
            srcv = t["xn_ring"][:, tau % 2, :].rearrange("p (c n) -> p c n", c=KC)
            P.op("sync", lambda e, o=out, s=srcv: e.dma_start(out=o, in_=s, transpose=True),
                 reads=[srcv], writes=[XK[tau]], inc=16)

        for b in range(BPC):
            def emit_kv(tau):
                k1 = t["k1"][tau % 3]
                kr = t["kr"][tau % 3]
                for nh in range(2):
                    P.mm(P3[0:1, 512 * nh:512 * (nh + 1)], t["ones_col"][:],
                         k1[:, 512 * nh:512 * (nh + 1)],
                         start=(tau == 0), stop=(tau == NT - 1))
                for h in range(8):
                    P.mm(P2[:, 128 * h:128 * (h + 1)],
                         kr[:, 128 * h:128 * (h + 1)],
                         t["vh_ring"][tau % 3][:, 128 * h:128 * (h + 1)],
                         start=(tau == 0 and h % 4 == 0), stop=(tau == NT - 1))

            if b == 0:
                for tau in range(NT):
                    emit_ln(0, tau)
            for tau in range(NT):
                for k in range(KC):
                    vout = t["vh_ring"][tau % 3][:, 128 * k:128 * (k + 1)]
                    vin = t["xn_fm"][:, k, GUARD + 128 * tau: GUARD + 128 * (tau + 1)]
                    P.op("sync", lambda e, o=vout, s=vin:
                         e.dma_start(out=o, in_=s, transpose=True),
                         reads=[XK[tau]], writes=[t["vh_ring"][tau % 3][:]], inc=16)
                tb = t["tabT"][tau % 2]
                P.dma("act", tb[:, 0, :], d["cosT_s"][128 * tau:128 * (tau + 1), :])
                P.dma("act", tb[:, 1, :], d["sinT_s"][128 * tau:128 * (tau + 1), :])
                psK = (P0, P1)[tau % 2]
                for k in range(KC):
                    lhsT = t["xn_fm"][:, k, GUARD + 128 * tau: GUARD + 128 * (tau + 1)]
                    for nh in range(2):
                        P.op("pe", lambda e, o=psK[:, 512 * nh:512 * (nh + 1)], l=lhsT,
                             r=t["krhs"][:, k, 512 * nh:512 * (nh + 1)],
                             st=(k == 0): e.matmul(o, l, r, start=st, stop=False,
                                                   skip_group_check=True),
                             reads=[XK[tau], t["krhs"][:]], writes=[psK[:]], inc=1)
                for nh in range(2):
                    P.mm(psK[:, 512 * nh:512 * (nh + 1)], t["ones_row"][:],
                         t["bkb"][0:1, 512 * nh:512 * (nh + 1)], start=False, stop=True)
                if tau > 1:
                    emit_kv(tau - 2)
                k1 = t["k1"][tau % 3]
                t1 = t["t1"][tau % 2]
                t2 = t["t2"][tau % 2]
                P.act(t2[:], psK[:], AF.Relu, scale=-1.0)
                P.act(t1[:], t2[:], AF.Exp, scale=-1.0)
                P.act(t2[:], psK[:], AF.Relu)
                P.tt(k1[:], t1[:], t2[:], ALU.add)
                kr = t["kr"][tau % 3]
                cosT_, sinT_ = tb[:, 0, :], tb[:, 1, :]
                r1 = t["r1"][tau % 2]
                r2k = t["r2k"][tau % 2]
                P.tt(r1[:, 0:512], cosT_, k1[:, 0:512], ALU.mult)
                P.tt(r1[:, 512:1024], sinT_, k1[:, 512:1024], ALU.mult)
                P.tt(kr[:, 0:512], r1[:, 0:512], r1[:, 512:1024],
                     ALU.subtract)
                P.tt(r2k[:, 0:512], cosT_, k1[:, 512:1024], ALU.mult)
                P.tt(r2k[:, 512:1024], sinT_, k1[:, 0:512], ALU.mult)
                P.tt(kr[:, 512:1024], r2k[:, 0:512], r2k[:, 512:1024],
                     ALU.add)
                if DEBUG_L1 and b == 0 and tau == 0:
                    P.dma("pool", d["kr_dbg"], kr[:])
                    P.dma("act", d["kr2_dbg"], kr[:])
                    P.dma("pool", d["t1_dbg"], t["t1"][tau % 2][:])
                    P.dma("pool", d["t2_dbg"], t["t2"][tau % 2][:])
                    P.dma("pool", d["k1_dbg"], k1[:])
                    P.dma("pool", d["tb_dbg"], tb[:])
            emit_kv(NT - 2)
            emit_kv(NT - 1)
            P.ts(t["kv_sb"][:], P2[:], RSQRT_N, None, ALU.mult)
            P.ts(t["km_bf"][:], P3[0:1, :], 1.0 / N, None, ALU.mult)
            for c in range(KC):
                P.mm(P2[:, c:c + 1], t["km_bf"][0:1, 128 * c:128 * (c + 1)],
                     t["ones11"][:], start=(c == 0), stop=(c == KC - 1))
            P.copy(t["km_cols"][:], P2[:, 0:KC])
            for c in range(KC):
                P.ts(t["A_sb"][:, c, :], t["mask_sb"][:, c, :],
                     t["km_cols"][:, c:c + 1], None, ALU.mult)

            def emit_zmm(m, zi):
                q1 = t["q1"][zi % 2]
                for th in range(2):
                    psZ = (P2, P3)[th]
                    for ns in range(2):
                        lo = 1024 * th + 512 * ns
                        P.mm(psZ[0:8, 512 * ns:512 * (ns + 1)], t["A_sb"][:, m, :],
                             q1[:, lo:lo + 512], start=(zi == 0), stop=(zi == 7))

            pair_done = {}
            order = [0, 4, 1, 5, 2, 6, 3, 7]
            for mi, m in enumerate(order):
                wqb = t["wq_buf"][mi % 2]
                P.dma("sync", wqb[:], d["wq_s"][:, :, 128 * m:128 * (m + 1)])
                q1 = t["q1"][mi % 2]
                for th in range(2):
                    psQ = (P0, P1)[th]
                    for k in range(KC):
                        for ns in range(2):
                            lo = 1024 * th + 512 * ns
                            seg = 2 * th + ns
                            P.op("pe", lambda e, o=psQ[:, 512 * ns:512 * (ns + 1)],
                                 l=wqb[:, k, :],
                                 r=t["xn_fm"][:, k, GUARD + lo: GUARD + lo + 512],
                                 st=(k == 0), sp=(k == KC - 1):
                                 e.matmul(o, l, r, start=st, stop=sp,
                                          skip_group_check=True),
                                 reads=[wqb[:], *(XK[4 * seg + j] for j in range(4))],
                                 writes=[psQ[:]], inc=1)
                    if th == 1 and mi > 0:
                        emit_zmm(order[mi - 1], mi - 1)
                    sl_q = slice(1024 * th, 1024 * (th + 1))
                    t1 = t["t1"][th]
                    t2 = t["t2"][th]
                    P.act(t2[:], psQ[:], AF.Relu, scale=-1.0,
                          bias=t["nbq_sb"][:, m:m + 1])
                    P.act(t1[:], t2[:], AF.Exp, scale=-1.0)
                    P.act(t2[:], psQ[:], AF.Relu, bias=t["bq_sb"][:, m:m + 1])
                    P.tt(q1[:, sl_q], t2[:], t1[:], ALU.add)
                if DEBUG_L1 and b == 0 and m == 0:
                    P.dma("pool", d["q1_dbg"], q1[:])
                pair_done[m] = q1
                cpair = m if m < 4 else m - 4
                if cpair in pair_done and cpair + 4 in pair_done:
                    qE, qO = pair_done[cpair], pair_done[cpair + 4]
                    for th in range(2):
                        sl = slice(1024 * th, 1024 * (th + 1))
                        tf = t["tabF"]
                        r1 = t["r1"][th]
                        r2k = t["r2k"][th]
                        P.dma("act", tf[:, 0, :],
                              d["cosF_s"][128 * cpair:128 * (cpair + 1), sl])
                        P.dma("act", tf[:, 1, :],
                              d["sinF_s"][128 * cpair:128 * (cpair + 1), sl])
                        cosF_, sinF_ = tf[:, 0, :], tf[:, 1, :]
                        P.tt(r1[:], cosF_, qE[:, sl], ALU.mult)
                        P.tt(r2k[:], sinF_, qO[:, sl], ALU.mult)
                        P.tt(qr[:, cpair, sl], r1[:], r2k[:], ALU.subtract)
                        P.tt(r1[:, 0:512], cosF_[:, 0:512], qO[:, sl][:, 0:512], ALU.mult)
                        P.tt(r1[:, 512:1024], sinF_[:, 0:512], qE[:, sl][:, 0:512], ALU.mult)
                        P.tt(qr[:, cpair + 4, sl][:, 0:512], r1[:, 0:512],
                             r1[:, 512:1024], ALU.add)
                        P.tt(r2k[:, 0:512], cosF_[:, 512:1024], qO[:, sl][:, 512:1024], ALU.mult)
                        P.tt(r2k[:, 512:1024], sinF_[:, 512:1024], qE[:, sl][:, 512:1024], ALU.mult)
                        P.tt(qr[:, cpair + 4, sl][:, 512:1024], r2k[:, 0:512],
                             r2k[:, 512:1024], ALU.add)
                    del pair_done[cpair], pair_done[cpair + 4]
            emit_zmm(order[7], 7)
            for th in range(2):
                psZ = (P2, P3)[th]
                P.ts(t["x_buf"][0][0:8, 0:1024], psZ[0:8, :], 1e-6, None, ALU.add)
                for tt8 in range(8):
                    P.mm(P0[:, 8 * tt8:8 * (tt8 + 1)],
                         t["x_buf"][0][0:8, 128 * tt8:128 * (tt8 + 1)], t["eye8_f"][:],
                         start=(tt8 == 0), stop=True)
                P.copy(t["zrow"][:], P0[:, 0:64])
                P.recip(t["z_tm"][:, 8 * th:8 * (th + 1), :],
                        t["zrow"][:].rearrange("p (a e) -> p a e", a=8))

            for tau in range(NT):
                psA = (P0, P1)[tau % 2]
                psL = (P2, P3)[tau % 2]
                for h in range(8):
                    P.mm(psA[:, 128 * h:128 * (h + 1)],
                         qr[:, h, 128 * tau:128 * (tau + 1)],
                         t["kv_sb"][:, 128 * h:128 * (h + 1)],
                         start=(h % 4 == 0), stop=True)
                nmm = 0
                for tap in range(3):
                    for k in range(KC):
                        lhsT = t["xn_fm"][:, k,
                                          GUARD + 128 * tau + tap - 1:
                                          GUARD + 128 * tau + tap - 1 + 128]
                        rkeys = [XK[j] for j in (tau - 1, tau, tau + 1)
                                 if 0 <= j < NT]
                        for nh in range(2):
                            P.op("pe", lambda e, o=psL[:, 512 * nh:512 * (nh + 1)],
                                 l=lhsT,
                                 r=t["lw_sb"][:, 8 * tap + k, 512 * nh:512 * (nh + 1)],
                                 st=(nmm == 0), sp=(nmm == 23):
                                 e.matmul(o, l, r, start=st, stop=sp,
                                          skip_group_check=True),
                                 reads=[t["lw_sb"][:], *rkeys], writes=[psL[:]], inc=1)
                        nmm += 1
                zb = t["z_tm"][:, tau, :].unsqueeze(2).broadcast_to([128, 8, 128])
                Sb = t["S_buf"][tau % 2]
                P.tt(Sb[:].rearrange("p (h n) -> p h n", h=8),
                     psA[:].rearrange("p (h n) -> p h n", h=8), zb, ALU.mult)
                P.tt(Sb[:], Sb[:], psL[:], ALU.add)
                P.dma("act", d["S_out"][b, :, tau, :], Sb[:])
                if tau >= 1 and b + 1 < BPC:
                    emit_ln(b + 1, tau - 1)
            if b + 1 < BPC:
                emit_ln(b + 1, NT - 1)
            if DEBUG_L1 and b == 0:
                P.op("pool", lambda e: e.dma_start(out=d["xnfm_dbg"], in_=t["xn_fm"][:]),
                     reads=list(XK), writes=[d["xnfm_dbg"]], inc=16)
                P.dma("pool", d["kv_dbg"], t["kv_sb"][:])
                P.dma("pool", d["z_dbg"], t["z_tm"][:].rearrange("p a e -> p (a e)"))
                P.dma("pool", d["qr_dbg"], qr)
                P.dma("pool", d["km_dbg"], t["km_cols"][:])

        P.red(t["mxf"][:], t["mx"][:], ALU.max)
        _colmax_to_scalar(P, t, t["mxf"][:], t["m1v"][:], P0)
        P.dma("pool", d["m1_out"], t["m1v"][:])
        P.wait_all("pool")

        blk = ctx.enter_context(nc.Block())
        P.finalize(blk)
    return nc


def build_l2():
    nc = bass.Bass("TRN2", target_bir_lowering=False)
    d = {}
    d["xn_in"] = nc.dram_tensor("xn_in", [BPC, 128, NT, C], F32, kind="ExternalInput").ap()
    d["m1vec"] = nc.dram_tensor("m1vec", [1, NCORES], F32, kind="ExternalInput").ap()
    d["clip_x"] = nc.dram_tensor("clip_x", [1, 1], F32, kind="ExternalInput").ap()
    d["wg"] = nc.dram_tensor("wg", [I, C], F32, kind="ExternalInput").ap()
    d["wu"] = nc.dram_tensor("wu", [I, C], F32, kind="ExternalInput").ap()
    d["ident"] = nc.dram_tensor("ident", [128, 128], F32, kind="ExternalInput").ap()
    d["h_out"] = nc.dram_tensor("h_out", [BPC, 128, IC, N], F32, kind="ExternalOutput").ap()
    d["m2_out"] = nc.dram_tensor("m2_out", [1, 1], F32, kind="ExternalOutput").ap()
    scr = nc.dram_tensor("scr11", [1, 1], F32, kind="Internal").ap()

    with ExitStack() as ctx:
        t = {}
        t["ident_f"] = _sb(nc, ctx, "ident_f", [128, 128], F32)
        t["wgT"] = _sb(nc, ctx, "wgT", [128, KC, I], BF16)
        t["wuT"] = _sb(nc, ctx, "wuT", [128, KC, I], BF16)
        t["rmax"] = _sb(nc, ctx, "rmax", [128, 1], F32)
        t["rq"] = _sb(nc, ctx, "rq", [128, 1], F32)
        t["swg"] = _sb(nc, ctx, "swg", [128, IC], F32)
        t["swu"] = _sb(nc, ctx, "swu", [128, IC], F32)
        t["sgc"] = _sb(nc, ctx, "sgc", [128, IC], F32)
        t["suc"] = _sb(nc, ctx, "suc", [128, IC], F32)
        t["m1s"] = _sb(nc, ctx, "m1s", [1, NCORES], F32)
        t["m1"] = _sb(nc, ctx, "m1", [1, 1], F32)
        t["sig"] = _sb(nc, ctx, "sig", [1, 1], F32)
        t["s1"] = _sb(nc, ctx, "s1", [1, 1], F32)
        t["r1"] = _sb(nc, ctx, "r1", [1, 1], F32)
        t["r1b"] = _sb(nc, ctx, "r1b", [128, 1], F32)
        t["s1b"] = _sb(nc, ctx, "s1b", [128, 1], F32)
        t["hm"] = _sb(nc, ctx, "hm", [128, 1], F32)
        t["hmx"] = _sb(nc, ctx, "hmx", [128, 1], F32)
        t["m1row"] = _sb(nc, ctx, "m1row", [1, 128], F32)
        t["m2v"] = _sb(nc, ctx, "m2v", [1, 1], F32)
        P0 = _ps(nc, ctx, "P0", [128, 512])
        P1 = _ps(nc, ctx, "P1", [128, 512])
        P2 = _ps(nc, ctx, "P2", [128, 512])
        P3 = _ps(nc, ctx, "P3", [128, 512])

        P = Prog(nc, ctx)
        P.dma("sync", t["ident_f"][:], d["ident"])
        P.dma("sync", t["m1s"][:], d["m1vec"])
        P.red(t["m1"][:], t["m1s"][:], ALU.max)
        P.dma("sync", t["sig"][:], d["clip_x"])
        P.act(t["sig"][:], t["sig"][:], AF.Sigmoid)
        P.tt(t["s1"][:], t["sig"][:], t["m1"][:], ALU.mult)
        P.ts(t["s1"][:], t["s1"][:], 1.0 / 127.0, None, ALU.mult)
        P.recip(t["r1"][:], t["s1"][:])
        _scalar_bcast(P, t, "sync", t["s1"][:], t["s1b"][:], scr)
        _scalar_bcast(P, t, "sync", t["r1"][:], t["r1b"][:], scr)
        with ExitStack() as c2:
            wnat = _sb(nc, c2, "wnat", [128, KC, IC, 128], BF16)
            wrow = [_sb(nc, c2, f"wrow{i}", [128, C], F32) for i in range(2)]
            qt = _sb(nc, c2, "qt", [128, C], F32)
            qi = _sb(nc, c2, "qi", [128, C], I32)
            for wsrc, wT, sw in ((d["wg"], t["wgT"], t["swg"]), (d["wu"], t["wuT"], t["swu"])):
                for cc in range(IC):
                    wr = wrow[cc % 2]
                    P.dma("sync", wr[:], wsrc.rearrange("(cc p) m -> p cc m", p=128)[:, cc, :])
                    P.red(t["rmax"][:], wr[:], ALU.max, absval=True)
                    P.recip(t["rq"][:], t["rmax"][:])
                    P.ts(t["rq"][:], t["rq"][:], 127.0, None, ALU.mult)
                    P.ts(sw[:, cc:cc + 1], t["rmax"][:], 1.0 / 127.0, None, ALU.mult)
                    P.ts(qt[:], wr[:], t["rq"][:, 0:1], None, ALU.mult)
                    P.copy(qi[:], qt[:])
                    P.copy(wnat[:, :, cc, :], qi[:].rearrange("p (h n) -> p h n", h=KC))
                for hc in range(KC):
                    P.dmat("act", wT[:, hc, :].rearrange("p (i n) -> p i n", i=IC),
                           wnat[:, hc, :, :])
            P.barrier()
        t["xq"] = [_sb(nc, ctx, f"xq{i}", [128, KC, N], BF16) for i in range(2)]
        t["xnb"] = [_sb(nc, ctx, f"xnb{i}", [128, C], F32) for i in range(2)]
        t["xqt"] = [_sb(nc, ctx, f"xqt{i}", [128, C], BF16) for i in range(2)]
        t["qtmp"] = _sb(nc, ctx, "qtmp", [128, C], F32)
        t["qtmp2"] = _sb(nc, ctx, "qtmp2", [128, C], I32)
        t["sg_buf"] = _sb(nc, ctx, "sg_buf", [128, 512], F32)
        t["up_buf"] = _sb(nc, ctx, "up_buf", [128, 512], F32)
        t["h_buf"] = [_sb(nc, ctx, f"hb{i}", [128, 512], F32) for i in range(2)]
        P.tt(t["sgc"][:], t["swg"][:], t["s1b"][:, 0:1].broadcast_to([128, IC]), ALU.mult)
        P.tt(t["suc"][:], t["swu"][:], t["s1b"][:, 0:1].broadcast_to([128, IC]), ALU.mult)
        P.memset(t["hmx"][:], 0.0)

        def emit_quant_tau(g, j):
            gb, gs = divmod(g, 4)
            tau = 4 * gs + j
            xnb = t["xnb"][tau % 2]
            xqt = t["xqt"][tau % 2]
            P.dma("sync", xnb[:], d["xn_in"][gb, :, tau, :])
            _emit_quant3(P, xqt[:], xnb[:], t["r1b"][:, 0:1], t["qtmp"][:], t["qtmp2"][:])
            out = t["xq"][gb % 2][:, :, 128 * tau:128 * (tau + 1)]
            src = xqt[:].rearrange("p (c n) -> p c n", c=KC)
            P.op("act", lambda e, o=out, s=src: e.dma_start(out=o, in_=s, transpose=True),
                 reads=[src], writes=[f"xq{gb % 2}s{gs}"], inc=16)

        for j in range(4):
            emit_quant_tau(0, j)
        SLOTS = (4, 9, 14, 19)
        for b in range(BPC):
            for s in range(4):
                g = 4 * b + s
                lo = 512 * s
                xqb = t["xq"][b % 2]
                xqkey = f"xq{b % 2}s{s}"
                for m in range(IC):
                    psG = (P0, P1)[m % 2]
                    psU = (P2, P3)[m % 2]
                    for k in range(KC):
                        P.op("pe", lambda e, o=psG[:], l=t["wgT"][:, k, 128 * m:128 * (m + 1)],
                             r=xqb[:, k, lo:lo + 512], st=(k == 0), sp=(k == KC - 1):
                             e.matmul(o, l, r, start=st, stop=sp, skip_group_check=True),
                             reads=[t["wgT"][:, k, 128 * m:128 * (m + 1)], xqkey],
                             writes=[psG[:]], inc=1)
                        P.op("pe", lambda e, o=psU[:], l=t["wuT"][:, k, 128 * m:128 * (m + 1)],
                             r=xqb[:, k, lo:lo + 512], st=(k == 0), sp=(k == KC - 1):
                             e.matmul(o, l, r, start=st, stop=sp, skip_group_check=True),
                             reads=[t["wuT"][:, k, 128 * m:128 * (m + 1)], xqkey],
                             writes=[psU[:]], inc=1)
                    P.act(t["sg_buf"][:], psG[:], AF.Silu, scale=t["sgc"][:, m:m + 1])
                    P.ts(t["up_buf"][:], psU[:], t["suc"][:, m:m + 1], None, ALU.mult)
                    hb = t["h_buf"][m % 2]
                    P.tt(hb[:], t["sg_buf"][:], t["up_buf"][:], ALU.mult)
                    P.red(t["hm"][:], hb[:], ALU.max, absval=True)
                    P.tt(t["hmx"][:], t["hmx"][:], t["hm"][:], ALU.max)
                    P.dma("pool", d["h_out"][b, :, m, lo:lo + 512], hb[:])
                    if m in SLOTS and g + 1 <= 4 * BPC - 1:
                        emit_quant_tau(g + 1, SLOTS.index(m))

        _colmax_to_scalar(P, t, t["hmx"][:], t["m2v"][:], P0)
        P.dma("pool", d["m2_out"], t["m2v"][:])
        P.wait_all("pool")
        blk = ctx.enter_context(nc.Block())
        P.finalize(blk)
    return nc


def build_l3():
    nc = bass.Bass("TRN2", target_bir_lowering=False)
    d = {}
    d["h_in"] = nc.dram_tensor("h_in", [BPC, 128, IC, N], F32, kind="ExternalInput").ap()
    d["m2vec"] = nc.dram_tensor("m2vec", [1, NCORES], F32, kind="ExternalInput").ap()
    d["clip_d"] = nc.dram_tensor("clip_d", [1, 1], F32, kind="ExternalInput").ap()
    d["wd"] = nc.dram_tensor("wd", [C, I], F32, kind="ExternalInput").ap()
    d["S_in"] = nc.dram_tensor("S_in", [BPC, 128, NT, C], BF16, kind="ExternalInput").ap()
    d["x"] = nc.dram_tensor("x", [BPC, N, C], F32, kind="ExternalInput").ap()
    d["lb"] = nc.dram_tensor("lb", [1, C], F32, kind="ExternalInput").ap()
    d["y"] = nc.dram_tensor("y", [BPC, N, C], F32, kind="ExternalOutput").ap()
    scr = nc.dram_tensor("scr11", [1, 1], F32, kind="Internal").ap()

    with ExitStack() as ctx:
        t = {}
        t["wdT"] = _sb(nc, ctx, "wdT", [128, IC, C], BF16)
        t["rmax"] = _sb(nc, ctx, "rmax", [128, 1], F32)
        t["rq"] = _sb(nc, ctx, "rq", [128, 1], F32)
        t["sw"] = _sb(nc, ctx, "sw", [128, 1], F32)
        t["m2s"] = _sb(nc, ctx, "m2s", [1, NCORES], F32)
        t["m2"] = _sb(nc, ctx, "m2", [1, 1], F32)
        t["sig"] = _sb(nc, ctx, "sig", [1, 1], F32)
        t["s2"] = _sb(nc, ctx, "s2", [1, 1], F32)
        t["r2"] = _sb(nc, ctx, "r2", [1, 1], F32)
        t["r2b"] = _sb(nc, ctx, "r2b", [128, 1], F32)
        t["s2b"] = _sb(nc, ctx, "s2b", [128, 1], F32)
        t["brep"] = _sb(nc, ctx, "brep", [128, C], BF16)
        t["hb"] = [_sb(nc, ctx, f"hb{i}", [128, C], F32) for i in range(2)]
        t["Sb"] = [_sb(nc, ctx, "Sb0", [128, C], BF16)] * 2
        t["xb"] = [_sb(nc, ctx, f"xb{i}", [128, C], F32) for i in range(2)]
        t["ub"] = [_sb(nc, ctx, f"ub{i}", [128, C], F32) for i in range(2)]
        P0 = _ps(nc, ctx, "P0", [128, 1024])
        P1 = _ps(nc, ctx, "P1", [128, 1024])

        P = Prog(nc, ctx)
        P.dma("sync", t["m2s"][:], d["m2vec"])
        P.red(t["m2"][:], t["m2s"][:], ALU.max)
        P.dma("sync", t["sig"][:], d["clip_d"])
        P.act(t["sig"][:], t["sig"][:], AF.Sigmoid)
        P.tt(t["s2"][:], t["sig"][:], t["m2"][:], ALU.mult)
        P.ts(t["s2"][:], t["s2"][:], 1.0 / 127.0, None, ALU.mult)
        P.recip(t["r2"][:], t["s2"][:])
        _scalar_bcast(P, t, "sync", t["s2"][:], t["s2b"][:], scr)
        _scalar_bcast(P, t, "sync", t["r2"][:], t["r2b"][:], scr)
        P.dma("pool", t["brep"][:], d["lb"].partition_broadcast(128))
        with ExitStack() as c2:
            wdnat = _sb(nc, c2, "wdnat", [128, IC, KC, 128], BF16)
            wrow = [_sb(nc, c2, f"wrow{i}", [128, I], F32) for i in range(2)]
            qt = _sb(nc, c2, "qt", [128, I], F32)
            qi3 = _sb(nc, c2, "qi3", [128, I], I32)
            qt2 = _sb(nc, c2, "qt2", [128, I], F32)
            for hc in range(KC):
                wr = wrow[hc % 2]
                P.dma("sync", wr[:], d["wd"].rearrange("(hc p) m -> p hc m", p=128)[:, hc, :])
                P.red(t["rmax"][:], wr[:], ALU.max, absval=True)
                P.recip(t["rq"][:], t["rmax"][:])
                P.ts(t["rq"][:], t["rq"][:], 127.0, None, ALU.mult)
                P.ts(t["sw"][:], t["rmax"][:], 1.0 / 127.0, None, ALU.mult)
                P.ts(qt[:], wr[:], t["rq"][:, 0:1], None, ALU.mult)
                P.copy(qi3[:], qt[:])
                P.copy(qt[:], qi3[:])
                P.ts(wdnat[:, :, hc, :], qt[:].rearrange("p (i n) -> p i n", i=IC),
                     t["sw"][:, 0:1], None, ALU.mult)
            for ic in range(IC):
                P.dmat("act", t["wdT"][:, ic, :].rearrange("p (h n) -> p h n", h=KC),
                       wdnat[:, ic, :, :])
            P.barrier()
        HH = (IC + 1) // 2
        t["hq"] = [_sb(nc, ctx, f"hq{i}", [128, HH, N], BF16) for i in range(3)]
        t["q2b"] = _sb(nc, ctx, "q2b", [128, 512], BF16)
        t["qi5"] = _sb(nc, ctx, "qi5", [128, 512], I32)

        def emit_hquant(b, ic):
            half, j = divmod(ic, HH)
            slot = (2 * b + half) % 3
            for g in range(2):
                hb = t["hb"][g]
                P.dma("sync", hb[:], d["h_in"][b, :, ic, 1024 * g:1024 * (g + 1)])
                P.ts(hb[:], hb[:], t["r2b"][:, 0:1], None, ALU.mult)
                for gg in range(2):
                    P.copy(t["qi5"][:], hb[:, 512 * gg:512 * (gg + 1)])
                    P.copy(t["q2b"][:], t["qi5"][:])
                    P.ts(t["hq"][slot][:, j, 1024 * g + 512 * gg:
                                       1024 * g + 512 * (gg + 1)],
                         t["q2b"][:], 127.0, -128.0, ALU.min, ALU.max)

        for ic in range(IC):
            emit_hquant(0, ic)
        for b in range(BPC):
            for tau in range(NT):
                psD = (P0, P1)[tau % 2]
                for ic in range(IC):
                    half, j = divmod(ic, HH)
                    slot = (2 * b + half) % 3
                    for nh in range(2):
                        P.mm(psD[:, 512 * nh:512 * (nh + 1)],
                             t["hq"][slot][:, j, 128 * tau:128 * (tau + 1)],
                             t["wdT"][:, ic, 512 * nh:512 * (nh + 1)],
                             start=(ic == 0), stop=(ic == IC - 1))
                Sb = t["Sb"][tau % 2]
                xb = t["xb"][tau % 2]
                P.dma("sync", Sb[:], d["S_in"][b, :, tau, :])
                P.dma("sync", xb[:], d["x"][b, 128 * tau:128 * (tau + 1), :])
                ub = t["ub"][tau % 2]
                P.stt(ub[:], psD[:], t["s2b"][:, 0:1], Sb[:], ALU.mult, ALU.add)
                P.tt(ub[:], ub[:], xb[:], ALU.add)
                P.tt(ub[:], ub[:], t["brep"][:], ALU.add)
                P.dma("pool", d["y"][b, 128 * tau:128 * (tau + 1), :], ub[:])
                if b + 1 < BPC and 1 <= tau <= HH:
                    emit_hquant(b + 1, tau - 1)
            if b + 1 < BPC:
                for ic in range(HH, IC):
                    emit_hquant(b + 1, ic)
        P.wait_all("pool")
        blk = ctx.enter_context(nc.Block())
        P.finalize(blk)
    return nc


def _host_tables():
    j = np.arange(512, dtype=np.float64)
    theta = 10000.0 ** (-j / 512.0)
    n = np.arange(N, dtype=np.float64)
    ang = n[:, None] * theta[None, :]
    cosT = np.cos(ang).astype(np.float32)
    sinT = np.sin(ang).astype(np.float32)
    return cosT, sinT, np.ascontiguousarray(cosT.T), np.ascontiguousarray(sinT.T)


def _host_masks():
    m = np.zeros((8, 128, 8), dtype=np.float32)
    for c in range(8):
        for p in range(128):
            r = 128 * c + p
            orig = 2 * r if r < 512 else 2 * (r - 512) + 1
            m[c, p, orig // 128] = 1.0
    return m


_PERM = np.concatenate([np.arange(0, 1024, 2), np.arange(1, 1024, 2)])


def kernel(L0_attn_x, L0_attn_qk_w, L0_attn_qk_b, L0_attn_lepe_w, L0_attn_lepe_b,
           L0_ffn_clip_x, L0_ffn_clip_down, L0_ffn_w_gate, L0_ffn_w_up, L0_ffn_w_down):
    x = np.ascontiguousarray(np.asarray(L0_attn_x, dtype=np.float32))
    qk_w = np.asarray(L0_attn_qk_w, dtype=np.float32)
    qk_b = np.asarray(L0_attn_qk_b, dtype=np.float32)
    wq = np.ascontiguousarray(qk_w[:, :C][:, _PERM])
    wk = np.ascontiguousarray(qk_w[:, C:][:, _PERM])
    bq_p = qk_b[:C][_PERM]
    bk_p = qk_b[C:][_PERM]
    bq_cols = np.ascontiguousarray(bq_p.reshape(KC, 128).T)
    bk_row = np.ascontiguousarray(bk_p.reshape(1, C))
    lw = np.ascontiguousarray(np.asarray(L0_attn_lepe_w, dtype=np.float32))
    lb = np.ascontiguousarray(np.asarray(L0_attn_lepe_b, dtype=np.float32).reshape(1, C))
    wg = np.ascontiguousarray(np.asarray(L0_ffn_w_gate, dtype=np.float32))
    wu = np.ascontiguousarray(np.asarray(L0_ffn_w_up, dtype=np.float32))
    wd = np.ascontiguousarray(np.asarray(L0_ffn_w_down, dtype=np.float32))
    clip_x = np.asarray(L0_ffn_clip_x, dtype=np.float32).reshape(1, 1)
    clip_d = np.asarray(L0_ffn_clip_down, dtype=np.float32).reshape(1, 1)
    cosT, sinT, cosF, sinF = _host_tables()
    eye8 = np.eye(8, dtype=np.float32)
    ident = np.eye(128, dtype=np.float32)
    mask = _host_masks()

    xs = [np.ascontiguousarray(x[BPC * c:BPC * (c + 1)]) for c in range(NCORES)]

    nc1 = build_l1()
    in1 = [dict(x=xs[c], wq=wq, wk=wk, bq=bq_cols, bk=bk_row, lw=lw,
                cosF=cosF, sinF=sinF, cosT=cosT, sinT=sinT,
                eye8=eye8, ident=ident, mask=mask) for c in range(NCORES)]
    r1 = run_bass_kernel_spmd(nc1, in1, core_ids=list(range(NCORES)))
    m1vec = np.concatenate([r1.results[c]["m1_out"].reshape(1, 1)
                            for c in range(NCORES)], axis=1)

    nc2 = build_l2()
    in2 = [dict(xn_in=r1.results[c]["xn_out"], m1vec=m1vec, clip_x=clip_x,
                wg=wg, wu=wu, ident=ident) for c in range(NCORES)]
    r2 = run_bass_kernel_spmd(nc2, in2, core_ids=list(range(NCORES)))
    m2vec = np.concatenate([r2.results[c]["m2_out"].reshape(1, 1)
                            for c in range(NCORES)], axis=1)

    nc3 = build_l3()
    in3 = [dict(h_in=r2.results[c]["h_out"], m2vec=m2vec, clip_d=clip_d,
                wd=wd, S_in=r1.results[c]["S_out"], x=xs[c], lb=lb)
           for c in range(NCORES)]
    r3 = run_bass_kernel_spmd(nc3, in3, core_ids=list(range(NCORES)))
    y = np.concatenate([np.asarray(r3.results[c]["y"]) for c in range(NCORES)], axis=0)
    return y.astype(np.float32)


# version 15
# speedup vs baseline: 1.1634x; 1.0119x over previous
import numpy as np
import ml_dtypes
from contextlib import ExitStack

import concourse.bass as bass
import concourse.mybir as mybir
from concourse.bass_utils import run_bass_kernel_spmd

F32 = mybir.dt.float32
BF16 = mybir.dt.bfloat16
AF = mybir.ActivationFunctionType
ALU = mybir.AluOpType
AX = mybir.AxisListType
BF = ml_dtypes.bfloat16

C_RND = 12582912.0
RSQRT_N = 0.022097086912079608
NCORES = 8
B, N, C = 32, 2048, 1024
BPC = B // NCORES
NT = N // 128
KC = C // 128
I = 2816
IC = I // 128
DEBUG_L1 = False
DEBUG_BARRIER = False
DIAG_TAB_BOUNCE = False
GUARD = 16
NF = GUARD + N + GUARD


class Prog:
    STRICT = False
    STRICT_ENGS = ()
    ENG = ("sync", "act", "pe", "dve", "pool")
    _BLK = {"sync": "sync", "act": "scalar", "pe": "tensor", "dve": "vector",
            "pool": "gpsimd"}
    NSLOT = {"sync": 8, "act": 8, "pe": 0, "dve": 0, "pool": 8}
    KEYS = tuple([(e, "c") for e in ("sync", "act", "pe", "dve", "pool")]
                 + [(e, "d", s) for e in ("sync", "act", "pool")
                    for s in range(8)])

    def __init__(self, nc, ctx):
        self.nc = nc
        self.sems = {k: ctx.enter_context(
            nc.semaphore("sem_" + "_".join(str(p) for p in k)))
            for k in self.KEYS}
        self.cnt = {k: 0 for k in self.KEYS}
        self.known = {e: {k: 0 for k in self.KEYS} for e in self.ENG}
        self.dma_seq = {e: 0 for e in self.ENG}
        self.last_w = {}
        self.readers = {}
        self.ops = {e: [] for e in self.ENG}

    @staticmethod
    def _key(a):
        if a is None:
            return None
        if isinstance(a, str):
            return a
        t = getattr(a, "tensor", a)
        return id(t)

    def op(self, eng, emit, reads=(), writes=(), inc=1):
        if inc >= 16:
            key = (eng, "d", self.dma_seq[eng] % self.NSLOT[eng])
            self.dma_seq[eng] += 1
        else:
            key = (eng, "c")
        waits = {}
        if Prog.STRICT or eng in Prog.STRICT_ENGS:
            for k2 in self.KEYS:
                if self.cnt[k2] > self.known[eng].get(k2, 0):
                    waits[k2] = self.cnt[k2]

        def need(k2, c2):
            if k2 == key == ("pe", "c"):
                return
            if self.known[eng].get(k2, 0) < c2 and waits.get(k2, 0) < c2:
                waits[k2] = c2

        if inc >= 16 and self.cnt[key]:
            need(key, self.cnt[key])
        rk = [k for k in (self._key(a) for a in reads) if k is not None]
        wk = [k for k in (self._key(a) for a in writes) if k is not None]
        for k in rk:
            lw = self.last_w.get(k)
            if lw:
                need(*lw)
        for k in wk:
            lw = self.last_w.get(k)
            if lw:
                need(*lw)
            for k2, c2 in self.readers.get(k, {}).items():
                need(k2, c2)
        after = self.cnt[key] + inc
        self.cnt[key] = after
        for k2, c2 in waits.items():
            self.known[eng][k2] = max(self.known[eng][k2], c2)
        for k in wk:
            self.last_w[k] = (key, after)
            self.readers[k] = {}
        for k in rk:
            d = self.readers.setdefault(k, {})
            d[key] = after
        self.ops[eng].append((waits, emit, inc, key))

    def wait_all(self, eng):
        waits = {}
        for k2 in self.KEYS:
            if self.cnt[k2] > self.known[eng][k2]:
                waits[k2] = self.cnt[k2]
                self.known[eng][k2] = self.cnt[k2]
        if waits:
            self.ops[eng].append((waits, None, 0, None))

    def barrier(self):
        for e in self.ENG:
            self.wait_all(e)

    def dma(self, q, out, in_, extra_reads=(), **kw):
        self.op(q, lambda e: e.dma_start(out=out, in_=in_, **kw),
                reads=[in_, *extra_reads], writes=[out], inc=16)

    def dmat(self, q, out, in_):
        self.op(q, lambda e: e.dma_start(out=out, in_=in_, transpose=True),
                reads=[in_], writes=[out], inc=16)

    def mm(self, out, lhsT, rhs, start, stop):
        self.op("pe", lambda e: e.matmul(out, lhsT, rhs, start=start, stop=stop,
                                         skip_group_check=True),
                reads=[lhsT, rhs], writes=[out], inc=1)

    def act(self, out, in_, func, bias=0.0, scale=1.0, eng="act"):
        er = [a for a in (bias, scale) if hasattr(a, "tensor")]
        self.op(eng, lambda e: e.activation(out, in_, func, bias=bias, scale=scale),
                reads=[in_, *er], writes=[out])

    def ts(self, out, in0, s1, s2, op0, op1=ALU.bypass, eng="dve"):
        er = [a for a in (s1, s2) if hasattr(a, "tensor")]
        self.op(eng, lambda e: e.tensor_scalar(out, in0, s1, s2, op0, op1),
                reads=[in0, *er], writes=[out])

    def tt(self, out, a, b, alu, eng="dve"):
        self.op(eng, lambda e: e.tensor_tensor(out, a, b, alu),
                reads=[a, b], writes=[out])

    def stt(self, out, in0, s, in1, op0, op1, eng="dve"):
        er = [s] if hasattr(s, "tensor") else []
        self.op(eng, lambda e: e.scalar_tensor_tensor(out, in0, s, in1, op0, op1),
                reads=[in0, in1, *er], writes=[out])

    def red(self, out, in_, alu, absval=False, eng="dve"):
        self.op(eng, lambda e: e.tensor_reduce(out, in_, AX.X, alu,
                                               apply_absolute_value=absval),
                reads=[in_], writes=[out])

    def copy(self, out, in_, eng="dve"):
        self.op(eng, lambda e: e.tensor_copy(out, in_), reads=[in_], writes=[out])

    def recip(self, out, in_):
        self.op("dve", lambda e: e.reciprocal(out, in_), reads=[in_], writes=[out])

    def memset(self, ap, v, eng="dve"):
        self.op(eng, lambda e: e.memset(ap, v), reads=[], writes=[ap])

    def finalize(self, blk):
        for e in self.ENG:
            w = {k: self.cnt[k] for k in self.KEYS if k[0] == e and self.cnt[k]}
            if w:
                self.ops[e].append((w, None, 0, None))
        for e in self.ENG:
            def run(eng, _e=e):
                for waits, emit, inc, key in self.ops[_e]:
                    for k2 in self.KEYS:
                        if k2 in waits:
                            eng.wait_ge(self.sems[k2], waits[k2])
                    if emit is not None:
                        ins = emit(eng)
                        if inc:
                            ins.then_inc(self.sems[key], inc)
            getattr(blk, self._BLK[e])(run)


def _sb(nc, ctx, name, shape, dt):
    return ctx.enter_context(nc.sbuf_tensor(name, shape, dt))


def _ps(nc, ctx, name, shape, dt=F32):
    return ctx.enter_context(nc.psum_tensor(name, shape, dt))


def _colmax_to_scalar(P, t, col_f32, out11, psrow):
    P.op("pe", lambda e: e.transpose(psrow[0:1, 0:128], col_f32, t["ident_f"][:]),
         reads=[col_f32, t["ident_f"][:]], writes=[psrow])
    P.copy(t["m1row"][:], psrow[0:1, 0:128])
    P.red(out11, t["m1row"][:], ALU.max)


def _scalar_bcast(P, t, q, src11, dst_col, scratch_dram):
    P.dma("pool", scratch_dram, src11)
    P.dma(q, dst_col, scratch_dram.partition_broadcast(128))


def _emit_quant3(P, out_bf, in_ap, rcol, tmp_f32, tmp2_f32):
    P.ts(tmp_f32, in_ap, rcol, C_RND, ALU.mult, ALU.add)
    P.ts(tmp2_f32, tmp_f32, C_RND, 127.0, ALU.subtract, ALU.min)
    P.ts(out_bf, tmp2_f32, -128.0, None, ALU.max)


def build_l1():
    nc = bass.Bass("TRN2", target_bir_lowering=False)
    d = {}
    d["x"] = nc.dram_tensor("x", [BPC, N, C], F32, kind="ExternalInput").ap()
    d["wq"] = nc.dram_tensor("wq", [C, C], F32, kind="ExternalInput").ap()
    d["wk"] = nc.dram_tensor("wk", [C, C], F32, kind="ExternalInput").ap()
    d["bq"] = nc.dram_tensor("bq", [128, KC], F32, kind="ExternalInput").ap()
    d["bk"] = nc.dram_tensor("bk", [1, C], F32, kind="ExternalInput").ap()
    d["lw"] = nc.dram_tensor("lw", [3, C, C], F32, kind="ExternalInput").ap()
    d["cosF"] = nc.dram_tensor("cosF", [512, N], F32, kind="ExternalInput").ap()
    d["sinF"] = nc.dram_tensor("sinF", [512, N], F32, kind="ExternalInput").ap()
    d["cosT"] = nc.dram_tensor("cosT", [N, 512], F32, kind="ExternalInput").ap()
    d["sinT"] = nc.dram_tensor("sinT", [N, 512], F32, kind="ExternalInput").ap()
    d["eye8"] = nc.dram_tensor("eye8", [8, 8], F32, kind="ExternalInput").ap()
    d["ident"] = nc.dram_tensor("ident", [128, 128], F32, kind="ExternalInput").ap()
    d["mask"] = nc.dram_tensor("mask", [8, 128, 8], F32, kind="ExternalInput").ap()
    d["S_out"] = nc.dram_tensor("S_out", [BPC, 128, NT, C], BF16, kind="ExternalOutput").ap()
    d["xn_out"] = nc.dram_tensor("xn_out", [BPC, 128, NT, C], F32, kind="ExternalOutput").ap()
    d["m1_out"] = nc.dram_tensor("m1_out", [1, 1], F32, kind="ExternalOutput").ap()
    d["cosT_s"] = nc.dram_tensor("cosT_s", [N, 512], BF16, kind="Internal").ap()
    d["sinT_s"] = nc.dram_tensor("sinT_s", [N, 512], BF16, kind="Internal").ap()
    d["cosF_s"] = nc.dram_tensor("cosF_s", [512, N], BF16, kind="Internal").ap()
    d["sinF_s"] = nc.dram_tensor("sinF_s", [512, N], BF16, kind="Internal").ap()
    d["wq_s"] = nc.dram_tensor("wq_s", [128, KC, C], BF16, kind="Internal").ap()
    if DEBUG_L1:
        d["kv_dbg"] = nc.dram_tensor("kv_dbg", [128, C], F32, kind="ExternalOutput").ap()
        d["z_dbg"] = nc.dram_tensor("z_dbg", [128, NT * 8], F32, kind="ExternalOutput").ap()
        d["qr_dbg"] = nc.dram_tensor("qr_dbg", [128, KC, N], F32, kind="ExternalOutput").ap()
        d["q1_dbg"] = nc.dram_tensor("q1_dbg", [128, N], F32, kind="ExternalOutput").ap()
        d["kr_dbg"] = nc.dram_tensor("kr_dbg", [128, C], F32, kind="ExternalOutput").ap()
        d["kr2_dbg"] = nc.dram_tensor("kr2_dbg", [128, C], BF16, kind="ExternalOutput").ap()
        d["t1_dbg"] = nc.dram_tensor("t1_dbg", [128, C], F32, kind="ExternalOutput").ap()
        d["t2_dbg"] = nc.dram_tensor("t2_dbg", [128, C], F32, kind="ExternalOutput").ap()
        d["k1_dbg"] = nc.dram_tensor("k1_dbg", [128, C], F32, kind="ExternalOutput").ap()
        d["tb_dbg"] = nc.dram_tensor("tb_dbg", [128, 2, 512], F32, kind="ExternalOutput").ap()
        d["psk_dbg"] = nc.dram_tensor("psk_dbg", [128, C], F32, kind="ExternalOutput").ap()
        d["psk2_dbg"] = nc.dram_tensor("psk2_dbg", [128, C], F32, kind="ExternalOutput").ap()
        d["km_dbg"] = nc.dram_tensor("km_dbg", [128, KC], F32, kind="ExternalOutput").ap()
        d["xnfm_dbg"] = nc.dram_tensor("xnfm_dbg", [128, KC, NF], BF16, kind="ExternalOutput").ap()
        d["psq_dbg"] = nc.dram_tensor("psq_dbg", [128, C], F32, kind="ExternalOutput").ap()

    with ExitStack() as ctx:
        t = {}
        t["ident_f"] = _sb(nc, ctx, "ident_f", [128, 128], F32)
        t["mask_sb"] = _sb(nc, ctx, "mask_sb", [128, 8, 8], F32)
        t["bq_sb"] = _sb(nc, ctx, "bq_sb", [128, KC], F32)
        t["nbq_sb"] = _sb(nc, ctx, "nbq_sb", [128, KC], F32)
        t["bkb"] = _sb(nc, ctx, "bkb", [1, C], BF16)
        t["ones_row"] = _sb(nc, ctx, "ones_row", [1, 128], BF16)
        t["ones_col"] = _sb(nc, ctx, "ones_col", [128, 1], BF16)
        t["ones11"] = _sb(nc, ctx, "ones11", [1, 1], BF16)
        t["krhs"] = _sb(nc, ctx, "krhs", [128, KC, C], BF16)
        t["lw_sb"] = _sb(nc, ctx, "lw_sb", [128, 24, C], BF16)
        t["wq_buf"] = [_sb(nc, ctx, f"wq{i}", [128, KC, 128], BF16) for i in range(2)]
        t["tabF"] = _sb(nc, ctx, "tabF", [128, 2, 1024], BF16)
        t["tabT"] = [_sb(nc, ctx, f"tabT{i}", [128, 2, 512], BF16) for i in range(2)]
        t["tabT2"] = [_sb(nc, ctx, f"tabT2_{i}", [128, 2, 512], BF16) for i in range(2)] if DIAG_TAB_BOUNCE else None
        t["xn_ring"] = _sb(nc, ctx, "xn_ring", [128, 2, C], BF16)
        t["vh_ring"] = [_sb(nc, ctx, f"vh{i}", [128, C], BF16) for i in range(3)]
        t["qr"] = _sb(nc, ctx, "qr", [128, KC * N], BF16)
        t["xn_fm"] = _sb(nc, ctx, "xn_fm", [128, KC, NF], BF16)
        t["bst"] = [_sb(nc, ctx, f"bst{i}", [128, 12], F32) for i in range(2)]
        t["mv"] = [_sb(nc, ctx, f"mv{i}", [128, 2], F32) for i in range(2)]
        t["vtmp"] = [_sb(nc, ctx, f"vtmp{i}", [128, 1], F32) for i in range(2)]
        t["rstd"] = [_sb(nc, ctx, f"rstd{i}", [128, 1], F32) for i in range(2)]
        t["rstd2"] = [_sb(nc, ctx, f"rstd2{i}", [128, 1], F32) for i in range(2)]
        t["nbias"] = [_sb(nc, ctx, f"nbias{i}", [128, 1], F32) for i in range(2)]
        t["t1"] = [_sb(nc, ctx, f"t1_{i}", [128, C], BF16) for i in range(2)]
        t["t2"] = [_sb(nc, ctx, f"t2_{i}", [128, C], BF16) for i in range(2)]
        t["x_buf"] = [_sb(nc, ctx, f"xbuf{i}", [128, C], F32) for i in range(2)]
        t["k1"] = [_sb(nc, ctx, f"k1_{i}", [128, C], BF16) for i in range(3)]
        t["kr"] = [_sb(nc, ctx, f"kr_{i}", [128, C], BF16) for i in range(3)]
        t["kv_sb"] = _sb(nc, ctx, "kv_sb", [128, C], BF16)
        t["km_bf"] = _sb(nc, ctx, "km_bf", [1, C], BF16)
        t["km_cols"] = _sb(nc, ctx, "km_cols", [128, KC], F32)
        t["A_sb"] = _sb(nc, ctx, "A_sb", [128, KC, 8], BF16)
        t["q1"] = [_sb(nc, ctx, f"q1_{i}", [128, N], BF16) for i in range(2)]
        t["r1"] = [_sb(nc, ctx, f"r1_{i}", [128, C], BF16) for i in range(2)]
        t["r2k"] = [_sb(nc, ctx, f"r2k_{i}", [128, C], BF16) for i in range(2)]
        t["eye8_f"] = _sb(nc, ctx, "eye8_f", [8, 8], F32)
        t["z_tm"] = _sb(nc, ctx, "z_tm", [128, NT, 8], F32)
        t["zrow"] = _sb(nc, ctx, "zrow", [128, 64], F32)
        t["S_buf"] = [_sb(nc, ctx, f"Sb{i}", [128, C], BF16) for i in range(2)]
        t["psq_sav"] = None
        t["t5_dbg"] = None
        t["mx"] = _sb(nc, ctx, "mx", [128, BPC * NT], F32)
        t["mxf"] = _sb(nc, ctx, "mxf", [128, 1], F32)
        t["m1row"] = _sb(nc, ctx, "m1row", [1, 128], F32)
        t["m1v"] = _sb(nc, ctx, "m1v", [1, 1], F32)
        P0 = _ps(nc, ctx, "P0", [128, 1024])
        P1 = _ps(nc, ctx, "P1", [128, 1024])
        P2 = _ps(nc, ctx, "P2", [128, 1024])
        P3 = _ps(nc, ctx, "P3", [128, 1024])

        P = Prog(nc, ctx)
        P.dma("sync", t["ident_f"][:], d["ident"])
        P.dma("sync", t["eye8_f"][:], d["eye8"])
        P.dma("sync", t["mask_sb"][:], d["mask"].rearrange("c p h -> p c h"))
        P.dma("sync", t["bq_sb"][:], d["bq"])
        P.ts(t["nbq_sb"][:], t["bq_sb"][:], -1.0, None, ALU.mult)
        P.dma("pool", t["bkb"][:], d["bk"])
        P.memset(t["ones_row"][:], 1.0)
        P.memset(t["ones_col"][:], 1.0)
        P.memset(t["ones11"][:], 1.0)
        P.op("dve", lambda e: e.memset(t["xn_fm"][:], 0.0),
             reads=[], writes=[f"xnfm{i}" for i in range(NT)])
        P.dma("pool", t["krhs"][:], d["wk"].rearrange("(kc p) n -> p kc n", p=128))
        P.dma("pool", t["lw_sb"][:], d["lw"].rearrange("t (kc p) m -> p (t kc) m", p=128))
        P.dma("pool", d["cosT_s"], d["cosT"])
        P.dma("pool", d["sinT_s"], d["sinT"])
        P.dma("pool", d["cosF_s"], d["cosF"])
        P.dma("pool", d["sinF_s"], d["sinF"])
        P.dma("pool", d["wq_s"], d["wq"].rearrange("(kc p) m -> p kc m", p=128))

        qr = t["qr"][:].rearrange("p (c n) -> p c n", c=KC)

        XK = [f"xnfm{i}" for i in range(NT)]

        def emit_ln(b, tau):
            xb = t["x_buf"][tau % 2]
            P.dma("pool", xb[:], d["x"][b, 128 * tau:128 * (tau + 1), :])
            bst = t["bst"][tau % 2]
            mv = t["mv"][tau % 2]
            vtmp = t["vtmp"][tau % 2]
            rstd = t["rstd"][tau % 2]
            rstd2 = t["rstd2"][tau % 2]
            nbias = t["nbias"][tau % 2]
            for g in range(2):
                P.op("dve", lambda e, o=bst[:, 6 * g:6 * (g + 1)],
                     i=xb[:, 512 * g:512 * (g + 1)]: e.bn_stats(o, i),
                     reads=[xb[:]], writes=[bst[:]])
            P.op("dve", lambda e, o=mv[:], i=bst[:]: e.bn_aggr(o, i),
                 reads=[bst[:]], writes=[mv[:]])
            P.ts(vtmp[:], mv[:, 1:2], 1e-5, None, ALU.add)
            P.act(rstd[:], vtmp[:], AF.Ln)
            P.act(rstd2[:], rstd[:], AF.Exp, scale=-0.5)
            P.stt(nbias[:], mv[:, 0:1], -1.0, rstd2[:, 0:1], ALU.mult, ALU.mult)
            P.act(xb[:], xb[:], AF.Identity, bias=nbias[:, 0:1],
                  scale=rstd2[:, 0:1])
            P.copy(t["xn_ring"][:, tau % 2, :], xb[:])
            P.dma("pool", d["xn_out"][b, :, tau, :], xb[:])
            P.red(t["mx"][:, b * NT + tau: b * NT + tau + 1], xb[:],
                  ALU.max, absval=True)
            out = t["xn_fm"][:, :, GUARD + 128 * tau: GUARD + 128 * (tau + 1)]
            srcv = t["xn_ring"][:, tau % 2, :].rearrange("p (c n) -> p c n", c=KC)
            P.op("sync", lambda e, o=out, s=srcv: e.dma_start(out=o, in_=s, transpose=True),
                 reads=[srcv], writes=[XK[tau]], inc=16)

        for b in range(BPC):
            def emit_kv(tau):
                k1 = t["k1"][tau % 3]
                kr = t["kr"][tau % 3]
                for nh in range(2):
                    P.mm(P3[0:1, 512 * nh:512 * (nh + 1)], t["ones_col"][:],
                         k1[:, 512 * nh:512 * (nh + 1)],
                         start=(tau == 0), stop=(tau == NT - 1))
                for h in range(8):
                    P.mm(P2[:, 128 * h:128 * (h + 1)],
                         kr[:, 128 * h:128 * (h + 1)],
                         t["vh_ring"][tau % 3][:, 128 * h:128 * (h + 1)],
                         start=(tau == 0 and h % 4 == 0), stop=(tau == NT - 1))

            if b == 0:
                for tau in range(NT):
                    emit_ln(0, tau)
            for tau in range(NT):
                for k in range(KC):
                    vout = t["vh_ring"][tau % 3][:, 128 * k:128 * (k + 1)]
                    vin = t["xn_fm"][:, k, GUARD + 128 * tau: GUARD + 128 * (tau + 1)]
                    P.op("sync", lambda e, o=vout, s=vin:
                         e.dma_start(out=o, in_=s, transpose=True),
                         reads=[XK[tau]], writes=[t["vh_ring"][tau % 3][:]], inc=16)
                tb = t["tabT"][tau % 2]
                P.dma("act", tb[:, 0, :], d["cosT_s"][128 * tau:128 * (tau + 1), :])
                P.dma("act", tb[:, 1, :], d["sinT_s"][128 * tau:128 * (tau + 1), :])
                psK = (P0, P1)[tau % 2]
                for k in range(KC):
                    lhsT = t["xn_fm"][:, k, GUARD + 128 * tau: GUARD + 128 * (tau + 1)]
                    for nh in range(2):
                        P.op("pe", lambda e, o=psK[:, 512 * nh:512 * (nh + 1)], l=lhsT,
                             r=t["krhs"][:, k, 512 * nh:512 * (nh + 1)],
                             st=(k == 0): e.matmul(o, l, r, start=st, stop=False,
                                                   skip_group_check=True),
                             reads=[XK[tau], t["krhs"][:]], writes=[psK[:]], inc=1)
                for nh in range(2):
                    P.mm(psK[:, 512 * nh:512 * (nh + 1)], t["ones_row"][:],
                         t["bkb"][0:1, 512 * nh:512 * (nh + 1)], start=False, stop=True)
                if tau > 1:
                    emit_kv(tau - 2)
                k1 = t["k1"][tau % 3]
                t1 = t["t1"][tau % 2]
                t2 = t["t2"][tau % 2]
                P.act(t2[:], psK[:], AF.Relu, scale=-1.0)
                P.act(t1[:], t2[:], AF.Exp, scale=-1.0)
                P.act(t2[:], psK[:], AF.Relu)
                P.tt(k1[:], t1[:], t2[:], ALU.add)
                kr = t["kr"][tau % 3]
                cosT_, sinT_ = tb[:, 0, :], tb[:, 1, :]
                r1 = t["r1"][tau % 2]
                r2k = t["r2k"][tau % 2]
                P.tt(r1[:, 0:512], cosT_, k1[:, 0:512], ALU.mult)
                P.tt(r1[:, 512:1024], sinT_, k1[:, 512:1024], ALU.mult)
                P.tt(kr[:, 0:512], r1[:, 0:512], r1[:, 512:1024],
                     ALU.subtract)
                P.tt(r2k[:, 0:512], cosT_, k1[:, 512:1024], ALU.mult)
                P.tt(r2k[:, 512:1024], sinT_, k1[:, 0:512], ALU.mult)
                P.tt(kr[:, 512:1024], r2k[:, 0:512], r2k[:, 512:1024],
                     ALU.add)
                if DEBUG_L1 and b == 0 and tau == 0:
                    P.dma("pool", d["kr_dbg"], kr[:])
                    P.dma("act", d["kr2_dbg"], kr[:])
                    P.dma("pool", d["t1_dbg"], t["t1"][tau % 2][:])
                    P.dma("pool", d["t2_dbg"], t["t2"][tau % 2][:])
                    P.dma("pool", d["k1_dbg"], k1[:])
                    P.dma("pool", d["tb_dbg"], tb[:])
            emit_kv(NT - 2)
            emit_kv(NT - 1)
            P.ts(t["kv_sb"][:], P2[:], RSQRT_N, None, ALU.mult)
            P.ts(t["km_bf"][:], P3[0:1, :], 1.0 / N, None, ALU.mult)
            for c in range(KC):
                P.mm(P2[:, c:c + 1], t["km_bf"][0:1, 128 * c:128 * (c + 1)],
                     t["ones11"][:], start=(c == 0), stop=(c == KC - 1))
            P.copy(t["km_cols"][:], P2[:, 0:KC])
            for c in range(KC):
                P.ts(t["A_sb"][:, c, :], t["mask_sb"][:, c, :],
                     t["km_cols"][:, c:c + 1], None, ALU.mult)

            def emit_zmm(m, zi):
                q1 = t["q1"][zi % 2]
                for th in range(2):
                    psZ = (P2, P3)[th]
                    for ns in range(2):
                        lo = 1024 * th + 512 * ns
                        P.mm(psZ[0:8, 512 * ns:512 * (ns + 1)], t["A_sb"][:, m, :],
                             q1[:, lo:lo + 512], start=(zi == 0), stop=(zi == 7))

            pair_done = {}
            order = [0, 4, 1, 5, 2, 6, 3, 7]
            for mi, m in enumerate(order):
                wqb = t["wq_buf"][mi % 2]
                P.dma("sync", wqb[:], d["wq_s"][:, :, 128 * m:128 * (m + 1)])
                q1 = t["q1"][mi % 2]
                for th in range(2):
                    psQ = (P0, P1)[th]
                    for k in range(KC):
                        for ns in range(2):
                            lo = 1024 * th + 512 * ns
                            seg = 2 * th + ns
                            P.op("pe", lambda e, o=psQ[:, 512 * ns:512 * (ns + 1)],
                                 l=wqb[:, k, :],
                                 r=t["xn_fm"][:, k, GUARD + lo: GUARD + lo + 512],
                                 st=(k == 0), sp=(k == KC - 1):
                                 e.matmul(o, l, r, start=st, stop=sp,
                                          skip_group_check=True),
                                 reads=[wqb[:], *(XK[4 * seg + j] for j in range(4))],
                                 writes=[psQ[:]], inc=1)
                    if th == 1 and mi > 0:
                        emit_zmm(order[mi - 1], mi - 1)
                    sl_q = slice(1024 * th, 1024 * (th + 1))
                    t1 = t["t1"][th]
                    t2 = t["t2"][th]
                    P.act(t2[:], psQ[:], AF.Relu, scale=-1.0,
                          bias=t["nbq_sb"][:, m:m + 1])
                    P.act(t1[:], t2[:], AF.Exp, scale=-1.0)
                    P.act(t2[:], psQ[:], AF.Relu, bias=t["bq_sb"][:, m:m + 1])
                    P.tt(q1[:, sl_q], t2[:], t1[:], ALU.add)
                if DEBUG_L1 and b == 0 and m == 0:
                    P.dma("pool", d["q1_dbg"], q1[:])
                pair_done[m] = q1
                cpair = m if m < 4 else m - 4
                if cpair in pair_done and cpair + 4 in pair_done:
                    qE, qO = pair_done[cpair], pair_done[cpair + 4]
                    for th in range(2):
                        sl = slice(1024 * th, 1024 * (th + 1))
                        tf = t["tabF"]
                        r1 = t["r1"][th]
                        r2k = t["r2k"][th]
                        P.dma("act", tf[:, 0, :],
                              d["cosF_s"][128 * cpair:128 * (cpair + 1), sl])
                        P.dma("act", tf[:, 1, :],
                              d["sinF_s"][128 * cpair:128 * (cpair + 1), sl])
                        cosF_, sinF_ = tf[:, 0, :], tf[:, 1, :]
                        P.tt(r1[:], cosF_, qE[:, sl], ALU.mult)
                        P.tt(r2k[:], sinF_, qO[:, sl], ALU.mult)
                        P.tt(qr[:, cpair, sl], r1[:], r2k[:], ALU.subtract)
                        P.tt(r1[:, 0:512], cosF_[:, 0:512], qO[:, sl][:, 0:512], ALU.mult)
                        P.tt(r1[:, 512:1024], sinF_[:, 0:512], qE[:, sl][:, 0:512], ALU.mult)
                        P.tt(qr[:, cpair + 4, sl][:, 0:512], r1[:, 0:512],
                             r1[:, 512:1024], ALU.add)
                        P.tt(r2k[:, 0:512], cosF_[:, 512:1024], qO[:, sl][:, 512:1024], ALU.mult)
                        P.tt(r2k[:, 512:1024], sinF_[:, 512:1024], qE[:, sl][:, 512:1024], ALU.mult)
                        P.tt(qr[:, cpair + 4, sl][:, 512:1024], r2k[:, 0:512],
                             r2k[:, 512:1024], ALU.add)
                    del pair_done[cpair], pair_done[cpair + 4]
            emit_zmm(order[7], 7)
            for th in range(2):
                psZ = (P2, P3)[th]
                P.ts(t["x_buf"][0][0:8, 0:1024], psZ[0:8, :], 1e-6, None, ALU.add)
                for tt8 in range(8):
                    P.mm(P0[:, 8 * tt8:8 * (tt8 + 1)],
                         t["x_buf"][0][0:8, 128 * tt8:128 * (tt8 + 1)], t["eye8_f"][:],
                         start=(tt8 == 0), stop=True)
                P.copy(t["zrow"][:], P0[:, 0:64])
                P.recip(t["z_tm"][:, 8 * th:8 * (th + 1), :],
                        t["zrow"][:].rearrange("p (a e) -> p a e", a=8))

            for tau in range(NT):
                psA = (P0, P1)[tau % 2]
                psL = (P2, P3)[tau % 2]
                for h in range(8):
                    P.mm(psA[:, 128 * h:128 * (h + 1)],
                         qr[:, h, 128 * tau:128 * (tau + 1)],
                         t["kv_sb"][:, 128 * h:128 * (h + 1)],
                         start=(h % 4 == 0), stop=True)
                nmm = 0
                for tap in range(3):
                    for k in range(KC):
                        lhsT = t["xn_fm"][:, k,
                                          GUARD + 128 * tau + tap - 1:
                                          GUARD + 128 * tau + tap - 1 + 128]
                        rkeys = [XK[j] for j in (tau - 1, tau, tau + 1)
                                 if 0 <= j < NT]
                        for nh in range(2):
                            P.op("pe", lambda e, o=psL[:, 512 * nh:512 * (nh + 1)],
                                 l=lhsT,
                                 r=t["lw_sb"][:, 8 * tap + k, 512 * nh:512 * (nh + 1)],
                                 st=(nmm == 0), sp=(nmm == 23):
                                 e.matmul(o, l, r, start=st, stop=sp,
                                          skip_group_check=True),
                                 reads=[t["lw_sb"][:], *rkeys], writes=[psL[:]], inc=1)
                        nmm += 1
                zb = t["z_tm"][:, tau, :].unsqueeze(2).broadcast_to([128, 8, 128])
                Sb = t["S_buf"][tau % 2]
                P.tt(Sb[:].rearrange("p (h n) -> p h n", h=8),
                     psA[:].rearrange("p (h n) -> p h n", h=8), zb, ALU.mult)
                P.tt(Sb[:], Sb[:], psL[:], ALU.add)
                P.dma("act", d["S_out"][b, :, tau, :], Sb[:])
                if tau >= 1 and b + 1 < BPC:
                    emit_ln(b + 1, tau - 1)
            if b + 1 < BPC:
                emit_ln(b + 1, NT - 1)
            if DEBUG_L1 and b == 0:
                P.op("pool", lambda e: e.dma_start(out=d["xnfm_dbg"], in_=t["xn_fm"][:]),
                     reads=list(XK), writes=[d["xnfm_dbg"]], inc=16)
                P.dma("pool", d["kv_dbg"], t["kv_sb"][:])
                P.dma("pool", d["z_dbg"], t["z_tm"][:].rearrange("p a e -> p (a e)"))
                P.dma("pool", d["qr_dbg"], qr)
                P.dma("pool", d["km_dbg"], t["km_cols"][:])

        P.red(t["mxf"][:], t["mx"][:], ALU.max)
        _colmax_to_scalar(P, t, t["mxf"][:], t["m1v"][:], P0)
        P.dma("pool", d["m1_out"], t["m1v"][:])
        P.wait_all("pool")

        blk = ctx.enter_context(nc.Block())
        P.finalize(blk)
    return nc


def build_l2():
    nc = bass.Bass("TRN2", target_bir_lowering=False)
    d = {}
    d["xn_in"] = nc.dram_tensor("xn_in", [BPC, 128, NT, C], F32, kind="ExternalInput").ap()
    d["m1vec"] = nc.dram_tensor("m1vec", [1, NCORES], F32, kind="ExternalInput").ap()
    d["clip_x"] = nc.dram_tensor("clip_x", [1, 1], F32, kind="ExternalInput").ap()
    d["wg"] = nc.dram_tensor("wg", [I, C], F32, kind="ExternalInput").ap()
    d["wu"] = nc.dram_tensor("wu", [I, C], F32, kind="ExternalInput").ap()
    d["ident"] = nc.dram_tensor("ident", [128, 128], F32, kind="ExternalInput").ap()
    d["h_out"] = nc.dram_tensor("h_out", [BPC, 128, IC, N], F32, kind="ExternalOutput").ap()
    d["m2_out"] = nc.dram_tensor("m2_out", [1, 1], F32, kind="ExternalOutput").ap()
    scr = nc.dram_tensor("scr11", [1, 1], F32, kind="Internal").ap()

    with ExitStack() as ctx:
        t = {}
        t["ident_f"] = _sb(nc, ctx, "ident_f", [128, 128], F32)
        t["wgT"] = _sb(nc, ctx, "wgT", [128, KC, I], BF16)
        t["wuT"] = _sb(nc, ctx, "wuT", [128, KC, I], BF16)
        t["rmax"] = _sb(nc, ctx, "rmax", [128, 1], F32)
        t["rq"] = _sb(nc, ctx, "rq", [128, 1], F32)
        t["swg"] = _sb(nc, ctx, "swg", [128, IC], F32)
        t["swu"] = _sb(nc, ctx, "swu", [128, IC], F32)
        t["sgc"] = _sb(nc, ctx, "sgc", [128, IC], F32)
        t["suc"] = _sb(nc, ctx, "suc", [128, IC], F32)
        t["m1s"] = _sb(nc, ctx, "m1s", [1, NCORES], F32)
        t["m1"] = _sb(nc, ctx, "m1", [1, 1], F32)
        t["sig"] = _sb(nc, ctx, "sig", [1, 1], F32)
        t["s1"] = _sb(nc, ctx, "s1", [1, 1], F32)
        t["r1"] = _sb(nc, ctx, "r1", [1, 1], F32)
        t["r1b"] = _sb(nc, ctx, "r1b", [128, 1], F32)
        t["s1b"] = _sb(nc, ctx, "s1b", [128, 1], F32)
        t["hm"] = _sb(nc, ctx, "hm", [128, 1], F32)
        t["hmx"] = _sb(nc, ctx, "hmx", [128, 1], F32)
        t["m1row"] = _sb(nc, ctx, "m1row", [1, 128], F32)
        t["m2v"] = _sb(nc, ctx, "m2v", [1, 1], F32)
        P0 = _ps(nc, ctx, "P0", [128, 512])
        P1 = _ps(nc, ctx, "P1", [128, 512])
        P2 = _ps(nc, ctx, "P2", [128, 512])
        P3 = _ps(nc, ctx, "P3", [128, 512])

        P = Prog(nc, ctx)
        P.dma("sync", t["ident_f"][:], d["ident"])
        P.dma("sync", t["m1s"][:], d["m1vec"])
        P.red(t["m1"][:], t["m1s"][:], ALU.max)
        P.dma("sync", t["sig"][:], d["clip_x"])
        P.act(t["sig"][:], t["sig"][:], AF.Sigmoid)
        P.tt(t["s1"][:], t["sig"][:], t["m1"][:], ALU.mult)
        P.ts(t["s1"][:], t["s1"][:], 1.0 / 127.0, None, ALU.mult)
        P.recip(t["r1"][:], t["s1"][:])
        _scalar_bcast(P, t, "sync", t["s1"][:], t["s1b"][:], scr)
        _scalar_bcast(P, t, "sync", t["r1"][:], t["r1b"][:], scr)
        with ExitStack() as c2:
            wnat = _sb(nc, c2, "wnat", [128, KC, IC, 128], BF16)
            wrow = [_sb(nc, c2, f"wrow{i}", [128, C], F32) for i in range(2)]
            qt = _sb(nc, c2, "qt", [128, C], F32)
            crnd = _sb(nc, c2, "crnd", [128, 1], F32)
            ncrnd = _sb(nc, c2, "ncrnd", [128, 1], F32)
            P.memset(crnd[:], C_RND)
            P.memset(ncrnd[:], -C_RND)
            for wsrc, wT, sw in ((d["wg"], t["wgT"], t["swg"]), (d["wu"], t["wuT"], t["swu"])):
                for cc in range(IC):
                    wr = wrow[cc % 2]
                    P.dma("sync", wr[:], wsrc.rearrange("(cc p) m -> p cc m", p=128)[:, cc, :])
                    P.red(t["rmax"][:], wr[:], ALU.max, absval=True)
                    P.recip(t["rq"][:], t["rmax"][:])
                    P.ts(t["rq"][:], t["rq"][:], 127.0, None, ALU.mult)
                    P.ts(sw[:, cc:cc + 1], t["rmax"][:], 1.0 / 127.0, None, ALU.mult)
                    P.act(qt[:], wr[:], AF.Identity, bias=crnd[:, 0:1],
                          scale=t["rq"][:, 0:1])
                    P.act(wnat[:, :, cc, :], qt[:].rearrange("p (h n) -> p h n", h=KC),
                          AF.Identity, bias=ncrnd[:, 0:1])
                for hc in range(KC):
                    P.dmat("act", wT[:, hc, :].rearrange("p (i n) -> p i n", i=IC),
                           wnat[:, hc, :, :])
            P.barrier()
        t["xq"] = [_sb(nc, ctx, f"xq{i}", [128, KC, N], BF16) for i in range(2)]
        t["xnb"] = [_sb(nc, ctx, f"xnb{i}", [128, C], F32) for i in range(2)]
        t["xqt"] = [_sb(nc, ctx, f"xqt{i}", [128, C], BF16) for i in range(2)]
        t["qtmp"] = _sb(nc, ctx, "qtmp", [128, C], F32)
        t["qtmp2"] = _sb(nc, ctx, "qtmp2", [128, C], F32)
        t["sg_buf"] = _sb(nc, ctx, "sg_buf", [128, 512], F32)
        t["up_buf"] = _sb(nc, ctx, "up_buf", [128, 512], F32)
        t["h_buf"] = [_sb(nc, ctx, f"hb{i}", [128, 512], F32) for i in range(2)]
        P.tt(t["sgc"][:], t["swg"][:], t["s1b"][:, 0:1].broadcast_to([128, IC]), ALU.mult)
        P.tt(t["suc"][:], t["swu"][:], t["s1b"][:, 0:1].broadcast_to([128, IC]), ALU.mult)
        P.memset(t["hmx"][:], 0.0)

        def emit_quant_tau(g, j):
            gb, gs = divmod(g, 4)
            tau = 4 * gs + j
            xnb = t["xnb"][tau % 2]
            xqt = t["xqt"][tau % 2]
            P.dma("sync", xnb[:], d["xn_in"][gb, :, tau, :])
            _emit_quant3(P, xqt[:], xnb[:], t["r1b"][:, 0:1], t["qtmp"][:], t["qtmp2"][:])
            out = t["xq"][gb % 2][:, :, 128 * tau:128 * (tau + 1)]
            src = xqt[:].rearrange("p (c n) -> p c n", c=KC)
            P.op("act", lambda e, o=out, s=src: e.dma_start(out=o, in_=s, transpose=True),
                 reads=[src], writes=[f"xq{gb % 2}s{gs}"], inc=16)

        for j in range(4):
            emit_quant_tau(0, j)
        SLOTS = (4, 9, 14, 19)
        for b in range(BPC):
            for s in range(4):
                g = 4 * b + s
                lo = 512 * s
                xqb = t["xq"][b % 2]
                xqkey = f"xq{b % 2}s{s}"
                for m in range(IC):
                    psG = (P0, P1)[m % 2]
                    psU = (P2, P3)[m % 2]
                    for k in range(KC):
                        P.op("pe", lambda e, o=psG[:], l=t["wgT"][:, k, 128 * m:128 * (m + 1)],
                             r=xqb[:, k, lo:lo + 512], st=(k == 0), sp=(k == KC - 1):
                             e.matmul(o, l, r, start=st, stop=sp, skip_group_check=True),
                             reads=[t["wgT"][:, k, 128 * m:128 * (m + 1)], xqkey],
                             writes=[psG[:]], inc=1)
                        P.op("pe", lambda e, o=psU[:], l=t["wuT"][:, k, 128 * m:128 * (m + 1)],
                             r=xqb[:, k, lo:lo + 512], st=(k == 0), sp=(k == KC - 1):
                             e.matmul(o, l, r, start=st, stop=sp, skip_group_check=True),
                             reads=[t["wuT"][:, k, 128 * m:128 * (m + 1)], xqkey],
                             writes=[psU[:]], inc=1)
                    P.act(t["sg_buf"][:], psG[:], AF.Silu, scale=t["sgc"][:, m:m + 1])
                    P.ts(t["up_buf"][:], psU[:], t["suc"][:, m:m + 1], None, ALU.mult)
                    hb = t["h_buf"][m % 2]
                    P.tt(hb[:], t["sg_buf"][:], t["up_buf"][:], ALU.mult)
                    P.red(t["hm"][:], hb[:], ALU.max, absval=True)
                    P.tt(t["hmx"][:], t["hmx"][:], t["hm"][:], ALU.max)
                    P.dma("pool", d["h_out"][b, :, m, lo:lo + 512], hb[:])
                    if m in SLOTS and g + 1 <= 4 * BPC - 1:
                        emit_quant_tau(g + 1, SLOTS.index(m))

        _colmax_to_scalar(P, t, t["hmx"][:], t["m2v"][:], P0)
        P.dma("pool", d["m2_out"], t["m2v"][:])
        P.wait_all("pool")
        blk = ctx.enter_context(nc.Block())
        P.finalize(blk)
    return nc


def build_l3():
    nc = bass.Bass("TRN2", target_bir_lowering=False)
    d = {}
    d["h_in"] = nc.dram_tensor("h_in", [BPC, 128, IC, N], F32, kind="ExternalInput").ap()
    d["m2vec"] = nc.dram_tensor("m2vec", [1, NCORES], F32, kind="ExternalInput").ap()
    d["clip_d"] = nc.dram_tensor("clip_d", [1, 1], F32, kind="ExternalInput").ap()
    d["wd"] = nc.dram_tensor("wd", [C, I], F32, kind="ExternalInput").ap()
    d["S_in"] = nc.dram_tensor("S_in", [BPC, 128, NT, C], BF16, kind="ExternalInput").ap()
    d["x"] = nc.dram_tensor("x", [BPC, N, C], F32, kind="ExternalInput").ap()
    d["lb"] = nc.dram_tensor("lb", [1, C], F32, kind="ExternalInput").ap()
    d["y"] = nc.dram_tensor("y", [BPC, N, C], F32, kind="ExternalOutput").ap()
    scr = nc.dram_tensor("scr11", [1, 1], F32, kind="Internal").ap()

    with ExitStack() as ctx:
        t = {}
        t["wdT"] = _sb(nc, ctx, "wdT", [128, IC, C], BF16)
        t["rmax"] = _sb(nc, ctx, "rmax", [128, 1], F32)
        t["rq"] = _sb(nc, ctx, "rq", [128, 1], F32)
        t["sw"] = _sb(nc, ctx, "sw", [128, 1], F32)
        t["m2s"] = _sb(nc, ctx, "m2s", [1, NCORES], F32)
        t["m2"] = _sb(nc, ctx, "m2", [1, 1], F32)
        t["sig"] = _sb(nc, ctx, "sig", [1, 1], F32)
        t["s2"] = _sb(nc, ctx, "s2", [1, 1], F32)
        t["r2"] = _sb(nc, ctx, "r2", [1, 1], F32)
        t["r2b"] = _sb(nc, ctx, "r2b", [128, 1], F32)
        t["s2b"] = _sb(nc, ctx, "s2b", [128, 1], F32)
        t["brep"] = _sb(nc, ctx, "brep", [128, C], BF16)
        t["hb"] = [_sb(nc, ctx, f"hb{i}", [128, C], F32) for i in range(2)]
        t["Sb"] = [_sb(nc, ctx, f"Sb{i}", [128, C], BF16) for i in range(2)]
        t["xb"] = [_sb(nc, ctx, f"xb{i}", [128, C], F32) for i in range(2)]
        t["ub"] = [_sb(nc, ctx, f"ub{i}", [128, C], F32) for i in range(2)]
        P0 = _ps(nc, ctx, "P0", [128, 1024])
        P1 = _ps(nc, ctx, "P1", [128, 1024])

        P = Prog(nc, ctx)
        P.dma("sync", t["m2s"][:], d["m2vec"])
        P.red(t["m2"][:], t["m2s"][:], ALU.max)
        P.dma("sync", t["sig"][:], d["clip_d"])
        P.act(t["sig"][:], t["sig"][:], AF.Sigmoid)
        P.tt(t["s2"][:], t["sig"][:], t["m2"][:], ALU.mult)
        P.ts(t["s2"][:], t["s2"][:], 1.0 / 127.0, None, ALU.mult)
        P.recip(t["r2"][:], t["s2"][:])
        _scalar_bcast(P, t, "sync", t["s2"][:], t["s2b"][:], scr)
        _scalar_bcast(P, t, "sync", t["r2"][:], t["r2b"][:], scr)
        P.dma("pool", t["brep"][:], d["lb"].partition_broadcast(128))
        with ExitStack() as c2:
            wdnat = _sb(nc, c2, "wdnat", [128, IC, KC, 128], BF16)
            wrow = [_sb(nc, c2, f"wrow{i}", [128, I], F32) for i in range(2)]
            qt = _sb(nc, c2, "qt", [128, I], F32)
            crnd3 = _sb(nc, c2, "crnd3", [128, 1], F32)
            P.memset(crnd3[:], C_RND)
            for hc in range(KC):
                wr = wrow[hc % 2]
                P.dma("sync", wr[:], d["wd"].rearrange("(hc p) m -> p hc m", p=128)[:, hc, :])
                P.red(t["rmax"][:], wr[:], ALU.max, absval=True)
                P.recip(t["rq"][:], t["rmax"][:])
                P.ts(t["rq"][:], t["rq"][:], 127.0, None, ALU.mult)
                P.ts(t["sw"][:], t["rmax"][:], 1.0 / 127.0, None, ALU.mult)
                P.act(qt[:], wr[:], AF.Identity, bias=crnd3[:, 0:1],
                      scale=t["rq"][:, 0:1])
                P.stt(wdnat[:, :, hc, :], qt[:].rearrange("p (i n) -> p i n", i=IC),
                      C_RND, t["sw"][:, 0:1].unsqueeze(1).broadcast_to([128, IC, 128]),
                      ALU.subtract, ALU.mult)
            for ic in range(IC):
                P.dmat("act", t["wdT"][:, ic, :].rearrange("p (h n) -> p h n", h=KC),
                       wdnat[:, ic, :, :])
            P.barrier()
        HH = (IC + 1) // 2
        t["hq"] = [_sb(nc, ctx, f"hq{i}", [128, HH, N], BF16) for i in range(3)]
        t["q2b"] = _sb(nc, ctx, "q2b", [128, 512], BF16)

        def emit_hquant(b, ic):
            half, j = divmod(ic, HH)
            slot = (2 * b + half) % 3
            for g in range(2):
                hb = t["hb"][g]
                P.dma("sync", hb[:], d["h_in"][b, :, ic, 1024 * g:1024 * (g + 1)])
                P.ts(hb[:], hb[:], t["r2b"][:, 0:1], C_RND, ALU.mult, ALU.add)
                for gg in range(2):
                    P.ts(t["q2b"][:], hb[:, 512 * gg:512 * (gg + 1)],
                         C_RND, 127.0, ALU.subtract, ALU.min)
                    P.ts(t["hq"][slot][:, j, 1024 * g + 512 * gg:
                                       1024 * g + 512 * (gg + 1)],
                         t["q2b"][:], -128.0, None, ALU.max)

        for ic in range(IC):
            emit_hquant(0, ic)
        for b in range(BPC):
            for tau in range(NT):
                psD = (P0, P1)[tau % 2]
                for ic in range(IC):
                    half, j = divmod(ic, HH)
                    slot = (2 * b + half) % 3
                    for nh in range(2):
                        P.mm(psD[:, 512 * nh:512 * (nh + 1)],
                             t["hq"][slot][:, j, 128 * tau:128 * (tau + 1)],
                             t["wdT"][:, ic, 512 * nh:512 * (nh + 1)],
                             start=(ic == 0), stop=(ic == IC - 1))
                Sb = t["Sb"][tau % 2]
                xb = t["xb"][tau % 2]
                P.dma("sync", Sb[:], d["S_in"][b, :, tau, :])
                P.dma("sync", xb[:], d["x"][b, 128 * tau:128 * (tau + 1), :])
                ub = t["ub"][tau % 2]
                P.stt(ub[:], psD[:], t["s2b"][:, 0:1], Sb[:], ALU.mult, ALU.add)
                P.tt(ub[:], ub[:], xb[:], ALU.add)
                P.tt(ub[:], ub[:], t["brep"][:], ALU.add)
                P.dma("pool", d["y"][b, 128 * tau:128 * (tau + 1), :], ub[:])
                if b + 1 < BPC and 1 <= tau <= HH:
                    emit_hquant(b + 1, tau - 1)
            if b + 1 < BPC:
                for ic in range(HH, IC):
                    emit_hquant(b + 1, ic)
        P.wait_all("pool")
        blk = ctx.enter_context(nc.Block())
        P.finalize(blk)
    return nc


def _host_tables():
    j = np.arange(512, dtype=np.float64)
    theta = 10000.0 ** (-j / 512.0)
    n = np.arange(N, dtype=np.float64)
    ang = n[:, None] * theta[None, :]
    cosT = np.cos(ang).astype(np.float32)
    sinT = np.sin(ang).astype(np.float32)
    return cosT, sinT, np.ascontiguousarray(cosT.T), np.ascontiguousarray(sinT.T)


def _host_masks():
    m = np.zeros((8, 128, 8), dtype=np.float32)
    for c in range(8):
        for p in range(128):
            r = 128 * c + p
            orig = 2 * r if r < 512 else 2 * (r - 512) + 1
            m[c, p, orig // 128] = 1.0
    return m


_PERM = np.concatenate([np.arange(0, 1024, 2), np.arange(1, 1024, 2)])


def kernel(L0_attn_x, L0_attn_qk_w, L0_attn_qk_b, L0_attn_lepe_w, L0_attn_lepe_b,
           L0_ffn_clip_x, L0_ffn_clip_down, L0_ffn_w_gate, L0_ffn_w_up, L0_ffn_w_down):
    x = np.ascontiguousarray(np.asarray(L0_attn_x, dtype=np.float32))
    qk_w = np.asarray(L0_attn_qk_w, dtype=np.float32)
    qk_b = np.asarray(L0_attn_qk_b, dtype=np.float32)
    wq = np.ascontiguousarray(qk_w[:, :C][:, _PERM])
    wk = np.ascontiguousarray(qk_w[:, C:][:, _PERM])
    bq_p = qk_b[:C][_PERM]
    bk_p = qk_b[C:][_PERM]
    bq_cols = np.ascontiguousarray(bq_p.reshape(KC, 128).T)
    bk_row = np.ascontiguousarray(bk_p.reshape(1, C))
    lw = np.ascontiguousarray(np.asarray(L0_attn_lepe_w, dtype=np.float32))
    lb = np.ascontiguousarray(np.asarray(L0_attn_lepe_b, dtype=np.float32).reshape(1, C))
    wg = np.ascontiguousarray(np.asarray(L0_ffn_w_gate, dtype=np.float32))
    wu = np.ascontiguousarray(np.asarray(L0_ffn_w_up, dtype=np.float32))
    wd = np.ascontiguousarray(np.asarray(L0_ffn_w_down, dtype=np.float32))
    clip_x = np.asarray(L0_ffn_clip_x, dtype=np.float32).reshape(1, 1)
    clip_d = np.asarray(L0_ffn_clip_down, dtype=np.float32).reshape(1, 1)
    cosT, sinT, cosF, sinF = _host_tables()
    eye8 = np.eye(8, dtype=np.float32)
    ident = np.eye(128, dtype=np.float32)
    mask = _host_masks()

    xs = [np.ascontiguousarray(x[BPC * c:BPC * (c + 1)]) for c in range(NCORES)]

    nc1 = build_l1()
    in1 = [dict(x=xs[c], wq=wq, wk=wk, bq=bq_cols, bk=bk_row, lw=lw,
                cosF=cosF, sinF=sinF, cosT=cosT, sinT=sinT,
                eye8=eye8, ident=ident, mask=mask) for c in range(NCORES)]
    r1 = run_bass_kernel_spmd(nc1, in1, core_ids=list(range(NCORES)))
    m1vec = np.concatenate([r1.results[c]["m1_out"].reshape(1, 1)
                            for c in range(NCORES)], axis=1)

    nc2 = build_l2()
    in2 = [dict(xn_in=r1.results[c]["xn_out"], m1vec=m1vec, clip_x=clip_x,
                wg=wg, wu=wu, ident=ident) for c in range(NCORES)]
    r2 = run_bass_kernel_spmd(nc2, in2, core_ids=list(range(NCORES)))
    m2vec = np.concatenate([r2.results[c]["m2_out"].reshape(1, 1)
                            for c in range(NCORES)], axis=1)

    nc3 = build_l3()
    in3 = [dict(h_in=r2.results[c]["h_out"], m2vec=m2vec, clip_d=clip_d,
                wd=wd, S_in=r1.results[c]["S_out"], x=xs[c], lb=lb)
           for c in range(NCORES)]
    r3 = run_bass_kernel_spmd(nc3, in3, core_ids=list(range(NCORES)))
    y = np.concatenate([np.asarray(r3.results[c]["y"]) for c in range(NCORES)], axis=0)
    return y.astype(np.float32)


# version 16
# speedup vs baseline: 1.1793x; 1.0136x over previous
import numpy as np
import ml_dtypes
from contextlib import ExitStack

import concourse.bass as bass
import concourse.mybir as mybir
from concourse.bass_utils import run_bass_kernel_spmd

F32 = mybir.dt.float32
BF16 = mybir.dt.bfloat16
AF = mybir.ActivationFunctionType
ALU = mybir.AluOpType
AX = mybir.AxisListType
BF = ml_dtypes.bfloat16

C_RND = 12582912.0
RSQRT_N = 0.022097086912079608
NCORES = 8
B, N, C = 32, 2048, 1024
BPC = B // NCORES
NT = N // 128
KC = C // 128
I = 2816
IC = I // 128
DEBUG_L1 = False
DEBUG_BARRIER = False
DIAG_TAB_BOUNCE = False
GUARD = 16
NF = GUARD + N + GUARD


class Prog:
    STRICT = False
    STRICT_ENGS = ()
    ENG = ("sync", "act", "pe", "dve", "pool")
    _BLK = {"sync": "sync", "act": "scalar", "pe": "tensor", "dve": "vector",
            "pool": "gpsimd"}
    NSLOT = {"sync": 12, "act": 8, "pe": 0, "dve": 0, "pool": 8}
    KEYS = tuple([(e, "c") for e in ("sync", "act", "pe", "dve", "pool")]
                 + [(e, "d", s) for e in ("sync", "act", "pool")
                    for s in range(12) if s < (12 if e == "sync" else 8)])

    def __init__(self, nc, ctx):
        self.nc = nc
        self.sems = {k: ctx.enter_context(
            nc.semaphore("sem_" + "_".join(str(p) for p in k)))
            for k in self.KEYS}
        self.cnt = {k: 0 for k in self.KEYS}
        self.known = {e: {k: 0 for k in self.KEYS} for e in self.ENG}
        self.dma_seq = {e: 0 for e in self.ENG}
        self.last_w = {}
        self.readers = {}
        self.ops = {e: [] for e in self.ENG}

    @staticmethod
    def _key(a):
        if a is None:
            return None
        if isinstance(a, str):
            return a
        t = getattr(a, "tensor", a)
        return id(t)

    def op(self, eng, emit, reads=(), writes=(), inc=1):
        if inc >= 16:
            key = (eng, "d", self.dma_seq[eng] % self.NSLOT[eng])
            self.dma_seq[eng] += 1
        else:
            key = (eng, "c")
        waits = {}
        if Prog.STRICT or eng in Prog.STRICT_ENGS:
            for k2 in self.KEYS:
                if self.cnt[k2] > self.known[eng].get(k2, 0):
                    waits[k2] = self.cnt[k2]

        def need(k2, c2):
            if k2 == key == ("pe", "c"):
                return
            if self.known[eng].get(k2, 0) < c2 and waits.get(k2, 0) < c2:
                waits[k2] = c2

        if inc >= 16 and self.cnt[key]:
            need(key, self.cnt[key])
        rk = [k for k in (self._key(a) for a in reads) if k is not None]
        wk = [k for k in (self._key(a) for a in writes) if k is not None]
        for k in rk:
            lw = self.last_w.get(k)
            if lw:
                need(*lw)
        for k in wk:
            lw = self.last_w.get(k)
            if lw:
                need(*lw)
            for k2, c2 in self.readers.get(k, {}).items():
                need(k2, c2)
        after = self.cnt[key] + inc
        self.cnt[key] = after
        for k2, c2 in waits.items():
            self.known[eng][k2] = max(self.known[eng][k2], c2)
        for k in wk:
            self.last_w[k] = (key, after)
            self.readers[k] = {}
        for k in rk:
            d = self.readers.setdefault(k, {})
            d[key] = after
        self.ops[eng].append((waits, emit, inc, key))

    def wait_all(self, eng):
        waits = {}
        for k2 in self.KEYS:
            if self.cnt[k2] > self.known[eng][k2]:
                waits[k2] = self.cnt[k2]
                self.known[eng][k2] = self.cnt[k2]
        if waits:
            self.ops[eng].append((waits, None, 0, None))

    def barrier(self):
        for e in self.ENG:
            self.wait_all(e)

    def dma(self, q, out, in_, extra_reads=(), **kw):
        self.op(q, lambda e: e.dma_start(out=out, in_=in_, **kw),
                reads=[in_, *extra_reads], writes=[out], inc=16)

    def dmat(self, q, out, in_):
        self.op(q, lambda e: e.dma_start(out=out, in_=in_, transpose=True),
                reads=[in_], writes=[out], inc=16)

    def mm(self, out, lhsT, rhs, start, stop):
        self.op("pe", lambda e: e.matmul(out, lhsT, rhs, start=start, stop=stop,
                                         skip_group_check=True),
                reads=[lhsT, rhs], writes=[out], inc=1)

    def act(self, out, in_, func, bias=0.0, scale=1.0, eng="act"):
        er = [a for a in (bias, scale) if hasattr(a, "tensor")]
        self.op(eng, lambda e: e.activation(out, in_, func, bias=bias, scale=scale),
                reads=[in_, *er], writes=[out])

    def ts(self, out, in0, s1, s2, op0, op1=ALU.bypass, eng="dve"):
        er = [a for a in (s1, s2) if hasattr(a, "tensor")]
        self.op(eng, lambda e: e.tensor_scalar(out, in0, s1, s2, op0, op1),
                reads=[in0, *er], writes=[out])

    def tt(self, out, a, b, alu, eng="dve"):
        self.op(eng, lambda e: e.tensor_tensor(out, a, b, alu),
                reads=[a, b], writes=[out])

    def stt(self, out, in0, s, in1, op0, op1, eng="dve"):
        er = [s] if hasattr(s, "tensor") else []
        self.op(eng, lambda e: e.scalar_tensor_tensor(out, in0, s, in1, op0, op1),
                reads=[in0, in1, *er], writes=[out])

    def red(self, out, in_, alu, absval=False, eng="dve"):
        self.op(eng, lambda e: e.tensor_reduce(out, in_, AX.X, alu,
                                               apply_absolute_value=absval),
                reads=[in_], writes=[out])

    def copy(self, out, in_, eng="dve"):
        self.op(eng, lambda e: e.tensor_copy(out, in_), reads=[in_], writes=[out])

    def recip(self, out, in_):
        self.op("dve", lambda e: e.reciprocal(out, in_), reads=[in_], writes=[out])

    def memset(self, ap, v, eng="dve"):
        self.op(eng, lambda e: e.memset(ap, v), reads=[], writes=[ap])

    def finalize(self, blk):
        for e in self.ENG:
            w = {k: self.cnt[k] for k in self.KEYS if k[0] == e and self.cnt[k]}
            if w:
                self.ops[e].append((w, None, 0, None))
        for e in self.ENG:
            def run(eng, _e=e):
                for waits, emit, inc, key in self.ops[_e]:
                    for k2 in self.KEYS:
                        if k2 in waits:
                            eng.wait_ge(self.sems[k2], waits[k2])
                    if emit is not None:
                        ins = emit(eng)
                        if inc:
                            ins.then_inc(self.sems[key], inc)
            getattr(blk, self._BLK[e])(run)


def _sb(nc, ctx, name, shape, dt):
    return ctx.enter_context(nc.sbuf_tensor(name, shape, dt))


def _ps(nc, ctx, name, shape, dt=F32):
    return ctx.enter_context(nc.psum_tensor(name, shape, dt))


def _colmax_to_scalar(P, t, col_f32, out11, psrow):
    P.op("pe", lambda e: e.transpose(psrow[0:1, 0:128], col_f32, t["ident_f"][:]),
         reads=[col_f32, t["ident_f"][:]], writes=[psrow])
    P.copy(t["m1row"][:], psrow[0:1, 0:128])
    P.red(out11, t["m1row"][:], ALU.max)


def _scalar_bcast(P, t, q, src11, dst_col, scratch_dram):
    P.dma("pool", scratch_dram, src11)
    P.dma(q, dst_col, scratch_dram.partition_broadcast(128))


def _emit_quant3(P, out_bf, in_ap, rcol, tmp_f32, tmp2_f32):
    P.ts(tmp_f32, in_ap, rcol, C_RND, ALU.mult, ALU.add)
    P.ts(tmp2_f32, tmp_f32, C_RND, 127.0, ALU.subtract, ALU.min)
    P.ts(out_bf, tmp2_f32, -128.0, None, ALU.max)


def build_l1():
    nc = bass.Bass("TRN2", target_bir_lowering=False)
    d = {}
    d["x"] = nc.dram_tensor("x", [BPC, N, C], F32, kind="ExternalInput").ap()
    d["wq"] = nc.dram_tensor("wq", [C, C], F32, kind="ExternalInput").ap()
    d["wk"] = nc.dram_tensor("wk", [C, C], F32, kind="ExternalInput").ap()
    d["bq"] = nc.dram_tensor("bq", [128, KC], F32, kind="ExternalInput").ap()
    d["bk"] = nc.dram_tensor("bk", [1, C], F32, kind="ExternalInput").ap()
    d["lw"] = nc.dram_tensor("lw", [3, C, C], F32, kind="ExternalInput").ap()
    d["cosF"] = nc.dram_tensor("cosF", [512, N], F32, kind="ExternalInput").ap()
    d["sinF"] = nc.dram_tensor("sinF", [512, N], F32, kind="ExternalInput").ap()
    d["cosT"] = nc.dram_tensor("cosT", [N, 512], F32, kind="ExternalInput").ap()
    d["sinT"] = nc.dram_tensor("sinT", [N, 512], F32, kind="ExternalInput").ap()
    d["eye8"] = nc.dram_tensor("eye8", [8, 8], F32, kind="ExternalInput").ap()
    d["ident"] = nc.dram_tensor("ident", [128, 128], F32, kind="ExternalInput").ap()
    d["mask"] = nc.dram_tensor("mask", [8, 128, 8], F32, kind="ExternalInput").ap()
    d["S_out"] = nc.dram_tensor("S_out", [BPC, 128, NT, C], BF16, kind="ExternalOutput").ap()
    d["xn_out"] = nc.dram_tensor("xn_out", [BPC, 128, NT, C], F32, kind="ExternalOutput").ap()
    d["m1_out"] = nc.dram_tensor("m1_out", [1, 1], F32, kind="ExternalOutput").ap()
    d["cosT_s"] = nc.dram_tensor("cosT_s", [N, 512], BF16, kind="Internal").ap()
    d["sinT_s"] = nc.dram_tensor("sinT_s", [N, 512], BF16, kind="Internal").ap()
    d["cosF_s"] = nc.dram_tensor("cosF_s", [512, N], BF16, kind="Internal").ap()
    d["sinF_s"] = nc.dram_tensor("sinF_s", [512, N], BF16, kind="Internal").ap()
    d["wq_s"] = nc.dram_tensor("wq_s", [128, KC, C], BF16, kind="Internal").ap()
    if DEBUG_L1:
        d["kv_dbg"] = nc.dram_tensor("kv_dbg", [128, C], F32, kind="ExternalOutput").ap()
        d["z_dbg"] = nc.dram_tensor("z_dbg", [128, NT * 8], F32, kind="ExternalOutput").ap()
        d["qr_dbg"] = nc.dram_tensor("qr_dbg", [128, KC, N], F32, kind="ExternalOutput").ap()
        d["q1_dbg"] = nc.dram_tensor("q1_dbg", [128, N], F32, kind="ExternalOutput").ap()
        d["kr_dbg"] = nc.dram_tensor("kr_dbg", [128, C], F32, kind="ExternalOutput").ap()
        d["kr2_dbg"] = nc.dram_tensor("kr2_dbg", [128, C], BF16, kind="ExternalOutput").ap()
        d["t1_dbg"] = nc.dram_tensor("t1_dbg", [128, C], F32, kind="ExternalOutput").ap()
        d["t2_dbg"] = nc.dram_tensor("t2_dbg", [128, C], F32, kind="ExternalOutput").ap()
        d["k1_dbg"] = nc.dram_tensor("k1_dbg", [128, C], F32, kind="ExternalOutput").ap()
        d["tb_dbg"] = nc.dram_tensor("tb_dbg", [128, 2, 512], F32, kind="ExternalOutput").ap()
        d["psk_dbg"] = nc.dram_tensor("psk_dbg", [128, C], F32, kind="ExternalOutput").ap()
        d["psk2_dbg"] = nc.dram_tensor("psk2_dbg", [128, C], F32, kind="ExternalOutput").ap()
        d["km_dbg"] = nc.dram_tensor("km_dbg", [128, KC], F32, kind="ExternalOutput").ap()
        d["xnfm_dbg"] = nc.dram_tensor("xnfm_dbg", [128, KC, NF], BF16, kind="ExternalOutput").ap()
        d["psq_dbg"] = nc.dram_tensor("psq_dbg", [128, C], F32, kind="ExternalOutput").ap()

    with ExitStack() as ctx:
        t = {}
        t["ident_f"] = _sb(nc, ctx, "ident_f", [128, 128], F32)
        t["mask_sb"] = _sb(nc, ctx, "mask_sb", [128, 8, 8], F32)
        t["bq_sb"] = _sb(nc, ctx, "bq_sb", [128, KC], F32)
        t["nbq_sb"] = _sb(nc, ctx, "nbq_sb", [128, KC], F32)
        t["bkb"] = _sb(nc, ctx, "bkb", [1, C], BF16)
        t["ones_row"] = _sb(nc, ctx, "ones_row", [1, 128], BF16)
        t["ones_col"] = _sb(nc, ctx, "ones_col", [128, 1], BF16)
        t["ones11"] = _sb(nc, ctx, "ones11", [1, 1], BF16)
        t["krhs"] = _sb(nc, ctx, "krhs", [128, KC, C], BF16)
        t["lw_sb"] = _sb(nc, ctx, "lw_sb", [128, 24, C], BF16)
        t["wq_buf"] = [_sb(nc, ctx, f"wq{i}", [128, KC, 128], BF16) for i in range(2)]
        t["tabF"] = _sb(nc, ctx, "tabF", [128, 2, 1024], BF16)
        t["tabT"] = [_sb(nc, ctx, f"tabT{i}", [128, 2, 512], BF16) for i in range(2)]
        t["tabT2"] = [_sb(nc, ctx, f"tabT2_{i}", [128, 2, 512], BF16) for i in range(2)] if DIAG_TAB_BOUNCE else None
        t["xn_ring"] = _sb(nc, ctx, "xn_ring", [128, 2, C], BF16)
        t["vh_ring"] = [_sb(nc, ctx, f"vh{i}", [128, C], BF16) for i in range(3)]
        t["qr"] = _sb(nc, ctx, "qr", [128, KC * N], BF16)
        t["xn_fm"] = _sb(nc, ctx, "xn_fm", [128, KC, NF], BF16)
        t["bst"] = [_sb(nc, ctx, f"bst{i}", [128, 12], F32) for i in range(2)]
        t["mv"] = [_sb(nc, ctx, f"mv{i}", [128, 2], F32) for i in range(2)]
        t["vtmp"] = [_sb(nc, ctx, f"vtmp{i}", [128, 1], F32) for i in range(2)]
        t["rstd"] = [_sb(nc, ctx, f"rstd{i}", [128, 1], F32) for i in range(2)]
        t["rstd2"] = [_sb(nc, ctx, f"rstd2{i}", [128, 1], F32) for i in range(2)]
        t["nbias"] = [_sb(nc, ctx, f"nbias{i}", [128, 1], F32) for i in range(2)]
        t["t1"] = [_sb(nc, ctx, f"t1_{i}", [128, C], BF16) for i in range(2)]
        t["t2"] = [_sb(nc, ctx, f"t2_{i}", [128, C], BF16) for i in range(2)]
        t["x_buf"] = [_sb(nc, ctx, f"xbuf{i}", [128, C], F32) for i in range(2)]
        t["k1"] = [_sb(nc, ctx, f"k1_{i}", [128, C], BF16) for i in range(3)]
        t["kr"] = [_sb(nc, ctx, f"kr_{i}", [128, C], BF16) for i in range(3)]
        t["kv_sb"] = _sb(nc, ctx, "kv_sb", [128, C], BF16)
        t["km_bf"] = _sb(nc, ctx, "km_bf", [1, C], BF16)
        t["km_cols"] = _sb(nc, ctx, "km_cols", [128, KC], F32)
        t["A_sb"] = _sb(nc, ctx, "A_sb", [128, KC, 8], BF16)
        t["q1"] = [_sb(nc, ctx, f"q1_{i}", [128, N], BF16) for i in range(2)]
        t["r1"] = [_sb(nc, ctx, f"r1_{i}", [128, C], BF16) for i in range(2)]
        t["r2k"] = [_sb(nc, ctx, f"r2k_{i}", [128, C], BF16) for i in range(2)]
        t["eye8_f"] = _sb(nc, ctx, "eye8_f", [8, 8], F32)
        t["z_tm"] = _sb(nc, ctx, "z_tm", [128, NT, 8], F32)
        t["zrow"] = _sb(nc, ctx, "zrow", [128, 64], F32)
        t["S_buf"] = [_sb(nc, ctx, f"Sb{i}", [128, C], BF16) for i in range(2)]
        t["psq_sav"] = None
        t["t5_dbg"] = None
        t["mx"] = _sb(nc, ctx, "mx", [128, BPC * NT], F32)
        t["mxf"] = _sb(nc, ctx, "mxf", [128, 1], F32)
        t["m1row"] = _sb(nc, ctx, "m1row", [1, 128], F32)
        t["m1v"] = _sb(nc, ctx, "m1v", [1, 1], F32)
        P0 = _ps(nc, ctx, "P0", [128, 1024])
        P1 = _ps(nc, ctx, "P1", [128, 1024])
        P2 = _ps(nc, ctx, "P2", [128, 1024])
        P3 = _ps(nc, ctx, "P3", [128, 1024])

        P = Prog(nc, ctx)
        P.dma("sync", t["ident_f"][:], d["ident"])
        P.dma("sync", t["eye8_f"][:], d["eye8"])
        P.dma("sync", t["mask_sb"][:], d["mask"].rearrange("c p h -> p c h"))
        P.dma("sync", t["bq_sb"][:], d["bq"])
        P.ts(t["nbq_sb"][:], t["bq_sb"][:], -1.0, None, ALU.mult)
        P.dma("pool", t["bkb"][:], d["bk"])
        P.memset(t["ones_row"][:], 1.0)
        P.memset(t["ones_col"][:], 1.0)
        P.memset(t["ones11"][:], 1.0)
        P.op("dve", lambda e: e.memset(t["xn_fm"][:], 0.0),
             reads=[], writes=[f"xnfm{i}" for i in range(NT)])
        P.dma("pool", t["krhs"][:], d["wk"].rearrange("(kc p) n -> p kc n", p=128))
        P.dma("pool", t["lw_sb"][:], d["lw"].rearrange("t (kc p) m -> p (t kc) m", p=128))
        P.dma("pool", d["cosT_s"], d["cosT"])
        P.dma("pool", d["sinT_s"], d["sinT"])
        P.dma("pool", d["cosF_s"], d["cosF"])
        P.dma("pool", d["sinF_s"], d["sinF"])
        P.dma("pool", d["wq_s"], d["wq"].rearrange("(kc p) m -> p kc m", p=128))

        qr = t["qr"][:].rearrange("p (c n) -> p c n", c=KC)

        XK = [f"xnfm{i}" for i in range(NT)]

        def emit_ln(b, tau):
            xb = t["x_buf"][tau % 2]
            P.dma("pool", xb[:], d["x"][b, 128 * tau:128 * (tau + 1), :])
            bst = t["bst"][tau % 2]
            mv = t["mv"][tau % 2]
            vtmp = t["vtmp"][tau % 2]
            rstd = t["rstd"][tau % 2]
            rstd2 = t["rstd2"][tau % 2]
            nbias = t["nbias"][tau % 2]
            for g in range(2):
                P.op("dve", lambda e, o=bst[:, 6 * g:6 * (g + 1)],
                     i=xb[:, 512 * g:512 * (g + 1)]: e.bn_stats(o, i),
                     reads=[xb[:]], writes=[bst[:]])
            P.op("dve", lambda e, o=mv[:], i=bst[:]: e.bn_aggr(o, i),
                 reads=[bst[:]], writes=[mv[:]])
            P.ts(vtmp[:], mv[:, 1:2], 1e-5, None, ALU.add)
            P.act(rstd[:], vtmp[:], AF.Ln)
            P.act(rstd2[:], rstd[:], AF.Exp, scale=-0.5)
            P.stt(nbias[:], mv[:, 0:1], -1.0, rstd2[:, 0:1], ALU.mult, ALU.mult)
            P.act(xb[:], xb[:], AF.Identity, bias=nbias[:, 0:1],
                  scale=rstd2[:, 0:1])
            P.copy(t["xn_ring"][:, tau % 2, :], xb[:])
            P.dma("pool", d["xn_out"][b, :, tau, :], xb[:])
            P.red(t["mx"][:, b * NT + tau: b * NT + tau + 1], xb[:],
                  ALU.max, absval=True)
            out = t["xn_fm"][:, :, GUARD + 128 * tau: GUARD + 128 * (tau + 1)]
            srcv = t["xn_ring"][:, tau % 2, :].rearrange("p (c n) -> p c n", c=KC)
            P.op("sync", lambda e, o=out, s=srcv: e.dma_start(out=o, in_=s, transpose=True),
                 reads=[srcv], writes=[XK[tau]], inc=16)

        for b in range(BPC):
            def emit_kv(tau):
                k1 = t["k1"][tau % 3]
                kr = t["kr"][tau % 3]
                for nh in range(2):
                    P.mm(P3[0:1, 512 * nh:512 * (nh + 1)], t["ones_col"][:],
                         k1[:, 512 * nh:512 * (nh + 1)],
                         start=(tau == 0), stop=(tau == NT - 1))
                for h in range(8):
                    P.mm(P2[:, 128 * h:128 * (h + 1)],
                         kr[:, 128 * h:128 * (h + 1)],
                         t["vh_ring"][tau % 3][:, 128 * h:128 * (h + 1)],
                         start=(tau == 0 and h % 4 == 0), stop=(tau == NT - 1))

            if b == 0:
                for tau in range(NT):
                    emit_ln(0, tau)
            for tau in range(NT):
                for k in range(KC):
                    vout = t["vh_ring"][tau % 3][:, 128 * k:128 * (k + 1)]
                    vin = t["xn_fm"][:, k, GUARD + 128 * tau: GUARD + 128 * (tau + 1)]
                    P.op("sync", lambda e, o=vout, s=vin:
                         e.dma_start(out=o, in_=s, transpose=True),
                         reads=[XK[tau]], writes=[t["vh_ring"][tau % 3][:]], inc=16)
                tb = t["tabT"][tau % 2]
                P.dma("act", tb[:, 0, :], d["cosT_s"][128 * tau:128 * (tau + 1), :])
                P.dma("act", tb[:, 1, :], d["sinT_s"][128 * tau:128 * (tau + 1), :])
                psK = (P0, P1)[tau % 2]
                for k in range(KC):
                    lhsT = t["xn_fm"][:, k, GUARD + 128 * tau: GUARD + 128 * (tau + 1)]
                    for nh in range(2):
                        P.op("pe", lambda e, o=psK[:, 512 * nh:512 * (nh + 1)], l=lhsT,
                             r=t["krhs"][:, k, 512 * nh:512 * (nh + 1)],
                             st=(k == 0): e.matmul(o, l, r, start=st, stop=False,
                                                   skip_group_check=True),
                             reads=[XK[tau], t["krhs"][:]], writes=[psK[:]], inc=1)
                for nh in range(2):
                    P.mm(psK[:, 512 * nh:512 * (nh + 1)], t["ones_row"][:],
                         t["bkb"][0:1, 512 * nh:512 * (nh + 1)], start=False, stop=True)
                if tau > 1:
                    emit_kv(tau - 2)
                k1 = t["k1"][tau % 3]
                t1 = t["t1"][tau % 2]
                t2 = t["t2"][tau % 2]
                P.act(t2[:], psK[:], AF.Relu, scale=-1.0)
                P.act(t1[:], t2[:], AF.Exp, scale=-1.0)
                P.act(t2[:], psK[:], AF.Relu)
                P.tt(k1[:], t1[:], t2[:], ALU.add)
                kr = t["kr"][tau % 3]
                cosT_, sinT_ = tb[:, 0, :], tb[:, 1, :]
                r1 = t["r1"][tau % 2]
                r2k = t["r2k"][tau % 2]
                P.tt(r1[:, 0:512], cosT_, k1[:, 0:512], ALU.mult)
                P.tt(r1[:, 512:1024], sinT_, k1[:, 512:1024], ALU.mult)
                P.tt(kr[:, 0:512], r1[:, 0:512], r1[:, 512:1024],
                     ALU.subtract)
                P.tt(r2k[:, 0:512], cosT_, k1[:, 512:1024], ALU.mult)
                P.tt(r2k[:, 512:1024], sinT_, k1[:, 0:512], ALU.mult)
                P.tt(kr[:, 512:1024], r2k[:, 0:512], r2k[:, 512:1024],
                     ALU.add)
                if DEBUG_L1 and b == 0 and tau == 0:
                    P.dma("pool", d["kr_dbg"], kr[:])
                    P.dma("act", d["kr2_dbg"], kr[:])
                    P.dma("pool", d["t1_dbg"], t["t1"][tau % 2][:])
                    P.dma("pool", d["t2_dbg"], t["t2"][tau % 2][:])
                    P.dma("pool", d["k1_dbg"], k1[:])
                    P.dma("pool", d["tb_dbg"], tb[:])
            emit_kv(NT - 2)
            emit_kv(NT - 1)
            P.ts(t["kv_sb"][:], P2[:], RSQRT_N, None, ALU.mult)
            P.ts(t["km_bf"][:], P3[0:1, :], 1.0 / N, None, ALU.mult)
            for c in range(KC):
                P.mm(P2[:, c:c + 1], t["km_bf"][0:1, 128 * c:128 * (c + 1)],
                     t["ones11"][:], start=(c == 0), stop=(c == KC - 1))
            P.copy(t["km_cols"][:], P2[:, 0:KC])
            for c in range(KC):
                P.ts(t["A_sb"][:, c, :], t["mask_sb"][:, c, :],
                     t["km_cols"][:, c:c + 1], None, ALU.mult)

            def emit_zmm(m, zi):
                q1 = t["q1"][zi % 2]
                for th in range(2):
                    psZ = (P2, P3)[th]
                    for ns in range(2):
                        lo = 1024 * th + 512 * ns
                        P.mm(psZ[0:8, 512 * ns:512 * (ns + 1)], t["A_sb"][:, m, :],
                             q1[:, lo:lo + 512], start=(zi == 0), stop=(zi == 7))

            pair_done = {}
            order = [0, 4, 1, 5, 2, 6, 3, 7]
            for mi, m in enumerate(order):
                wqb = t["wq_buf"][mi % 2]
                P.dma("sync", wqb[:], d["wq_s"][:, :, 128 * m:128 * (m + 1)])
                q1 = t["q1"][mi % 2]
                for th in range(2):
                    psQ = (P0, P1)[th]
                    for k in range(KC):
                        for ns in range(2):
                            lo = 1024 * th + 512 * ns
                            seg = 2 * th + ns
                            P.op("pe", lambda e, o=psQ[:, 512 * ns:512 * (ns + 1)],
                                 l=wqb[:, k, :],
                                 r=t["xn_fm"][:, k, GUARD + lo: GUARD + lo + 512],
                                 st=(k == 0), sp=(k == KC - 1):
                                 e.matmul(o, l, r, start=st, stop=sp,
                                          skip_group_check=True),
                                 reads=[wqb[:], *(XK[4 * seg + j] for j in range(4))],
                                 writes=[psQ[:]], inc=1)
                    if th == 1 and mi > 0:
                        emit_zmm(order[mi - 1], mi - 1)
                    sl_q = slice(1024 * th, 1024 * (th + 1))
                    t1 = t["t1"][th]
                    t2 = t["t2"][th]
                    P.act(t2[:], psQ[:], AF.Relu, scale=-1.0,
                          bias=t["nbq_sb"][:, m:m + 1])
                    P.act(t1[:], t2[:], AF.Exp, scale=-1.0)
                    P.act(t2[:], psQ[:], AF.Relu, bias=t["bq_sb"][:, m:m + 1])
                    P.tt(q1[:, sl_q], t2[:], t1[:], ALU.add)
                if DEBUG_L1 and b == 0 and m == 0:
                    P.dma("pool", d["q1_dbg"], q1[:])
                pair_done[m] = q1
                cpair = m if m < 4 else m - 4
                if cpair in pair_done and cpair + 4 in pair_done:
                    qE, qO = pair_done[cpair], pair_done[cpair + 4]
                    for th in range(2):
                        sl = slice(1024 * th, 1024 * (th + 1))
                        tf = t["tabF"]
                        r1 = t["r1"][th]
                        r2k = t["r2k"][th]
                        P.dma("act", tf[:, 0, :],
                              d["cosF_s"][128 * cpair:128 * (cpair + 1), sl])
                        P.dma("act", tf[:, 1, :],
                              d["sinF_s"][128 * cpair:128 * (cpair + 1), sl])
                        cosF_, sinF_ = tf[:, 0, :], tf[:, 1, :]
                        P.tt(r1[:], cosF_, qE[:, sl], ALU.mult)
                        P.tt(r2k[:], sinF_, qO[:, sl], ALU.mult)
                        P.tt(qr[:, cpair, sl], r1[:], r2k[:], ALU.subtract)
                        P.tt(r1[:, 0:512], cosF_[:, 0:512], qO[:, sl][:, 0:512], ALU.mult)
                        P.tt(r1[:, 512:1024], sinF_[:, 0:512], qE[:, sl][:, 0:512], ALU.mult)
                        P.tt(qr[:, cpair + 4, sl][:, 0:512], r1[:, 0:512],
                             r1[:, 512:1024], ALU.add)
                        P.tt(r2k[:, 0:512], cosF_[:, 512:1024], qO[:, sl][:, 512:1024], ALU.mult)
                        P.tt(r2k[:, 512:1024], sinF_[:, 512:1024], qE[:, sl][:, 512:1024], ALU.mult)
                        P.tt(qr[:, cpair + 4, sl][:, 512:1024], r2k[:, 0:512],
                             r2k[:, 512:1024], ALU.add)
                    del pair_done[cpair], pair_done[cpair + 4]
            emit_zmm(order[7], 7)
            for th in range(2):
                psZ = (P2, P3)[th]
                P.ts(t["x_buf"][0][0:8, 0:1024], psZ[0:8, :], 1e-6, None, ALU.add)
                for tt8 in range(8):
                    P.mm(P0[:, 8 * tt8:8 * (tt8 + 1)],
                         t["x_buf"][0][0:8, 128 * tt8:128 * (tt8 + 1)], t["eye8_f"][:],
                         start=(tt8 == 0), stop=True)
                P.copy(t["zrow"][:], P0[:, 0:64])
                P.recip(t["z_tm"][:, 8 * th:8 * (th + 1), :],
                        t["zrow"][:].rearrange("p (a e) -> p a e", a=8))

            for tau in range(NT):
                psA = (P0, P1)[tau % 2]
                psL = (P2, P3)[tau % 2]
                for h in range(8):
                    P.mm(psA[:, 128 * h:128 * (h + 1)],
                         qr[:, h, 128 * tau:128 * (tau + 1)],
                         t["kv_sb"][:, 128 * h:128 * (h + 1)],
                         start=(h % 4 == 0), stop=True)
                nmm = 0
                for tap in range(3):
                    for k in range(KC):
                        lhsT = t["xn_fm"][:, k,
                                          GUARD + 128 * tau + tap - 1:
                                          GUARD + 128 * tau + tap - 1 + 128]
                        rkeys = [XK[j] for j in (tau - 1, tau, tau + 1)
                                 if 0 <= j < NT]
                        for nh in range(2):
                            P.op("pe", lambda e, o=psL[:, 512 * nh:512 * (nh + 1)],
                                 l=lhsT,
                                 r=t["lw_sb"][:, 8 * tap + k, 512 * nh:512 * (nh + 1)],
                                 st=(nmm == 0), sp=(nmm == 23):
                                 e.matmul(o, l, r, start=st, stop=sp,
                                          skip_group_check=True),
                                 reads=[t["lw_sb"][:], *rkeys], writes=[psL[:]], inc=1)
                        nmm += 1
                zb = t["z_tm"][:, tau, :].unsqueeze(2).broadcast_to([128, 8, 128])
                Sb = t["S_buf"][tau % 2]
                P.tt(Sb[:].rearrange("p (h n) -> p h n", h=8),
                     psA[:].rearrange("p (h n) -> p h n", h=8), zb, ALU.mult)
                P.tt(Sb[:], Sb[:], psL[:], ALU.add)
                P.dma("act", d["S_out"][b, :, tau, :], Sb[:])
                if tau >= 1 and b + 1 < BPC:
                    emit_ln(b + 1, tau - 1)
            if b + 1 < BPC:
                emit_ln(b + 1, NT - 1)
            if DEBUG_L1 and b == 0:
                P.op("pool", lambda e: e.dma_start(out=d["xnfm_dbg"], in_=t["xn_fm"][:]),
                     reads=list(XK), writes=[d["xnfm_dbg"]], inc=16)
                P.dma("pool", d["kv_dbg"], t["kv_sb"][:])
                P.dma("pool", d["z_dbg"], t["z_tm"][:].rearrange("p a e -> p (a e)"))
                P.dma("pool", d["qr_dbg"], qr)
                P.dma("pool", d["km_dbg"], t["km_cols"][:])

        P.red(t["mxf"][:], t["mx"][:], ALU.max)
        _colmax_to_scalar(P, t, t["mxf"][:], t["m1v"][:], P0)
        P.dma("pool", d["m1_out"], t["m1v"][:])
        P.wait_all("pool")

        blk = ctx.enter_context(nc.Block())
        P.finalize(blk)
    return nc


def build_l2():
    nc = bass.Bass("TRN2", target_bir_lowering=False)
    d = {}
    d["xn_in"] = nc.dram_tensor("xn_in", [BPC, 128, NT, C], F32, kind="ExternalInput").ap()
    d["m1vec"] = nc.dram_tensor("m1vec", [1, NCORES], F32, kind="ExternalInput").ap()
    d["clip_x"] = nc.dram_tensor("clip_x", [1, 1], F32, kind="ExternalInput").ap()
    d["wg"] = nc.dram_tensor("wg", [I, C], F32, kind="ExternalInput").ap()
    d["wu"] = nc.dram_tensor("wu", [I, C], F32, kind="ExternalInput").ap()
    d["ident"] = nc.dram_tensor("ident", [128, 128], F32, kind="ExternalInput").ap()
    d["h_out"] = nc.dram_tensor("h_out", [BPC, 128, IC, N], F32, kind="ExternalOutput").ap()
    d["m2_out"] = nc.dram_tensor("m2_out", [1, 1], F32, kind="ExternalOutput").ap()
    scr = nc.dram_tensor("scr11", [1, 1], F32, kind="Internal").ap()

    with ExitStack() as ctx:
        t = {}
        t["ident_f"] = _sb(nc, ctx, "ident_f", [128, 128], F32)
        t["wgT"] = _sb(nc, ctx, "wgT", [128, KC, I], BF16)
        t["wuT"] = _sb(nc, ctx, "wuT", [128, KC, I], BF16)
        t["rmax"] = _sb(nc, ctx, "rmax", [128, 1], F32)
        t["rq"] = _sb(nc, ctx, "rq", [128, 1], F32)
        t["swg"] = _sb(nc, ctx, "swg", [128, IC], F32)
        t["swu"] = _sb(nc, ctx, "swu", [128, IC], F32)
        t["sgc"] = _sb(nc, ctx, "sgc", [128, IC], F32)
        t["suc"] = _sb(nc, ctx, "suc", [128, IC], F32)
        t["m1s"] = _sb(nc, ctx, "m1s", [1, NCORES], F32)
        t["m1"] = _sb(nc, ctx, "m1", [1, 1], F32)
        t["sig"] = _sb(nc, ctx, "sig", [1, 1], F32)
        t["s1"] = _sb(nc, ctx, "s1", [1, 1], F32)
        t["r1"] = _sb(nc, ctx, "r1", [1, 1], F32)
        t["r1b"] = _sb(nc, ctx, "r1b", [128, 1], F32)
        t["s1b"] = _sb(nc, ctx, "s1b", [128, 1], F32)
        t["hm"] = _sb(nc, ctx, "hm", [128, 1], F32)
        t["hmx"] = _sb(nc, ctx, "hmx", [128, 1], F32)
        t["m1row"] = _sb(nc, ctx, "m1row", [1, 128], F32)
        t["m2v"] = _sb(nc, ctx, "m2v", [1, 1], F32)
        P0 = _ps(nc, ctx, "P0", [128, 512])
        P1 = _ps(nc, ctx, "P1", [128, 512])
        P2 = _ps(nc, ctx, "P2", [128, 512])
        P3 = _ps(nc, ctx, "P3", [128, 512])

        P = Prog(nc, ctx)
        P.dma("sync", t["ident_f"][:], d["ident"])
        P.dma("sync", t["m1s"][:], d["m1vec"])
        P.red(t["m1"][:], t["m1s"][:], ALU.max)
        P.dma("sync", t["sig"][:], d["clip_x"])
        P.act(t["sig"][:], t["sig"][:], AF.Sigmoid)
        P.tt(t["s1"][:], t["sig"][:], t["m1"][:], ALU.mult)
        P.ts(t["s1"][:], t["s1"][:], 1.0 / 127.0, None, ALU.mult)
        P.recip(t["r1"][:], t["s1"][:])
        _scalar_bcast(P, t, "sync", t["s1"][:], t["s1b"][:], scr)
        _scalar_bcast(P, t, "sync", t["r1"][:], t["r1b"][:], scr)
        with ExitStack() as c2:
            wnat = _sb(nc, c2, "wnat", [128, KC, IC, 128], BF16)
            wrow = [_sb(nc, c2, f"wrow{i}", [128, C], F32) for i in range(2)]
            qt = _sb(nc, c2, "qt", [128, C], F32)
            for wsrc, wT, sw in ((d["wg"], t["wgT"], t["swg"]), (d["wu"], t["wuT"], t["swu"])):
                for cc in range(IC):
                    wr = wrow[cc % 2]
                    P.dma("sync", wr[:], wsrc.rearrange("(cc p) m -> p cc m", p=128)[:, cc, :])
                    P.red(t["rmax"][:], wr[:], ALU.max, absval=True)
                    P.recip(t["rq"][:], t["rmax"][:])
                    P.ts(t["rq"][:], t["rq"][:], 127.0, None, ALU.mult)
                    P.ts(sw[:, cc:cc + 1], t["rmax"][:], 1.0 / 127.0, None, ALU.mult)
                    P.ts(qt[:], wr[:], t["rq"][:, 0:1], C_RND, ALU.mult, ALU.add)
                    P.ts(wnat[:, :, cc, :], qt[:].rearrange("p (h n) -> p h n", h=KC),
                         C_RND, None, ALU.subtract)
                for hc in range(KC):
                    P.dmat("act", wT[:, hc, :].rearrange("p (i n) -> p i n", i=IC),
                           wnat[:, hc, :, :])
            P.barrier()
        t["xq"] = [_sb(nc, ctx, f"xq{i}", [128, KC, N], BF16) for i in range(2)]
        t["xnb"] = [_sb(nc, ctx, f"xnb{i}", [128, C], F32) for i in range(2)]
        t["xqt"] = [_sb(nc, ctx, f"xqt{i}", [128, C], BF16) for i in range(2)]
        t["qtmp"] = _sb(nc, ctx, "qtmp", [128, C], F32)
        t["qtmp2"] = _sb(nc, ctx, "qtmp2", [128, C], F32)
        t["sg_buf"] = _sb(nc, ctx, "sg_buf", [128, 512], F32)
        t["up_buf"] = _sb(nc, ctx, "up_buf", [128, 512], F32)
        t["h_buf"] = [_sb(nc, ctx, f"hb{i}", [128, 512], F32) for i in range(2)]
        P.tt(t["sgc"][:], t["swg"][:], t["s1b"][:, 0:1].broadcast_to([128, IC]), ALU.mult)
        P.tt(t["suc"][:], t["swu"][:], t["s1b"][:, 0:1].broadcast_to([128, IC]), ALU.mult)
        P.memset(t["hmx"][:], 0.0)

        def emit_quant_tau(g, j):
            gb, gs = divmod(g, 4)
            tau = 4 * gs + j
            xnb = t["xnb"][tau % 2]
            xqt = t["xqt"][tau % 2]
            P.dma("sync", xnb[:], d["xn_in"][gb, :, tau, :])
            _emit_quant3(P, xqt[:], xnb[:], t["r1b"][:, 0:1], t["qtmp"][:], t["qtmp2"][:])
            out = t["xq"][gb % 2][:, :, 128 * tau:128 * (tau + 1)]
            src = xqt[:].rearrange("p (c n) -> p c n", c=KC)
            P.op("act", lambda e, o=out, s=src: e.dma_start(out=o, in_=s, transpose=True),
                 reads=[src], writes=[f"xq{gb % 2}s{gs}"], inc=16)

        for j in range(4):
            emit_quant_tau(0, j)
        SLOTS = (4, 9, 14, 19)
        for b in range(BPC):
            for s in range(4):
                g = 4 * b + s
                lo = 512 * s
                xqb = t["xq"][b % 2]
                xqkey = f"xq{b % 2}s{s}"
                for m in range(IC):
                    psG = (P0, P1)[m % 2]
                    psU = (P2, P3)[m % 2]
                    for k in range(KC):
                        P.op("pe", lambda e, o=psG[:], l=t["wgT"][:, k, 128 * m:128 * (m + 1)],
                             r=xqb[:, k, lo:lo + 512], st=(k == 0), sp=(k == KC - 1):
                             e.matmul(o, l, r, start=st, stop=sp, skip_group_check=True),
                             reads=[t["wgT"][:, k, 128 * m:128 * (m + 1)], xqkey],
                             writes=[psG[:]], inc=1)
                        P.op("pe", lambda e, o=psU[:], l=t["wuT"][:, k, 128 * m:128 * (m + 1)],
                             r=xqb[:, k, lo:lo + 512], st=(k == 0), sp=(k == KC - 1):
                             e.matmul(o, l, r, start=st, stop=sp, skip_group_check=True),
                             reads=[t["wuT"][:, k, 128 * m:128 * (m + 1)], xqkey],
                             writes=[psU[:]], inc=1)
                    P.act(t["sg_buf"][:], psG[:], AF.Silu, scale=t["sgc"][:, m:m + 1])
                    P.ts(t["up_buf"][:], psU[:], t["suc"][:, m:m + 1], None, ALU.mult)
                    hb = t["h_buf"][m % 2]
                    P.tt(hb[:], t["sg_buf"][:], t["up_buf"][:], ALU.mult)
                    P.red(t["hm"][:], hb[:], ALU.max, absval=True)
                    P.tt(t["hmx"][:], t["hmx"][:], t["hm"][:], ALU.max)
                    P.dma("pool", d["h_out"][b, :, m, lo:lo + 512], hb[:])
                    if m in SLOTS and g + 1 <= 4 * BPC - 1:
                        emit_quant_tau(g + 1, SLOTS.index(m))

        _colmax_to_scalar(P, t, t["hmx"][:], t["m2v"][:], P0)
        P.dma("pool", d["m2_out"], t["m2v"][:])
        P.wait_all("pool")
        blk = ctx.enter_context(nc.Block())
        P.finalize(blk)
    return nc


def build_l3():
    nc = bass.Bass("TRN2", target_bir_lowering=False)
    d = {}
    d["h_in"] = nc.dram_tensor("h_in", [BPC, 128, IC, N], F32, kind="ExternalInput").ap()
    d["m2vec"] = nc.dram_tensor("m2vec", [1, NCORES], F32, kind="ExternalInput").ap()
    d["clip_d"] = nc.dram_tensor("clip_d", [1, 1], F32, kind="ExternalInput").ap()
    d["wd"] = nc.dram_tensor("wd", [C, I], F32, kind="ExternalInput").ap()
    d["S_in"] = nc.dram_tensor("S_in", [BPC, 128, NT, C], BF16, kind="ExternalInput").ap()
    d["x"] = nc.dram_tensor("x", [BPC, N, C], F32, kind="ExternalInput").ap()
    d["lb"] = nc.dram_tensor("lb", [1, C], F32, kind="ExternalInput").ap()
    d["y"] = nc.dram_tensor("y", [BPC, N, C], F32, kind="ExternalOutput").ap()
    scr = nc.dram_tensor("scr11", [1, 1], F32, kind="Internal").ap()

    with ExitStack() as ctx:
        t = {}
        t["wdT"] = _sb(nc, ctx, "wdT", [128, IC, C], BF16)
        t["rmax"] = _sb(nc, ctx, "rmax", [128, 1], F32)
        t["rq"] = _sb(nc, ctx, "rq", [128, 1], F32)
        t["sw"] = _sb(nc, ctx, "sw", [128, 1], F32)
        t["m2s"] = _sb(nc, ctx, "m2s", [1, NCORES], F32)
        t["m2"] = _sb(nc, ctx, "m2", [1, 1], F32)
        t["sig"] = _sb(nc, ctx, "sig", [1, 1], F32)
        t["s2"] = _sb(nc, ctx, "s2", [1, 1], F32)
        t["r2"] = _sb(nc, ctx, "r2", [1, 1], F32)
        t["r2b"] = _sb(nc, ctx, "r2b", [128, 1], F32)
        t["s2b"] = _sb(nc, ctx, "s2b", [128, 1], F32)
        t["brep"] = _sb(nc, ctx, "brep", [128, C], BF16)
        t["hb"] = [_sb(nc, ctx, f"hb{i}", [128, C], F32) for i in range(2)]
        t["Sb"] = [_sb(nc, ctx, f"Sb{i}", [128, C], BF16) for i in range(2)]
        t["xb"] = [_sb(nc, ctx, f"xb{i}", [128, C], F32) for i in range(2)]
        t["ub"] = [_sb(nc, ctx, f"ub{i}", [128, C], F32) for i in range(2)]
        P0 = _ps(nc, ctx, "P0", [128, 1024])
        P1 = _ps(nc, ctx, "P1", [128, 1024])

        P = Prog(nc, ctx)
        P.dma("sync", t["m2s"][:], d["m2vec"])
        P.red(t["m2"][:], t["m2s"][:], ALU.max)
        P.dma("sync", t["sig"][:], d["clip_d"])
        P.act(t["sig"][:], t["sig"][:], AF.Sigmoid)
        P.tt(t["s2"][:], t["sig"][:], t["m2"][:], ALU.mult)
        P.ts(t["s2"][:], t["s2"][:], 1.0 / 127.0, None, ALU.mult)
        P.recip(t["r2"][:], t["s2"][:])
        _scalar_bcast(P, t, "sync", t["s2"][:], t["s2b"][:], scr)
        _scalar_bcast(P, t, "sync", t["r2"][:], t["r2b"][:], scr)
        P.dma("pool", t["brep"][:], d["lb"].partition_broadcast(128))
        with ExitStack() as c2:
            wdnat = _sb(nc, c2, "wdnat", [128, IC, KC, 128], BF16)
            wrow = [_sb(nc, c2, f"wrow{i}", [128, I], F32) for i in range(2)]
            qt = _sb(nc, c2, "qt", [128, I], F32)
            qt2 = _sb(nc, c2, "qt2", [128, I], F32)
            for hc in range(KC):
                wr = wrow[hc % 2]
                P.dma("sync", wr[:], d["wd"].rearrange("(hc p) m -> p hc m", p=128)[:, hc, :])
                P.red(t["rmax"][:], wr[:], ALU.max, absval=True)
                P.recip(t["rq"][:], t["rmax"][:])
                P.ts(t["rq"][:], t["rq"][:], 127.0, None, ALU.mult)
                P.ts(t["sw"][:], t["rmax"][:], 1.0 / 127.0, None, ALU.mult)
                P.ts(qt[:], wr[:], t["rq"][:, 0:1], C_RND, ALU.mult, ALU.add)
                P.stt(wdnat[:, :, hc, :], qt[:].rearrange("p (i n) -> p i n", i=IC),
                      C_RND, t["sw"][:, 0:1].unsqueeze(1).broadcast_to([128, IC, 128]),
                      ALU.subtract, ALU.mult)
            for ic in range(IC):
                P.dmat("act", t["wdT"][:, ic, :].rearrange("p (h n) -> p h n", h=KC),
                       wdnat[:, ic, :, :])
            P.barrier()
        HH = (IC + 1) // 2
        t["hq"] = [_sb(nc, ctx, f"hq{i}", [128, HH, N], BF16) for i in range(3)]
        t["q2b"] = _sb(nc, ctx, "q2b", [128, 512], BF16)

        def emit_hquant(b, ic):
            half, j = divmod(ic, HH)
            slot = (2 * b + half) % 3
            for g in range(2):
                hb = t["hb"][g]
                P.dma("sync", hb[:], d["h_in"][b, :, ic, 1024 * g:1024 * (g + 1)])
                P.ts(hb[:], hb[:], t["r2b"][:, 0:1], C_RND, ALU.mult, ALU.add)
                for gg in range(2):
                    P.ts(t["q2b"][:], hb[:, 512 * gg:512 * (gg + 1)],
                         C_RND, 127.0, ALU.subtract, ALU.min)
                    P.ts(t["hq"][slot][:, j, 1024 * g + 512 * gg:
                                       1024 * g + 512 * (gg + 1)],
                         t["q2b"][:], -128.0, None, ALU.max)

        for ic in range(IC):
            emit_hquant(0, ic)
        for b in range(BPC):
            for tau in range(NT):
                psD = (P0, P1)[tau % 2]
                for ic in range(IC):
                    half, j = divmod(ic, HH)
                    slot = (2 * b + half) % 3
                    for nh in range(2):
                        P.mm(psD[:, 512 * nh:512 * (nh + 1)],
                             t["hq"][slot][:, j, 128 * tau:128 * (tau + 1)],
                             t["wdT"][:, ic, 512 * nh:512 * (nh + 1)],
                             start=(ic == 0), stop=(ic == IC - 1))
                Sb = t["Sb"][tau % 2]
                xb = t["xb"][tau % 2]
                P.dma("sync", Sb[:], d["S_in"][b, :, tau, :])
                P.dma("sync", xb[:], d["x"][b, 128 * tau:128 * (tau + 1), :])
                ub = t["ub"][tau % 2]
                P.stt(ub[:], psD[:], t["s2b"][:, 0:1], Sb[:], ALU.mult, ALU.add)
                P.tt(ub[:], ub[:], xb[:], ALU.add)
                P.tt(ub[:], ub[:], t["brep"][:], ALU.add)
                P.dma("pool", d["y"][b, 128 * tau:128 * (tau + 1), :], ub[:])
                if b + 1 < BPC and 1 <= tau <= HH:
                    emit_hquant(b + 1, tau - 1)
            if b + 1 < BPC:
                for ic in range(HH, IC):
                    emit_hquant(b + 1, ic)
        P.wait_all("pool")
        blk = ctx.enter_context(nc.Block())
        P.finalize(blk)
    return nc


def _host_tables():
    j = np.arange(512, dtype=np.float64)
    theta = 10000.0 ** (-j / 512.0)
    n = np.arange(N, dtype=np.float64)
    ang = n[:, None] * theta[None, :]
    cosT = np.cos(ang).astype(np.float32)
    sinT = np.sin(ang).astype(np.float32)
    return cosT, sinT, np.ascontiguousarray(cosT.T), np.ascontiguousarray(sinT.T)


def _host_masks():
    m = np.zeros((8, 128, 8), dtype=np.float32)
    for c in range(8):
        for p in range(128):
            r = 128 * c + p
            orig = 2 * r if r < 512 else 2 * (r - 512) + 1
            m[c, p, orig // 128] = 1.0
    return m


_PERM = np.concatenate([np.arange(0, 1024, 2), np.arange(1, 1024, 2)])


def kernel(L0_attn_x, L0_attn_qk_w, L0_attn_qk_b, L0_attn_lepe_w, L0_attn_lepe_b,
           L0_ffn_clip_x, L0_ffn_clip_down, L0_ffn_w_gate, L0_ffn_w_up, L0_ffn_w_down):
    x = np.ascontiguousarray(np.asarray(L0_attn_x, dtype=np.float32))
    qk_w = np.asarray(L0_attn_qk_w, dtype=np.float32)
    qk_b = np.asarray(L0_attn_qk_b, dtype=np.float32)
    wq = np.ascontiguousarray(qk_w[:, :C][:, _PERM])
    wk = np.ascontiguousarray(qk_w[:, C:][:, _PERM])
    bq_p = qk_b[:C][_PERM]
    bk_p = qk_b[C:][_PERM]
    bq_cols = np.ascontiguousarray(bq_p.reshape(KC, 128).T)
    bk_row = np.ascontiguousarray(bk_p.reshape(1, C))
    lw = np.ascontiguousarray(np.asarray(L0_attn_lepe_w, dtype=np.float32))
    lb = np.ascontiguousarray(np.asarray(L0_attn_lepe_b, dtype=np.float32).reshape(1, C))
    wg = np.ascontiguousarray(np.asarray(L0_ffn_w_gate, dtype=np.float32))
    wu = np.ascontiguousarray(np.asarray(L0_ffn_w_up, dtype=np.float32))
    wd = np.ascontiguousarray(np.asarray(L0_ffn_w_down, dtype=np.float32))
    clip_x = np.asarray(L0_ffn_clip_x, dtype=np.float32).reshape(1, 1)
    clip_d = np.asarray(L0_ffn_clip_down, dtype=np.float32).reshape(1, 1)
    cosT, sinT, cosF, sinF = _host_tables()
    eye8 = np.eye(8, dtype=np.float32)
    ident = np.eye(128, dtype=np.float32)
    mask = _host_masks()

    xs = [np.ascontiguousarray(x[BPC * c:BPC * (c + 1)]) for c in range(NCORES)]

    nc1 = build_l1()
    in1 = [dict(x=xs[c], wq=wq, wk=wk, bq=bq_cols, bk=bk_row, lw=lw,
                cosF=cosF, sinF=sinF, cosT=cosT, sinT=sinT,
                eye8=eye8, ident=ident, mask=mask) for c in range(NCORES)]
    r1 = run_bass_kernel_spmd(nc1, in1, core_ids=list(range(NCORES)))
    m1vec = np.concatenate([r1.results[c]["m1_out"].reshape(1, 1)
                            for c in range(NCORES)], axis=1)

    nc2 = build_l2()
    in2 = [dict(xn_in=r1.results[c]["xn_out"], m1vec=m1vec, clip_x=clip_x,
                wg=wg, wu=wu, ident=ident) for c in range(NCORES)]
    r2 = run_bass_kernel_spmd(nc2, in2, core_ids=list(range(NCORES)))
    m2vec = np.concatenate([r2.results[c]["m2_out"].reshape(1, 1)
                            for c in range(NCORES)], axis=1)

    nc3 = build_l3()
    in3 = [dict(h_in=r2.results[c]["h_out"], m2vec=m2vec, clip_d=clip_d,
                wd=wd, S_in=r1.results[c]["S_out"], x=xs[c], lb=lb)
           for c in range(NCORES)]
    r3 = run_bass_kernel_spmd(nc3, in3, core_ids=list(range(NCORES)))
    y = np.concatenate([np.asarray(r3.results[c]["y"]) for c in range(NCORES)], axis=0)
    return y.astype(np.float32)
